# version 02
# baseline (speedup 1.0000x reference)
import numpy as np
import ml_dtypes
import concourse.bass as bass
import concourse.mybir as mybir
from concourse.bass_utils import run_bass_kernel_spmd

F32 = mybir.dt.float32
BF16 = mybir.dt.bfloat16
I32 = mybir.dt.int32
U8 = mybir.dt.uint8
U16 = mybir.dt.uint16
AF = mybir.ActivationFunctionType
ALU = mybir.AluOpType
AX = mybir.AxisListType
bfdt = ml_dtypes.bfloat16

T = 4096
D = 2048
C = 512
NCH = T // C
NDT = D // 128

QUEUES = ("t", "v", "s", "g", "y")


def _key(r):
    if isinstance(r, str):
        return r
    t = getattr(r, "tensor", None)
    if t is not None:
        return t.name
    return r.name


class Prog:

    def __init__(self, nc, same_engine_sync=True, dma_pool=4):
        self.nc = nc
        self.same_engine_sync = same_engine_sync
        self.instr = {q: [] for q in QUEUES}
        self.sems = {}
        self.cnt = {}
        self.hw = {q: {} for q in QUEUES}
        self.last_w = {}
        self.readers = {}
        self.dma_pool = dma_pool
        self._dma_rr = {q: 0 for q in QUEUES}
        for q in QUEUES:
            self._get_sem(q)

    def _get_sem(self, name):
        if name not in self.sems:
            self.sems[name] = self.nc.alloc_semaphore(name="sem_" + name)
            self.cnt[name] = 0
        return self.sems[name]

    def _deps_for(self, reads, writes, extra):
        deps = list(extra)
        for r in reads:
            t = self.last_w.get(_key(r))
            if t is not None:
                deps.append(t)
        for w in writes:
            k = _key(w)
            t = self.last_w.get(k)
            if t is not None:
                deps.append(t)
            deps.extend(self.readers.get(k, ()))
        return deps

    def _register(self, token, reads, writes):
        if token is None:
            return
        for r in reads:
            self.readers.setdefault(_key(r), []).append(token)
        for w in writes:
            k = _key(w)
            self.last_w[k] = token
            self.readers[k] = []

    def _filter(self, q, alldeps):
        waits = []
        for d in alldeps:
            if d is None:
                continue
            s, v = d
            if not self.same_engine_sync and s == q:
                continue
            if self.hw[q].get(s, 0) < v:
                self.hw[q][s] = v
                waits.append((s, v))
        return waits

    def emit(self, q, fn, reads=(), writes=(), deps=(), sem=None, inc=1):
        waits = self._filter(q, self._deps_for(reads, writes, deps))
        token = None
        if inc:
            sname = sem if sem is not None else q
            self._get_sem(sname)
            self.cnt[sname] += inc
            token = (sname, self.cnt[sname])
            if (not self.same_engine_sync) and sname == q \
                    and self.hw[q].get(sname, 0) < self.cnt[sname]:
                self.hw[q][sname] = self.cnt[sname]
            self.instr[q].append((waits, fn, sname, inc))
        else:
            self.instr[q].append((waits, fn, None, 0))
        self._register(token, reads, writes)
        return token

    def acc(self, q, fn, reads=(), bank=None, first=False, last=False, deps=()):
        if first:
            return self.emit(q, fn, reads=reads, writes=[bank], deps=deps)
        token = self.emit(q, fn, reads=reads, deps=deps)
        if last:
            k = _key(bank)
            self.last_w[k] = token
            self.readers[k] = []
        return token

    def dma(self, q, out, in_, reads=(), writes=(), deps=(), sem=None,
            track_out=True, track_in=True, **kw):
        if sem is None:
            sem = "d%s%d" % (q, self._dma_rr[q] % self.dma_pool)
            self._dma_rr[q] += 1
        self._get_sem(sem)
        deps = list(deps)
        if self.cnt[sem] > 0:
            deps.append((sem, self.cnt[sem]))
        reads = list(reads) + ([in_] if track_in else [])
        writes = list(writes) + ([out] if track_out else [])
        fn = lambda eng: eng.dma_start(out=out, in_=in_, **kw)
        return self.emit(q, fn, reads=reads, writes=writes, deps=deps, sem=sem, inc=16)

    def wait(self, q, deps):
        self.emit(q, None, deps=deps, inc=0)

    def tokens_now(self):
        return [(s, c) for s, c in self.cnt.items() if c > 0]

    def barrier(self, qs=QUEUES):
        toks = self.tokens_now()
        for q in qs:
            self.wait(q, toks)

    def build(self):
        nc = self.nc
        with nc.Block() as blk:
            def mk(q):
                def body(eng):
                    for waits, fn, sname, inc in self.instr[q]:
                        for s, v in waits:
                            eng.wait_ge(self.sems[s], v)
                        if fn is not None:
                            inst = fn(eng)
                            if sname is not None:
                                inst.then_inc(self.sems[sname], inc)
                return body
            blk.tensor(mk("t"))
            blk.vector(mk("v"))
            blk.scalar(mk("s"))
            blk.gpsimd(mk("g"))
            blk.sync(mk("y"))



def _rope_tables():
    half = 64
    inv_freq = 1.0 / 10000.0 ** (np.arange(half, dtype=np.float64) * 2.0 / 128.0)
    pos = np.arange(T, dtype=np.float64)
    ang = pos[None, :] * inv_freq[:, None]
    ang = np.concatenate([ang, ang], axis=0)
    return np.cos(ang).astype(np.float32), np.sin(ang).astype(np.float32)


class KB:

    def __init__(self, blocks=(0, 1, 2, 3)):
        self.nc = bass.Bass("TRN2", target_bir_lowering=False)
        self.p = Prog(self.nc, same_engine_sync=True, dma_pool=4)
        self.blocks = blocks
        self.inputs = {}
        self._consts = {}
        self.pb = [self.nc.alloc_psum_tensor("pb%d" % i, [128, 512], F32)
                   for i in range(8)]
        self.ones_col_bf = self.sb("ones_col_bf", [128, 1], BF16)
        self.ones_row_f = self.sb("ones_row_f", [1, 128], F32)
        self.p.emit("g", lambda e: e.memset(self.ones_col_bf[:], 1.0),
                    writes=[self.ones_col_bf])
        self.p.emit("g", lambda e: e.memset(self.ones_row_f[:], 1.0),
                    writes=[self.ones_row_f])
        self._ident_bf = None

    def sb(self, name, shape, dtype):
        return self.nc.alloc_sbuf_tensor(name, shape, dtype)

    def din(self, name, shape, dtype=F32):
        self.inputs[name] = (tuple(shape), dtype)
        return self.nc.dram_tensor(name, list(shape), dtype, kind="ExternalInput").ap()

    def dout(self, name, shape, dtype=F32):
        return self.nc.dram_tensor(name, list(shape), dtype, kind="ExternalOutput").ap()

    def dscratch(self, name, shape, dtype):
        return self.nc.dram_tensor(name, list(shape), dtype, kind="Internal").ap()

    def const(self, name, arr):
        if name not in self._consts:
            self._consts[name] = self.nc.inline_tensor(np.ascontiguousarray(arr), name).ap()
        return self._consts[name]

    def fconst(self, val):
        key = "fc_%r" % (val,)
        t = self._consts.get(key)
        if t is None:
            t = self.sb("fc%d" % len(self._consts), [128, 1], F32)
            self.p.emit("g", lambda e: e.memset(t[:], float(val)), writes=[t])
            self._consts[key] = t
        return t

    def ident_bf(self):
        if self._ident_bf is None:
            self._ident_bf = self.sb("ident_bf", [128, 128], BF16)
            src = self.const("ident_c", np.eye(128, dtype=bfdt))
            self.p.dma("y", self._ident_bf[:], src[:])
        return self._ident_bf

    def convert_weight(self, src_ap, R, M, name, tmp_cols=2048):
        scr = self.dscratch(name, [R // 128, 128, M], BF16)
        ncols = min(tmp_cols, M)
        if not hasattr(self, "_cvbuf"):
            self._cvbuf = [self.sb("cvbuf%d" % i, [128, 2048], BF16) for i in range(2)]
        buf = [b if ncols == 2048 else None for b in self._cvbuf]
        buf = self._cvbuf
        for kt in range(R // 128):
            for ci, c0 in enumerate(range(0, M, ncols)):
                b = buf[(kt * ((M + ncols - 1) // ncols) + ci) % 2]
                nw = min(ncols, M - c0)
                self.p.dma("g", b[:, :nw], src_ap[kt * 128:(kt + 1) * 128, c0:c0 + nw],
                           writes=[b], track_out=False)
                self.p.dma("y", scr[kt, :, c0:c0 + nw], b[:, :nw], reads=[b],
                           writes=["%s:%d:%d" % (name, kt, ci)], track_out=False)
        return scr

    def ln_chunk(self, xr, ch, ln, eps=1e-5, nd=NDT,
                 pb_sx=4, pb_sq=5, pb_a=6, pb_b=7):
        p, pb = self.p, self.pb
        c0 = ch * C
        if not hasattr(self, "_lnp"):
            self._lnp = dict(
                xf=[self.sb("ln_xf%d" % i, [128, C], F32) for i in range(2)],
                xb=[self.sb("ln_xb%d" % i, [128, C], BF16) for i in range(2)],
                sq=[self.sb("ln_sq%d" % i, [128, C], BF16) for i in range(2)],
                tmp=[self.sb("ln_tmp%d" % i, [128, C], F32) for i in range(2)],
                m=self.sb("m_row", [1, C], F32), v=self.sb("v_row", [1, C], F32),
                a=self.sb("a_row", [1, C], F32), b=self.sb("b_row", [1, C], F32),
            )
        L = self._lnp
        m_row, v_row, a_row, b_row = L["m"], L["v"], L["a"], L["b"]
        for i in range(nd):
            xf, xb, sq = L["xf"][i % 2], L["xb"][i % 2], L["sq"][i % 2]
            p.dma("y", xf[:], xr[i * 128:(i + 1) * 128, c0:c0 + C])
            p.emit("g", lambda e, xf=xf, xb=xb: e.tensor_copy(xb[:], xf[:]),
                   reads=[xf], writes=[xb])
            p.emit("s", lambda e, xf=xf, sq=sq: e.activation(sq[:], xf[:], AF.Square),
                   reads=[xf], writes=[sq])
            p.acc("t", lambda e, xb=xb, i=i: e.matmul(pb[pb_sx][:1, :C], self.ones_col_bf[:],
                                                      xb[:], start=(i == 0), stop=(i == nd - 1)),
                  reads=[self.ones_col_bf, xb], bank=pb[pb_sx],
                  first=(i == 0), last=(i == nd - 1))
            p.acc("t", lambda e, sq=sq, i=i: e.matmul(pb[pb_sq][:1, :C], self.ones_col_bf[:],
                                                      sq[:], start=(i == 0), stop=(i == nd - 1)),
                  reads=[self.ones_col_bf, sq], bank=pb[pb_sq],
                  first=(i == 0), last=(i == nd - 1))
        dinv = 1.0 / (nd * 128)
        p.emit("v", lambda e: e.tensor_scalar(m_row[:], pb[pb_sx][:1, :C], dinv, None, ALU.mult),
               reads=[pb[pb_sx]], writes=[m_row])
        p.emit("v", lambda e: e.tensor_scalar(v_row[:], pb[pb_sq][:1, :C], dinv, None, ALU.mult),
               reads=[pb[pb_sq]], writes=[v_row])
        p.emit("v", lambda e: e.tensor_tensor(a_row[:], m_row[:], m_row[:], ALU.mult),
               reads=[m_row], writes=[a_row])
        p.emit("v", lambda e: e.tensor_tensor(v_row[:], v_row[:], a_row[:], ALU.subtract),
               reads=[v_row, a_row], writes=[v_row])
        epsb = self.fconst(eps)
        p.emit("s", lambda e: e.activation(a_row[:], v_row[:], AF.Ln, bias=epsb[:1, :]),
               reads=[v_row, epsb], writes=[a_row])
        p.emit("s", lambda e: e.activation(a_row[:], a_row[:], AF.Exp, scale=-0.5),
               reads=[a_row], writes=[a_row])
        p.emit("v", lambda e: e.scalar_tensor_tensor(b_row[:], m_row[:], -1.0, a_row[:],
                                                     ALU.mult, ALU.mult),
               reads=[m_row, a_row], writes=[b_row])
        p.emit("t", lambda e: e.matmul(pb[pb_a][:, :C], self.ones_row_f[:], a_row[:],
                                       start=True, stop=True),
               reads=[self.ones_row_f, a_row], writes=[pb[pb_a]])
        p.emit("t", lambda e: e.matmul(pb[pb_b][:, :C], self.ones_row_f[:], b_row[:],
                                       start=True, stop=True),
               reads=[self.ones_row_f, b_row], writes=[pb[pb_b]])
        for i in range(nd):
            xf, tmp = L["xf"][i % 2], L["tmp"][i % 2]
            p.dma("y", xf[:], xr[i * 128:(i + 1) * 128, c0:c0 + C])
            p.emit("v", lambda e, xf=xf, tmp=tmp: e.tensor_tensor(tmp[:], xf[:], pb[pb_a][:, :C], ALU.mult),
                   reads=[xf, pb[pb_a]], writes=[tmp])
            p.emit("v", lambda e, tmp=tmp, i=i: e.tensor_tensor(ln[i][:], tmp[:], pb[pb_b][:, :C], ALU.add),
                   reads=[tmp, pb[pb_b]], writes=[ln[i]])

    def phase_l1_ffn(self, xr_in, x_out_writer):
        p, pb, nc = self.p, self.pb, self.nc
        INTER = 8192
        NJ = INTER // 128
        NO = 2 * INTER // 128
        wgu = self.din("L1_ffn_w_gate_upT", [D, 2 * INTER])
        wdn = self.din("L1_ffn_w_downT", [INTER, D])
        wgu_s = self.convert_weight(wgu, D, 2 * INTER, "wgu_s")
        wdn_s = self.convert_weight(wdn, INTER, D, "wdn_s")
        p.barrier()

        ln = [self.sb("f4_ln%d" % i, [128, C], BF16) for i in range(NDT)]
        xres = [self.sb("f4_xr%d" % i, [128, C], F32) for i in range(2)]
        h2 = [self.sb("f4_h%d" % j, [128, C], BF16) for j in range(NJ)]
        wkg = [self.sb("f4_wkg%d" % i, [128, 512], BF16) for i in range(3)]
        wku = [self.sb("f4_wku%d" % i, [128, 512], BF16) for i in range(3)]
        dk = [self.sb("f4_dk%d" % i, [128, 1024], BF16) for i in range(3)]
        stmp = [self.sb("f4_st%d" % i, [128, C], BF16) for i in range(2)]
        otile = [self.sb("f4_ot%d" % i, [128, C], F32) for i in range(2)]

        for ch in range(NCH):
            self.ln_chunk(xr_in, ch, ln)
            for g in range(16):
                for kt in range(16):
                    bg, bu = wkg[kt % 3], wku[kt % 3]
                    gb, ub = 4 * g * 128, (64 + 4 * g) * 128
                    p.dma("y", bg[:], wgu_s[kt, :, gb:gb + 512])
                    p.dma("y", bu[:], wgu_s[kt, :, ub:ub + 512])
                    for oi in range(8):
                        b = bg if oi < 4 else bu
                        col = (oi % 4) * 128
                        p.acc("t", lambda e, b=b, col=col, oi=oi, kt=kt:
                              e.matmul(pb[oi][:, :C], b[:, col:col + 128], ln[kt][:],
                                       start=(kt == 0), stop=(kt == 15)),
                              reads=[b, ln[kt]], bank=pb[oi],
                              first=(kt == 0), last=(kt == 15))
                for j in range(4):
                    st = stmp[j % 2]
                    p.emit("s", lambda e, j=j, st=st: e.activation(st[:], pb[j][:, :C], AF.Silu),
                           reads=[pb[j]], writes=[st])
                    jj = 4 * g + j
                    p.emit("v", lambda e, j=j, st=st, jj=jj:
                           e.tensor_tensor(h2[jj][:], st[:], pb[4 + j][:, :C], ALU.mult),
                           reads=[st, pb[4 + j]], writes=[h2[jj]])
            for g in range(2):
                for jt in range(NJ):
                    b = dk[jt % 3]
                    p.dma("y", b[:], wdn_s[jt, :, g * 1024:(g + 1) * 1024], writes=[b])
                    for oi in range(8):
                        p.acc("t", lambda e, b=b, oi=oi, jt=jt:
                              e.matmul(pb[oi][:, :C], b[:, oi * 128:(oi + 1) * 128], h2[jt][:],
                                       start=(jt == 0), stop=(jt == NJ - 1)),
                              reads=[b, h2[jt]], bank=pb[oi],
                              first=(jt == 0), last=(jt == NJ - 1))
                for oi in range(8):
                    i = g * 8 + oi
                    ob = otile[oi % 2]
                    xv = xres[oi % 2]
                    p.dma("y", xv[:], xr_in[i * 128:(i + 1) * 128, ch * C:(ch + 1) * C])
                    p.emit("v", lambda e, oi=oi, ob=ob, xv=xv:
                           e.tensor_tensor(ob[:], pb[oi][:, :C], xv[:], ALU.add),
                           reads=[pb[oi], xv], writes=[ob])
                    x_out_writer(i, ch, ob)

    def phase_l0_attn(self, xr_in, x_out_writer):
        p, pb, nc = self.p, self.pb, self.nc
        HEADS, KV = 16, 8
        SC = 1.0 / float(np.sqrt(2048.0))
        wqkv = self.din("L0_attn_WqkvT", [D, 4096])
        wot = self.din("L0_attn_WOT", [D, D])
        wqkv_s = self.convert_weight(wqkv, D, 4096, "wqkv_s")
        wot_s = self.convert_weight(wot, D, D, "wot_s")
        p.barrier()
        qs = self.dscratch("qs", [D, T], BF16)
        ks = self.dscratch("ks", [KV * 128, T], BF16)
        cxs = self.dscratch("cxs", [D, T], BF16)
        cosT, sinT = _rope_tables()
        cos_sb = self.sb("cos_sb", [128, T], BF16)
        sin_sb = self.sb("sin_sb", [128, T], BF16)
        p.dma("y", cos_sb[:], self.const("cos_c", cosT.astype(bfdt))[:])
        p.dma("y", sin_sb[:], self.const("sin_c", sinT.astype(bfdt))[:])
        ident = self.ident_bf()
        vT = [self.sb("vT%d" % kv, [128, T], BF16) for kv in range(KV)]
        Pt = self.sb("Pt", [128, 32 * C], BF16)
        self._Pt, self._vT = Pt, vT
        ln = [self.sb("a1_ln%d" % i, [128, C], BF16) for i in range(NDT)]
        slab = [self.sb("a1_wk%d" % i, [128, 1024], BF16) for i in range(3)]
        rot = self.sb("a1_rot", [128, C], F32)
        t1 = self.sb("a1_t1", [128, C], F32)
        sqb = [self.sb("a1_sq%d" % i, [128, C], BF16) for i in range(2)]
        vtmp = [self.sb("a1_vt%d" % i, [128, C], BF16) for i in range(2)]
        qkt = [self.sb("a1_qk%d" % i, [128, C], BF16) for i in range(2)]
        nqmax = self.sb("a1_nq", [1, 16], F32)
        nkmax = self.sb("a1_nk", [1, 16], F32)
        brow = self.sb("a1_brow", [1, 16], F32)
        tmp11 = self.sb("a1_t11", [1, 1], F32)
        bias_sb = self.sb("a1_bias", [128, 16], F32)
        kbuf = self.sb("a1_kb", [128, T], BF16)
        qbuf = [self.sb("a1_qb%d" % i, [128, C], BF16) for i in range(2)]
        zr_row = self.sb("a1_zr", [1, C], F32)
        zr_bc = self.sb("a1_zbc", [128, C], F32)
        ctile = [self.sb("a1_ct%d" % i, [128, C], BF16) for i in range(2)]
        cbuf = ln
        xres = [self.sb("a1_xr%d" % i, [128, C], F32) for i in range(2)]
        otile = [self.sb("a1_ot%d" % i, [128, C], F32) for i in range(2)]
        p.emit("g", lambda e: e.memset(nqmax[:], 0.0), writes=[nqmax])
        p.emit("g", lambda e: e.memset(nkmax[:], 0.0), writes=[nkmax])

        for ch in range(NCH):
            self.ln_chunk(xr_in, ch, ln)
            c0 = ch * C
            for g in range(4):
                ots = list(range(8 * g, 8 * g + 8))
                for kt in range(16):
                    b = slab[kt % 3]
                    p.dma("y", b[:], wqkv_s[kt, :, g * 1024:(g + 1) * 1024])
                    for oi in range(8):
                        p.acc("t", lambda e, b=b, oi=oi, kt=kt:
                              e.matmul(pb[oi][:, :C], b[:, oi * 128:(oi + 1) * 128],
                                       ln[kt][:], start=(kt == 0), stop=(kt == 15)),
                              reads=[b, ln[kt]], bank=pb[oi],
                              first=(kt == 0), last=(kt == 15))
                for oi in range(8):
                    ot = 8 * g + oi
                    if ot < 24:
                        p.emit("v", lambda e, oi=oi: e.tensor_scalar(
                            rot[0:64, :], pb[oi][64:128, :C], -1.0, None, ALU.mult),
                            reads=[pb[oi]], writes=[rot])
                        p.emit("v", lambda e, oi=oi: e.tensor_copy(
                            rot[64:128, :], pb[oi][0:64, :C]),
                            reads=[pb[oi]], writes=[rot])
                        p.emit("v", lambda e, oi=oi, c0=c0: e.tensor_tensor(
                            t1[:], pb[oi][:, :C], cos_sb[:, c0:c0 + C], ALU.mult),
                            reads=[pb[oi], cos_sb], writes=[t1])
                        p.emit("v", lambda e, c0=c0: e.tensor_tensor(
                            rot[:], rot[:], sin_sb[:, c0:c0 + C], ALU.mult),
                            reads=[rot, sin_sb], writes=[rot])
                        qk = qkt[oi % 2]
                        p.emit("v", lambda e, qk=qk: e.tensor_tensor(
                            qk[:], t1[:], rot[:], ALU.add),
                            reads=[t1, rot], writes=[qk])
                        if ot < 16:
                            dst, r0, tag = qs, ot * 128, "qs:%d:%d" % (ot, ch)
                        else:
                            dst, r0, tag = ks, (ot - 16) * 128, "ks:%d:%d" % (ot - 16, ch)
                        p.dma("y", dst[r0:r0 + 128, c0:c0 + C], qk[:], reads=[qk],
                              writes=[tag], track_out=False)
                        sq = sqb[oi % 2]
                        p.emit("v", lambda e, sq=sq, qk=qk: e.tensor_tensor(
                            sq[:], qk[:], qk[:], ALU.mult), reads=[qk], writes=[sq])
                        p.emit("t", lambda e, oi=oi, sq=sq: e.matmul(
                            pb[oi][:1, :C], self.ones_col_bf[:], sq[:], start=True, stop=True),
                            reads=[self.ones_col_bf, sq], writes=[pb[oi]])
                        p.emit("v", lambda e, oi=oi: e.reduce_max(
                            tmp11[:], pb[oi][:1, :C], AX.X), reads=[pb[oi]], writes=[tmp11])
                        if ot < 16:
                            p.emit("v", lambda e, ot=ot: e.tensor_tensor(
                                nqmax[:, ot:ot + 1], nqmax[:, ot:ot + 1], tmp11[:], ALU.max),
                                reads=[nqmax, tmp11], writes=[nqmax])
                        else:
                            kv = ot - 16
                            for hh in (2 * kv, 2 * kv + 1):
                                p.emit("v", lambda e, hh=hh: e.tensor_tensor(
                                    nkmax[:, hh:hh + 1], nkmax[:, hh:hh + 1], tmp11[:], ALU.max),
                                    reads=[nkmax, tmp11], writes=[nkmax])
                    else:
                        kv = ot - 24
                        vt = vtmp[oi % 2]
                        p.emit("v", lambda e, vt=vt, oi=oi: e.tensor_copy(vt[:], pb[oi][:, :C]),
                               reads=[pb[oi]], writes=[vt])
                        for j in range(4):
                            p.emit("t", lambda e, oi=oi, vt=vt, j=j: e.transpose(
                                pb[oi][:, :].bitcast(BF16)[:, :128],
                                vt[:, j * 128:(j + 1) * 128], ident[:]),
                                reads=[vt, ident], writes=[pb[oi]])
                            cb = (ch * 4 + j) * 128
                            p.emit("v", lambda e, kv=kv, oi=oi, cb=cb: e.tensor_copy(
                                vT[kv][:, cb:cb + 128], pb[oi][:, :].bitcast(BF16)[:, :128]),
                                reads=[pb[oi]], writes=["vT%d:%d" % (kv, cb)])
        p.emit("v", lambda e: e.tensor_tensor(brow[:], nqmax[:], nkmax[:], ALU.mult),
               reads=[nqmax, nkmax], writes=[brow])
        p.emit("s", lambda e: e.activation(brow[:], brow[:], AF.Ln),
               reads=[brow], writes=[brow])
        p.emit("s", lambda e: e.activation(brow[:], brow[:], AF.Exp, scale=0.5),
               reads=[brow], writes=[brow])
        p.emit("v", lambda e: e.tensor_scalar(brow[:], brow[:], -SC, None, ALU.mult),
               reads=[brow], writes=[brow])
        for h in range(16):
            bank = pb[h % 2]
            p.emit("t", lambda e, h=h, bank=bank: e.matmul(
                bank[:, :1], self.ones_row_f[:], brow[:, h:h + 1], start=True, stop=True),
                reads=[self.ones_row_f, brow], writes=[bank])
            p.emit("v", lambda e, h=h, bank=bank: e.tensor_copy(
                bias_sb[:, h:h + 1], bank[:, :1]), reads=[bank], writes=[bias_sb])
        p.barrier()

        for h in range(HEADS):
            kv = h // 2
            p.dma("y", kbuf[:], ks[kv * 128:(kv + 1) * 128, :])
            for qc in range(NCH):
                qb = qbuf[qc % 2]
                p.dma("y", qb[:], qs[h * 128:(h + 1) * 128, qc * C:(qc + 1) * C])
                pend = []
                for kt in range(32):
                    bank = pb[kt % 2]
                    p.emit("t", lambda e, bank=bank, kt=kt, qb=qb: e.matmul(
                        bank[:, :C], kbuf[:, kt * 128:(kt + 1) * 128], qb[:],
                        start=True, stop=True),
                        reads=[kbuf, qb], writes=[bank])
                    p.emit("s", lambda e, bank=bank, kt=kt, h=h: e.activation(
                        Pt[:, kt * C:(kt + 1) * C], bank[:, :C], AF.Exp,
                        bias=bias_sb[:, h:h + 1], scale=SC),
                        reads=[bank, bias_sb], writes=["Pt:%d" % kt])
                    pend.append(kt)
                    if len(pend) > 1:
                        self._attn_zc(pend.pop(0), kv)
                while pend:
                    self._attn_zc(pend.pop(0), kv)
                p.emit("v", lambda e: e.reciprocal(zr_row[:], pb[2][:1, :C]),
                       reads=[pb[2]], writes=[zr_row])
                p.emit("t", lambda e: e.matmul(pb[4][:, :C], self.ones_row_f[:],
                                               zr_row[:], start=True, stop=True),
                       reads=[self.ones_row_f, zr_row], writes=[pb[4]])
                p.emit("v", lambda e: e.tensor_copy(zr_bc[:], pb[4][:, :C]),
                       reads=[pb[4]], writes=[zr_bc])
                ct = ctile[qc % 2]
                p.emit("v", lambda e, ct=ct: e.tensor_tensor(ct[:], pb[3][:, :C],
                                                             zr_bc[:], ALU.mult),
                       reads=[pb[3], zr_bc], writes=[ct])
                p.dma("y", cxs[h * 128:(h + 1) * 128, qc * C:(qc + 1) * C], ct[:],
                      reads=[ct], writes=["cxs:%d:%d" % (h, qc)], track_out=False)
        p.barrier()

        for ch in range(NCH):
            c0 = ch * C
            for i in range(16):
                p.dma("y", cbuf[i][:], cxs[i * 128:(i + 1) * 128, c0:c0 + C])
            for g in range(2):
                for kt in range(16):
                    b = slab[kt % 3]
                    p.dma("y", b[:], wot_s[kt, :, g * 1024:(g + 1) * 1024])
                    for oi in range(8):
                        p.acc("t", lambda e, b=b, oi=oi, kt=kt: e.matmul(
                            pb[oi][:, :C], b[:, oi * 128:(oi + 1) * 128], cbuf[kt][:],
                            start=(kt == 0), stop=(kt == 15)),
                            reads=[b, cbuf[kt]], bank=pb[oi],
                            first=(kt == 0), last=(kt == 15))
                for oi in range(8):
                    i = g * 8 + oi
                    xv, ob = xres[oi % 2], otile[oi % 2]
                    p.dma("y", xv[:], xr_in[i * 128:(i + 1) * 128, c0:c0 + C])
                    p.emit("v", lambda e, oi=oi, xv=xv, ob=ob: e.tensor_tensor(
                        ob[:], pb[oi][:, :C], xv[:], ALU.add),
                        reads=[pb[oi], xv], writes=[ob])
                    x_out_writer(i, ch, ob)

    def _attn_zc(self, kt, kv):
        p, pb = self.p, self.pb
        Pt, vT = self._Pt, self._vT
        p.acc("t", lambda e, kt=kt: e.matmul(
            pb[2][:1, :C], self.ones_col_bf[:], Pt[:, kt * C:(kt + 1) * C],
            start=(kt == 0), stop=(kt == 31)),
            reads=[self.ones_col_bf, "Pt:%d" % kt], bank=pb[2],
            first=(kt == 0), last=(kt == 31))
        p.acc("t", lambda e, kt=kt, kv=kv: e.matmul(
            pb[3][:, :C], vT[kv][:, kt * 128:(kt + 1) * 128],
            Pt[:, kt * C:(kt + 1) * C], start=(kt == 0), stop=(kt == 31)),
            reads=[vT[kv], "Pt:%d" % kt], bank=pb[3],
            first=(kt == 0), last=(kt == 31))

    def _dequant(self, packed_ap, scalesT_ap, n_out_tiles, M_in, scr):
        p, pb = self.p, self.pb
        NW, NB, NG = M_in // 8, M_in // 2, M_in // 128
        if not hasattr(self, "_dq"):
            MX = 5632
            self._dq = dict(
                pk=self.sb("dq_pk", [128, MX // 8], I32),
                n16=self.sb("dq_n16", [128, MX // 2], U16),
                lo=self.sb("dq_lo", [128, MX // 2], U16),
                m=self.sb("dq_m", [128, MX // 2], U16),
                sb_=self.sb("dq_sb", [128, MX // 2], U16),
                bA=self.sb("dq_bA", [128, MX // 2], U16),
                bB=self.sb("dq_bB", [128, MX // 2], U16),
                mge=self.sb("dq_mge", [128, MX // 2], U16),
                bits=self.sb("dq_bits", [128, MX], U16),
                scl=self.sb("dq_scl", [128, MX // 128], mybir.dt.float16),
                sclf=self.sb("dq_sclf", [128, MX // 128], F32),
                cb=self.sb("dq_cb", [128, MX], BF16),
            )
        Q = self._dq
        pk, n16, lo = Q["pk"], Q["n16"], Q["lo"]
        m_, sb_, bA, bB, mge = Q["m"], Q["sb_"], Q["bA"], Q["bB"], Q["mge"]
        bits, scl, sclf, cb = Q["bits"], Q["scl"], Q["sclf"], Q["cb"]
        hi = n16
        ident = self.ident_bf()
        for o in range(n_out_tiles):
            r0 = o * 128
            p.dma("y", pk[:, :NW], packed_ap[r0:r0 + 128, :], writes=[pk], track_out=False)
            p.dma("y", scl[:, :NG], scalesT_ap[r0:r0 + 128, :], writes=[scl], track_out=False)
            p.emit("v", lambda e: e.tensor_copy(n16[:, :NB], pk[:, :NW].bitcast(U8)),
                   reads=[pk], writes=[n16])
            p.emit("v", lambda e: e.tensor_scalar(lo[:, :NB], n16[:, :NB], 15, None, ALU.bitwise_and),
                   reads=[n16], writes=[lo])
            p.emit("v", lambda e: e.tensor_scalar(n16[:, :NB], n16[:, :NB], 4, None, ALU.logical_shift_right),
                   reads=[n16], writes=[n16])
            for srcb, ph in ((lo, 0), (hi, 1)):
                p.emit("v", lambda e, s=srcb: e.tensor_scalar(m_[:, :NB], s[:, :NB], 7, None, ALU.bitwise_and),
                       reads=[srcb], writes=[m_])
                p.emit("v", lambda e, s=srcb: e.tensor_scalar(sb_[:, :NB], s[:, :NB], 8, 12,
                                                              ALU.bitwise_and, ALU.logical_shift_left),
                       reads=[srcb], writes=[sb_])
                p.emit("v", lambda e: e.tensor_single_scalar(bA[:, :NB], m_[:, :NB], 252, ALU.add),
                       reads=[m_], writes=[bA])
                p.emit("v", lambda e: e.tensor_single_scalar(bA[:, :NB], bA[:, :NB], 6, ALU.logical_shift_left),
                       reads=[bA], writes=[bA])
                p.emit("v", lambda e: e.tensor_scalar(bB[:, :NB], m_[:, :NB], 0x3F00, None, ALU.mult),
                       reads=[m_], writes=[bB])
                p.emit("v", lambda e: e.tensor_scalar(mge[:, :NB], m_[:, :NB], 2, None, ALU.is_ge),
                       reads=[m_], writes=[mge])
                p.emit("v", lambda e: e.copy_predicated(bB[:, :NB], mge[:, :NB], bA[:, :NB]),
                       reads=[mge, bA], writes=[bB])
                outv = bits[:, :M_in].rearrange("p (c two) -> p c two", two=2)[:, :, ph]
                p.emit("v", lambda e, o_=outv: e.tensor_tensor(o_, bB[:, :NB], sb_[:, :NB], ALU.bitwise_or),
                       reads=[bB, sb_], writes=["dqbits%d" % ph])
            p.emit("v", lambda e: e.tensor_copy(sclf[:, :NG], scl[:, :NG]),
                   reads=[scl], writes=[sclf])
            bview = bits[:, :M_in].bitcast(BF16).rearrange("p (g i) -> p g i", i=128)
            sview = sclf[:, :NG].unsqueeze(2).to_broadcast([128, NG, 128])
            p.emit("v", lambda e: e.tensor_tensor(bview, bview, sview, ALU.mult),
                   reads=["dqbits0", "dqbits1", sclf], writes=[bits])
            for j in range(NG):
                bank = pb[6 + (j % 2)]
                wqv = bits[:, :M_in].bitcast(BF16)
                p.emit("t", lambda e, j=j, bank=bank, wqv=wqv: e.transpose(
                    bank[:, :].bitcast(BF16)[:, :128], wqv[:, j * 128:(j + 1) * 128], ident[:]),
                    reads=[bits], writes=[bank])
                p.emit("v", lambda e, j=j, bank=bank: e.tensor_copy(
                    cb[:, j * 128:(j + 1) * 128], bank[:, :].bitcast(BF16)[:, :128]),
                    reads=[bank], writes=[cb])
            p.dma("y", scr[o].transpose([1, 0, 2]),
                  cb[:, :M_in].rearrange("p (j m) -> p j m", m=128),
                  reads=[cb], writes=["dqs" + scr.tensor.name + str(o)], track_out=False)

    def phase_l0_ffn(self, xr_in, x_out_writer):
        p, pb = self.p, self.pb
        I2, NJ, NO = 5632, 44, 88
        gup = self.din("L0_ffn_gate_up_packed", [2 * I2, 256], I32)
        gus = self.din("L0_ffn_gate_up_scalesT", [2 * I2, 16], mybir.dt.float16)
        dnp = self.din("L0_ffn_down_packed", [D, 704], I32)
        dns = self.din("L0_ffn_down_scalesT", [D, 44], mybir.dt.float16)
        wgu_scr = self.dscratch("wdq_gu", [NO, 16, 128, 128], BF16)
        wdn_scr = self.dscratch("wdq_dn", [16, NJ, 128, 128], BF16)
        self._dequant(gup, gus, NO, D, wgu_scr)
        self._dequant(dnp, dns, 16, I2, wdn_scr)
        p.barrier()

        ln = [self.sb("f2_ln%d" % i, [128, C], BF16) for i in range(NDT)]
        h2 = [self.sb("f2_h%d" % j, [128, C], BF16) for j in range(NJ)]
        ysb = [self.sb("f2_y%d" % i, [128, C], F32) for i in range(16)]
        wkg = [self.sb("f2_wkg%d" % i, [128, 512], BF16) for i in range(3)]
        wku = [self.sb("f2_wku%d" % i, [128, 512], BF16) for i in range(3)]
        dk = [self.sb("f2_dk%d" % i, [128, 1024], BF16) for i in range(3)]
        stmp = [self.sb("f2_st%d" % i, [128, C], BF16) for i in range(2)]
        sqy = [self.sb("f2_sqy%d" % i, [128, C], BF16) for i in range(2)]
        rr = self.sb("f2_rr", [1, C], F32)
        rbc = self.sb("f2_rbc", [128, C], F32)
        xres = [self.sb("f2_xr%d" % i, [128, C], F32) for i in range(2)]
        otile = [self.sb("f2_ot%d" % i, [128, C], F32) for i in range(2)]

        for ch in range(NCH):
            c0 = ch * C
            self.ln_chunk(xr_in, ch, ln)
            for g in range(11):
                for kt in range(16):
                    bg, bu = wkg[kt % 3], wku[kt % 3]
                    p.dma("y", bg[:].rearrange("p (o m) -> p o m", m=128),
                          wgu_scr[4 * g:4 * g + 4, kt].transpose([1, 0, 2]),
                          writes=[bg], track_out=False)
                    p.dma("y", bu[:].rearrange("p (o m) -> p o m", m=128),
                          wgu_scr[44 + 4 * g:44 + 4 * g + 4, kt].transpose([1, 0, 2]),
                          writes=[bu], track_out=False)
                    for oi in range(8):
                        b = bg if oi < 4 else bu
                        col = (oi % 4) * 128
                        p.acc("t", lambda e, b=b, col=col, oi=oi, kt=kt:
                              e.matmul(pb[oi][:, :C], b[:, col:col + 128], ln[kt][:],
                                       start=(kt == 0), stop=(kt == 15)),
                              reads=[b, ln[kt]], bank=pb[oi],
                              first=(kt == 0), last=(kt == 15))
                for j in range(4):
                    st = stmp[j % 2]
                    p.emit("s", lambda e, j=j, st=st: e.activation(st[:], pb[j][:, :C], AF.Silu),
                           reads=[pb[j]], writes=[st])
                    jj = 4 * g + j
                    p.emit("v", lambda e, j=j, st=st, jj=jj:
                           e.tensor_tensor(h2[jj][:], st[:], pb[4 + j][:, :C], ALU.mult),
                           reads=[st, pb[4 + j]], writes=[h2[jj]])
            for g in range(2):
                for jt in range(NJ):
                    b = dk[jt % 3]
                    p.dma("y", b[:].rearrange("p (o m) -> p o m", m=128),
                          wdn_scr[8 * g:8 * g + 8, jt].transpose([1, 0, 2]),
                          writes=[b], track_out=False)
                    for oi in range(8):
                        p.acc("t", lambda e, b=b, oi=oi, jt=jt:
                              e.matmul(pb[oi][:, :C], b[:, oi * 128:(oi + 1) * 128], h2[jt][:],
                                       start=(jt == 0), stop=(jt == NJ - 1)),
                              reads=[b, h2[jt]], bank=pb[oi],
                              first=(jt == 0), last=(jt == NJ - 1))
                for oi in range(8):
                    i = g * 8 + oi
                    p.emit("v", lambda e, oi=oi, i=i: e.tensor_copy(ysb[i][:], pb[oi][:, :C]),
                           reads=[pb[oi]], writes=[ysb[i]])
            for i in range(16):
                sq = sqy[i % 2]
                p.emit("s", lambda e, i=i, sq=sq: e.activation(sq[:], ysb[i][:], AF.Square),
                       reads=[ysb[i]], writes=[sq])
                p.acc("t", lambda e, sq=sq, i=i: e.matmul(pb[0][:1, :C], self.ones_col_bf[:],
                                                          sq[:], start=(i == 0), stop=(i == 15)),
                      reads=[self.ones_col_bf, sq], bank=pb[0],
                      first=(i == 0), last=(i == 15))
            epsb = self.fconst(1e-6)
            p.emit("s", lambda e: e.activation(rr[:], pb[0][:1, :C], AF.Ln,
                                               bias=epsb[:1, :], scale=1.0 / 2048.0),
                   reads=[pb[0], epsb], writes=[rr])
            p.emit("s", lambda e: e.activation(rr[:], rr[:], AF.Exp, scale=-0.5),
                   reads=[rr], writes=[rr])
            p.emit("t", lambda e: e.matmul(pb[1][:, :C], self.ones_row_f[:], rr[:],
                                           start=True, stop=True),
                   reads=[self.ones_row_f, rr], writes=[pb[1]])
            p.emit("v", lambda e: e.tensor_copy(rbc[:], pb[1][:, :C]),
                   reads=[pb[1]], writes=[rbc])
            for i in range(16):
                xv, ob = xres[i % 2], otile[i % 2]
                p.dma("y", xv[:], xr_in[i * 128:(i + 1) * 128, c0:c0 + C])
                p.emit("v", lambda e, i=i, ob=ob: e.tensor_tensor(ob[:], ysb[i][:], rbc[:], ALU.mult),
                       reads=[ysb[i], rbc], writes=[ob])
                p.emit("v", lambda e, ob=ob, xv=xv: e.tensor_tensor(ob[:], ob[:], xv[:], ALU.add),
                       reads=[ob, xv], writes=[ob])
                x_out_writer(i, ch, ob)

    def phase_l1_attn(self, xr_in, x_out_writer):
        p, pb = self.p, self.pb
        NH, HD, DST, QL, NCK = 64, 64, 128, 256, 16
        DI = 4096
        win = self.din("L1_attn_W_inT", [D, 8512])
        wout = self.din("L1_attn_W_outT", [DI, D])
        conv_w = self.din("L1_attn_conv_w", [4352, 4])
        conv_b = self.din("L1_attn_conv_b", [4352])
        dt_bias = self.din("L1_attn_dt_bias", [64])
        A_log = self.din("L1_attn_A_log", [64])
        D_param = self.din("L1_attn_D_param", [64])
        norm_w = self.din("L1_attn_norm_w", [4096])
        self._dtbuf = self.sb("m_dt", [64, T], F32)
        win_s = self.convert_weight(win, D, 8512, "win_s")
        wout_s = self.convert_weight(wout, DI, D, "wout_s")
        p.barrier()
        zs = self.dscratch("zs", [DI, T], BF16)
        xbcp = self.dscratch("xbcp", [4352, T], F32)
        xbcc = self.dscratch("xbcc", [4352, T], BF16)
        acum_scr = self.dscratch("acum_scr", [64, T], F32)
        csscr = self.dscratch("csscr", [NCK, 64], F32)
        yns = self.dscratch("yns", [DI, T], BF16)
        identb = self.ident_bf()
        identf = self.sb("ident_f", [128, 128], F32)
        p.dma("y", identf[:], self.const("identf_c", np.eye(128, dtype=np.float32))[:])

        ln = [self.sb("m_ln%d" % i, [128, C], BF16) for i in range(NDT)]
        slab = [self.sb("m_wk%d" % i, [128, 1024], BF16) for i in range(3)]
        ztile = [self.sb("m_zt%d" % i, [128, C], BF16) for i in range(2)]
        xbt = [self.sb("m_xbt%d" % i, [128, C], F32) for i in range(2)]

        groups = [list(range(8 * g, 8 * g + 8)) for g in range(8)] + [[64, 65, 66]]
        for ch in range(NCH):
            c0 = ch * C
            self.ln_chunk(xr_in, ch, ln)
            for ots in groups:
                g0 = ots[0] * 128
                gw = (ots[-1] * 128 + (64 if ots[-1] == 66 else 128)) - g0
                for kt in range(16):
                    b = slab[kt % 3]
                    p.dma("y", b[:, :gw], win_s[kt, :, g0:g0 + gw], writes=[b],
                          track_out=False)
                    for oi, ot in enumerate(ots):
                        mw = 64 if ot == 66 else 128
                        p.acc("t", lambda e, b=b, oi=oi, kt=kt, mw=mw:
                              e.matmul(pb[oi][:mw, :C], b[:, oi * 128:oi * 128 + mw],
                                       ln[kt][:], start=(kt == 0), stop=(kt == 15)),
                              reads=[b, ln[kt]], bank=pb[oi],
                              first=(kt == 0), last=(kt == 15))
                for oi, ot in enumerate(ots):
                    if ot < 32:
                        zt = ztile[oi % 2]
                        p.emit("v", lambda e, oi=oi, zt=zt: e.tensor_copy(zt[:], pb[oi][:, :C]),
                               reads=[pb[oi]], writes=[zt])
                        p.dma("y", zs[ot * 128:(ot + 1) * 128, c0:c0 + C], zt[:],
                              reads=[zt], writes=["zs:%d:%d" % (ot, ch)], track_out=False)
                    elif ot < 66:
                        xt = xbt[oi % 2]
                        p.emit("v", lambda e, oi=oi, xt=xt: e.tensor_copy(xt[:], pb[oi][:, :C]),
                               reads=[pb[oi]], writes=[xt])
                        r0 = (ot - 32) * 128
                        p.dma("y", xbcp[r0:r0 + 128, c0:c0 + C], xt[:],
                              reads=[xt], writes=["xbcp:%d:%d" % (ot, ch)], track_out=False)
                    else:
                        p.emit("v", lambda e, oi=oi, c0=c0: e.tensor_copy(
                            self._dtbuf[:, c0:c0 + C], pb[oi][:64, :C]),
                            reads=[pb[oi]], writes=[self._dtbuf])
        p.barrier()

        cw_sb = self.sb("m_cw", [128, 34 * 4], F32)
        cb_sb = self.sb("m_cb", [128, 34], F32)
        p.dma("y", cw_sb[:].rearrange("p (a k) -> p a k", k=4),
              conv_w.rearrange("(a p) k -> p a k", p=128), writes=[cw_sb], track_out=False)
        p.dma("y", cb_sb[:], conv_b.rearrange("(a p) -> p a", p=128),
              writes=[cb_sb], track_out=False, allow_slow_non_contiguous=True)
        SEG = 1024
        xin = [self.sb("m_xin%d" % i, [128, SEG + 3], F32) for i in range(2)]
        cacc = self.sb("m_cacc", [128, SEG], F32)
        cvout = self.sb("m_cvout", [128, SEG], BF16)
        for a in range(34):
            r0 = a * 128
            for sg in range(T // SEG):
                xi = xin[sg % 2]
                s0 = sg * SEG
                if sg == 0:
                    p.emit("g", lambda e, xi=xi: e.memset(xi[:, :3], 0.0), writes=[xi])
                    p.dma("y", xi[:, 3:], xbcp[r0:r0 + 128, 0:SEG], writes=[xi],
                          track_out=False)
                else:
                    p.dma("y", xi[:], xbcp[r0:r0 + 128, s0 - 3:s0 + SEG], writes=[xi],
                          track_out=False)
                p.emit("v", lambda e, xi=xi, a=a: e.tensor_scalar(
                    cacc[:], xi[:, 0:SEG], cw_sb[:, 4 * a:4 * a + 1], cb_sb[:, a:a + 1],
                    ALU.mult, ALU.add), reads=[xi, cw_sb, cb_sb], writes=[cacc])
                for k in range(1, 4):
                    p.emit("v", lambda e, xi=xi, a=a, k=k: e.scalar_tensor_tensor(
                        cacc[:], xi[:, k:k + SEG], cw_sb[:, 4 * a + k:4 * a + k + 1],
                        cacc[:], ALU.mult, ALU.add),
                        reads=[xi, cw_sb, cacc], writes=[cacc])
                p.emit("s", lambda e: e.activation(cvout[:], cacc[:], AF.Silu),
                       reads=[cacc], writes=[cvout])
                p.dma("y", xbcc[r0:r0 + 128, s0:s0 + SEG], cvout[:],
                      reads=[cvout], writes=["xbcc:%d:%d" % (a, sg)], track_out=False)
        dtb_col = self.sb("m_dtb", [64, 1], F32)
        acol = self.sb("m_acol", [64, 1], F32)
        dcol = self.sb("m_dcol", [64, 1], F32)
        p.dma("y", dtb_col[:], dt_bias.unsqueeze(1))
        p.dma("y", acol[:], A_log.unsqueeze(1))
        p.dma("y", dcol[:], D_param.unsqueeze(1))
        p.emit("s", lambda e: e.activation(acol[:], acol[:], AF.Exp),
               reads=[acol], writes=[acol])
        p.emit("v", lambda e: e.tensor_scalar(acol[:], acol[:], -1.0, None, ALU.mult),
               reads=[acol], writes=[acol])
        dts = self._dtbuf
        stmp = self.sb("m_stmp", [64, 2048], F32)
        one64 = self.fconst(1.0)
        for sg in range(2):
            s0 = sg * 2048
            sl = slice(s0, s0 + 2048)
            p.emit("v", lambda e, sl=sl: e.tensor_scalar(
                dts[:, sl], dts[:, sl], dtb_col[:], None, ALU.add),
                reads=[dts, dtb_col], writes=[dts])
            p.emit("s", lambda e, sl=sl: e.activation(stmp[:], dts[:, sl], AF.Abs),
                   reads=[dts], writes=[stmp])
            p.emit("s", lambda e: e.activation(stmp[:], stmp[:], AF.Exp, scale=-1.0),
                   reads=[stmp], writes=[stmp])
            p.emit("s", lambda e: e.activation(stmp[:], stmp[:], AF.Ln, bias=one64[:64, :]),
                   reads=[stmp, one64], writes=[stmp])
            p.emit("v", lambda e, sl=sl: e.tensor_scalar(
                dts[:, sl], dts[:, sl], 0.0, None, ALU.max),
                reads=[dts], writes=[dts])
            p.emit("v", lambda e, sl=sl: e.tensor_tensor(
                dts[:, sl], dts[:, sl], stmp[:], ALU.add),
                reads=[dts, stmp], writes=[dts])
        m1 = np.zeros((64, 128), np.float32)
        for k in range(64):
            m1[k, (k % 2) * 64:(k % 2) * 64 + 64] = 1.0
        pat = np.zeros((64, 32), np.float32)
        for k in range(64):
            pat[k, k // 2] = 1.0
        m1_sb = self.sb("m_m1", [64, 128], F32)
        pat_sb = self.sb("m_pat", [64, 32], F32)
        p.dma("y", m1_sb[:], self.const("m1_c", m1)[:])
        p.dma("y", pat_sb[:], self.const("pat_c", pat)[:])
        m2 = self.sb("m_m2", [64, 32], F32)
        p.emit("v", lambda e: e.tensor_tensor(
            m2[:], pat_sb[:], dcol[:].to_broadcast([64, 32]), ALU.mult),
            reads=[pat_sb, dcol], writes=[m2])
        p.emit("t", lambda e: e.matmul(pb[0][:, :32], m1_sb[:], m2[:], start=True, stop=True),
               reads=[m1_sb, m2], writes=[pb[0]])
        dbc = self.sb("m_dbc", [128, 32], F32)
        p.emit("v", lambda e: e.tensor_copy(dbc[:], pb[0][:, :32]),
               reads=[pb[0]], writes=[dbc])
        nw_sb = self.sb("m_nw", [128, 32], F32)
        p.dma("y", nw_sb[:], norm_w.rearrange("(a p) -> p a", p=128),
              writes=[nw_sb], track_out=False, allow_slow_non_contiguous=True)
        p.barrier()

        mask = np.zeros((128, 512), np.float32)
        for st in range(2):
            for pp in range(128):
                s_idx = st * 128 + pp
                mask[pp, st * 256 + s_idx:st * 256 + 256] = 1.0
        mask_sb = self.sb("m_mask", [128, 512], BF16)
        p.dma("y", mask_sb[:], self.const("mask_c", mask.astype(bfdt))[:])
        acum = self.sb("m_acum", [64, QL], F32)
        dAc = self.sb("m_dAc", [64, QL], F32)
        onesQ = self.sb("m_onesQ", [64, QL], F32)
        p.emit("g", lambda e: e.memset(onesQ[:], 1.0), writes=[onesQ])
        dec = self.sb("m_dec", [64, QL], F32)
        decf = self.sb("m_decf", [64, QL], F32)
        dtcb = self.sb("m_dtcb", [64, QL], BF16)
        acT = self.sb("m_acT", [128, 128], F32)
        acTn = self.sb("m_acTn", [128, 128], F32)
        dcT = self.sb("m_dcT", [128, 128], F32)
        dtT = self.sb("m_dtT", [128, 128], BF16)
        Bsb = self.sb("m_B", [128, QL], BF16)
        Csb = self.sb("m_C", [128, QL], BF16)
        BT = self.sb("m_BT", [128, 2 * 128], BF16)
        msc = self.sb("m_msc", [128, 2 * QL], BF16)
        Xsb = [self.sb("m_X%d" % j, [128, QL], BF16) for j in range(32)]
        XT = self.sb("m_XT", [128, 8192], BF16)
        yc = [self.sb("m_y%d" % j, [128, QL], BF16) for j in range(32)]
        P = self.sb("m_P", [128, 4096], F32)
        Pbf = self.sb("m_Pbf", [128, 4096], BF16)
        eA = self.sb("m_eA", [128, QL], BF16)
        Csc = self.sb("m_Csc", [128, QL], BF16)
        mtmp = self.sb("m_mtmp", [128, QL], F32)
        me = self.sb("m_me", [128, QL], BF16)
        Mt = [self.sb("m_Mt%d" % st, [128, QL], BF16) for st in range(2)]
        sBT = self.sb("m_sBT", [128, 128], BF16)
        arow = [self.sb("m_ar%d" % i, [1, QL], F32) for i in range(2)]
        csrow = self.sb("m_csr", [1, 64], F32)
        excsT = self.sb("m_excsT", [128, 64], F32)
        zt2 = [self.sb("m_z2%d" % j, [128, QL], BF16) for j in range(2)]
        sqg = [self.sb("m_sqg%d" % i, [128, QL], BF16) for i in range(2)]
        gtmp = [self.sb("m_gt%d" % i, [128, QL], BF16) for i in range(2)]
        rr = self.sb("m_rr", [1, QL], F32)
        rbc = self.sb("m_rbc", [128, QL], F32)
        p.emit("g", lambda e: e.memset(P[:], 0.0), writes=[P])

        eps5 = self.fconst(1e-5)
        for c in range(NCK):
            c0 = c * QL
            p.emit("v", lambda e, c0=c0: e.tensor_scalar(
                dAc[:], dts[:, c0:c0 + QL], acol[:], None, ALU.mult),
                reads=[dts, acol], writes=[dAc])
            p.emit("v", lambda e: e.tensor_tensor_scan(
                acum[:], onesQ[:], dAc[:], 0.0, ALU.mult, ALU.add),
                reads=[onesQ, dAc], writes=[acum])
            p.dma("y", acum_scr[:, c0:c0 + QL], acum[:], reads=[acum],
                  writes=["acs:%d" % c], track_out=False)
            p.dma("y", csscr[c].unsqueeze(1), acum[:, QL - 1:QL], reads=[acum],
                  writes=["css:%d" % c], track_out=False)
            p.emit("v", lambda e: e.tensor_scalar(
                dec[:], acum[:], acum[:, QL - 1:QL], None, ALU.subtract),
                reads=[acum], writes=[dec])
            p.emit("s", lambda e: e.activation(decf[:], dec[:], AF.Exp, scale=-1.0),
                   reads=[dec], writes=[decf])
            p.emit("v", lambda e, c0=c0: e.tensor_copy(dtcb[:], dts[:, c0:c0 + QL]),
                   reads=[dts], writes=[dtcb])
            for sub in range(2):
                p.emit("t", lambda e, sub=sub: e.transpose(
                    pb[5][:, :64], acum[:, sub * 128:(sub + 1) * 128], identf[:64, :64]),
                    reads=[acum, identf], writes=[pb[5]])
                p.emit("v", lambda e, sub=sub: e.tensor_copy(
                    acT[:, sub * 64:(sub + 1) * 64], pb[5][:, :64]),
                    reads=[pb[5]], writes=[acT])
                p.emit("t", lambda e, sub=sub: e.transpose(
                    pb[6][:, :64], decf[:, sub * 128:(sub + 1) * 128],
                    identf[:64, :64]), reads=[decf, identf], writes=[pb[6]])
                p.emit("v", lambda e, sub=sub: e.tensor_copy(
                    dcT[:, sub * 64:(sub + 1) * 64], pb[6][:, :64]),
                    reads=[pb[6]], writes=[dcT])
                p.emit("t", lambda e, sub=sub: e.transpose(
                    pb[6][:, :].bitcast(BF16)[:, :64], dtcb[:, sub * 128:(sub + 1) * 128],
                    identb[:64, :64]), reads=[dtcb, identb], writes=[pb[6]])
                p.emit("v", lambda e, sub=sub: e.tensor_copy(
                    dtT[:, sub * 64:(sub + 1) * 64], pb[6][:, :].bitcast(BF16)[:, :64]),
                    reads=[pb[6]], writes=[dtT])
            p.emit("v", lambda e: e.tensor_scalar(acTn[:], acT[:], -1.0, None, ALU.mult),
                   reads=[acT], writes=[acTn])
            p.dma("y", Bsb[:], xbcc[4096:4224, c0:c0 + QL], writes=[Bsb], track_out=False)
            p.dma("y", Csb[:], xbcc[4224:4352, c0:c0 + QL], writes=[Csb], track_out=False)
            for st in range(2):
                p.emit("t", lambda e, st=st: e.transpose(
                    pb[5][:, :].bitcast(BF16)[:, :128], Bsb[:, st * 128:(st + 1) * 128],
                    identb[:]), reads=[Bsb, identb], writes=[pb[5]])
                p.emit("v", lambda e, st=st: e.tensor_copy(
                    BT[:, st * 128:(st + 1) * 128], pb[5][:, :].bitcast(BF16)[:, :128]),
                    reads=[pb[5]], writes=[BT])
                p.emit("t", lambda e, st=st: e.matmul(
                    pb[st][:, :QL], Bsb[:, st * 128:(st + 1) * 128], Csb[:],
                    start=True, stop=True), reads=[Bsb, Csb], writes=[pb[st]])
                p.emit("v", lambda e, st=st: e.tensor_tensor(
                    msc[:, st * QL:(st + 1) * QL], pb[st][:, :QL],
                    mask_sb[:, st * QL:(st + 1) * QL], ALU.mult),
                    reads=[pb[st], mask_sb], writes=[msc])
            for j in range(32):
                p.dma("y", Xsb[j][:], xbcc[j * 128:(j + 1) * 128, c0:c0 + QL],
                      writes=[Xsb[j]], track_out=False)
                for sub in range(2):
                    bank = pb[5 + (j % 2)]
                    p.emit("t", lambda e, j=j, sub=sub, bank=bank: e.transpose(
                        bank[:, :].bitcast(BF16)[:, :128], Xsb[j][:, sub * 128:(sub + 1) * 128],
                        identb[:]), reads=[Xsb[j], identb], writes=[bank])
                    p.emit("v", lambda e, j=j, sub=sub, bank=bank: e.tensor_copy(
                        XT[:, sub * 4096 + j * 128:sub * 4096 + (j + 1) * 128],
                        bank[:, :].bitcast(BF16)[:, :128]),
                        reads=[bank], writes=["XTs%d" % sub])
            for sub in range(2):
                xv = XT[:, sub * 4096:(sub + 1) * 4096].rearrange("p (h i) -> p h i", i=64)
                dv = dtT[:, sub * 64:(sub + 1) * 64].unsqueeze(2).to_broadcast([128, 64, 64])
                p.emit("v", lambda e, xv=xv, dv=dv: e.tensor_tensor(xv, xv, dv, ALU.mult),
                       reads=[dtT], writes=["XTs%d" % sub])
            p.emit("v", lambda e: e.tensor_copy(Pbf[:], P[:]), reads=[P], writes=[Pbf])
            p.dma("g", csrow[:], csscr[c].unsqueeze(0), writes=[csrow],
                  track_out=False, reads=["css:%d" % c])
            p.emit("s", lambda e: e.activation(csrow[:], csrow[:], AF.Exp),
                   reads=[csrow], writes=[csrow])
            p.emit("t", lambda e: e.matmul(pb[2][:, :64], self.ones_row_f[:], csrow[:],
                                           start=True, stop=True),
                   reads=[self.ones_row_f, csrow], writes=[pb[2]])
            p.emit("v", lambda e: e.tensor_copy(excsT[:], pb[2][:, :64]),
                   reads=[pb[2]], writes=[excsT])
            pv = P[:].rearrange("p (h i) -> p h i", i=64)
            ev = excsT[:].unsqueeze(2).to_broadcast([128, 64, 64])
            p.emit("v", lambda e, pv=pv, ev=ev: e.tensor_tensor(pv, pv, ev, ALU.mult),
                   reads=[excsT, P], writes=[P])
            for h in range(NH):
                ar = arow[h % 2]
                p.dma("g", ar[:], acum_scr[h:h + 1, c0:c0 + QL], writes=[ar],
                      track_out=False, reads=["acs:%d" % c])
                p.emit("t", lambda e, ar=ar: e.matmul(
                    pb[2][:, :QL], self.ones_row_f[:], ar[:], start=True, stop=True),
                    reads=[self.ones_row_f, ar], writes=[pb[2]])
                p.emit("s", lambda e: e.activation(eA[:], pb[2][:, :QL], AF.Exp),
                       reads=[pb[2]], writes=[eA])
                p.emit("v", lambda e: e.tensor_tensor(Csc[:], Csb[:], eA[:], ALU.mult),
                       reads=[Csb, eA], writes=[Csc])
                for st in range(2):
                    p.emit("v", lambda e, st=st, h=h: e.tensor_scalar(
                        mtmp[:], pb[2][:, :QL], acTn[:, st * 64 + h:st * 64 + h + 1],
                        0.0, ALU.add, ALU.min),
                        reads=[pb[2], acTn], writes=[mtmp])
                    p.emit("s", lambda e: e.activation(me[:], mtmp[:], AF.Exp),
                           reads=[mtmp], writes=[me])
                    p.emit("v", lambda e, st=st: e.tensor_tensor(
                        Mt[st][:], me[:], msc[:, st * QL:(st + 1) * QL], ALU.mult),
                        reads=[me, msc], writes=[Mt[st]])
                hb = (h % 2) * 64
                p.emit("t", lambda e, h=h, hb=hb: e.matmul(
                    pb[3][hb:hb + 64, :QL], Pbf[:, h * 64:(h + 1) * 64], Csc[:],
                    start=True, stop=False), reads=[Pbf, Csc], writes=[pb[3]])
                for st in range(2):
                    p.acc("t", lambda e, st=st, h=h, hb=hb: e.matmul(
                        pb[3][hb:hb + 64, :QL], XT[:, st * 4096 + h * 64:st * 4096 + h * 64 + 64],
                        Mt[st][:], start=False, stop=(st == 1)),
                        reads=["XTs%d" % st, Mt[st]], bank=pb[3],
                        first=False, last=(st == 1))
                xj, half = h // 2, (h % 2) * 64
                p.emit("v", lambda e, h=h, xj=xj, half=half: e.scalar_tensor_tensor(
                    yc[xj][half:half + 64, :], Xsb[xj][half:half + 64, :],
                    dbc[half:half + 64, xj:xj + 1], pb[3][half:half + 64, :QL],
                    ALU.mult, ALU.add),
                    reads=[Xsb[xj], dbc, pb[3]], writes=["yck%d" % xj])
                for st in range(2):
                    p.emit("v", lambda e, st=st, h=h: e.tensor_scalar(
                        sBT[:], BT[:, st * 128:(st + 1) * 128],
                        dcT[:, st * 64 + h:st * 64 + h + 1], None, ALU.mult),
                        reads=[BT, dcT], writes=[sBT])
                    p.acc("t", lambda e, st=st, h=h: e.matmul(
                        pb[4][:, :64], sBT[:], XT[:, st * 4096 + h * 64:st * 4096 + h * 64 + 64],
                        start=(st == 0), stop=(st == 1)),
                        reads=[sBT, "XTs%d" % st], bank=pb[4],
                        first=(st == 0), last=(st == 1))
                p.emit("v", lambda e, h=h: e.tensor_tensor(
                    P[:, h * 64:(h + 1) * 64], P[:, h * 64:(h + 1) * 64],
                    pb[4][:, :64], ALU.add), reads=[P, pb[4]], writes=[P])
            for j in range(32):
                z2 = zt2[j % 2]
                p.dma("y", z2[:], zs[j * 128:(j + 1) * 128, c0:c0 + QL],
                      writes=[z2], track_out=False)
                gt = gtmp[j % 2]
                p.emit("s", lambda e, z2=z2, gt=gt: e.activation(gt[:], z2[:], AF.Silu),
                       reads=[z2], writes=[gt])
                p.emit("v", lambda e, j=j, gt=gt: e.tensor_tensor(
                    yc[j][:], yc[j][:], gt[:], ALU.mult),
                    reads=[gt], writes=["yck%d" % j])
                sq = sqg[j % 2]
                p.emit("s", lambda e, j=j, sq=sq: e.activation(sq[:], yc[j][:], AF.Square),
                       reads=["yck%d" % j], writes=[sq])
                p.acc("t", lambda e, sq=sq, j=j: e.matmul(
                    pb[7][:1, :QL], self.ones_col_bf[:], sq[:],
                    start=(j == 0), stop=(j == 31)),
                    reads=[self.ones_col_bf, sq], bank=pb[7],
                    first=(j == 0), last=(j == 31))
            p.emit("s", lambda e: e.activation(rr[:], pb[7][:1, :QL], AF.Ln,
                                               bias=eps5[:1, :], scale=1.0 / 4096.0),
                   reads=[pb[7], eps5], writes=[rr])
            p.emit("s", lambda e: e.activation(rr[:], rr[:], AF.Exp, scale=-0.5),
                   reads=[rr], writes=[rr])
            p.emit("t", lambda e: e.matmul(pb[7][:, :QL], self.ones_row_f[:], rr[:],
                                           start=True, stop=True),
                   reads=[self.ones_row_f, rr], writes=[pb[7]])
            p.emit("v", lambda e: e.tensor_copy(rbc[:], pb[7][:, :QL]),
                   reads=[pb[7]], writes=[rbc])
            for j in range(32):
                p.emit("v", lambda e, j=j: e.scalar_tensor_tensor(
                    yc[j][:], yc[j][:], nw_sb[:, j:j + 1], rbc[:], ALU.mult, ALU.mult),
                    reads=[nw_sb, rbc], writes=["yck%d" % j])
                p.dma("y", yns[j * 128:(j + 1) * 128, c0:c0 + QL], yc[j][:],
                      reads=["yck%d" % j], writes=["yns:%d:%d" % (j, c)],
                      track_out=False)
        p.barrier()

        ynb = Xsb
        xres = xbt
        otile = [xin[0], xin[1]]
        for cc in range(T // QL):
            c0 = cc * QL
            for j in range(32):
                p.dma("y", ynb[j][:], yns[j * 128:(j + 1) * 128, c0:c0 + QL],
                      writes=[ynb[j]], track_out=False)
            for g in range(2):
                for kt in range(32):
                    b = slab[kt % 3]
                    p.dma("y", b[:], wout_s[kt, :, g * 1024:(g + 1) * 1024],
                          writes=[b], track_out=False)
                    for oi in range(8):
                        p.acc("t", lambda e, b=b, oi=oi, kt=kt: e.matmul(
                            pb[oi][:, :QL], b[:, oi * 128:(oi + 1) * 128], ynb[kt][:],
                            start=(kt == 0), stop=(kt == 31)),
                            reads=[b, ynb[kt]], bank=pb[oi],
                            first=(kt == 0), last=(kt == 31))
                for oi in range(8):
                    i = g * 8 + oi
                    xv, ob = xres[oi % 2][:, :QL], otile[oi % 2][:, :QL]
                    xk, ok = xres[oi % 2], otile[oi % 2]
                    p.dma("y", xv, xr_in[i * 128:(i + 1) * 128, c0:c0 + QL],
                          writes=[xk], track_out=False)
                    p.emit("v", lambda e, oi=oi, xv=xv, ob=ob: e.tensor_tensor(
                        ob, pb[oi][:, :QL], xv, ALU.add),
                        reads=[pb[oi], xk], writes=[ok])
                    x_out_writer(i, cc, ob)

    def finish(self):
        self.p.barrier()
        self.p.build()



def _qkv_perm():
    i = np.arange(2048)
    permq = (i % 128) * 16 + i // 128
    j = np.arange(1024)
    permkv = (j % 128) * 8 + j // 128
    return permq, permkv


def prep_l0attn_inputs(W_Q, W_K, W_V, W_O):
    permq, permkv = _qkv_perm()
    W_Q, W_K = np.asarray(W_Q), np.asarray(W_K)
    W_V, W_O = np.asarray(W_V), np.asarray(W_O)
    wq = W_Q[permq].T
    wk = W_K[permkv].T
    wv = W_V[permkv].T
    return {
        "L0_attn_WqkvT": np.ascontiguousarray(np.concatenate([wq, wk, wv], axis=1)),
        "L0_attn_WOT": np.ascontiguousarray(W_O.T),
    }


def build_l0attn_program():
    kb = KB(blocks=(0,))
    xr_in = kb.din("x_in", [D, T])
    x_out = kb.dout("x_out", [D, T])

    def writer(i, ch, ob):
        kb.p.dma("y", x_out[i * 128:(i + 1) * 128, ch * C:(ch + 1) * C], ob[:],
                 reads=[ob], writes=["x_out:%d:%d" % (i, ch)], track_out=False)
    kb.phase_l0_attn(xr_in, writer)
    kb.finish()
    return kb


def prep_l0ffn_inputs(gup, gus, dnp, dns):
    return {
        "L0_ffn_gate_up_packed": np.ascontiguousarray(np.asarray(gup)),
        "L0_ffn_gate_up_scalesT": np.ascontiguousarray(np.asarray(gus).T),
        "L0_ffn_down_packed": np.ascontiguousarray(np.asarray(dnp)),
        "L0_ffn_down_scalesT": np.ascontiguousarray(np.asarray(dns).T),
    }


def build_l0ffn_program():
    kb = KB(blocks=(1,))
    xr_in = kb.din("x_in", [D, T])
    x_out = kb.dout("x_out", [D, T])

    def writer(i, ch, ob):
        kb.p.dma("y", x_out[i * 128:(i + 1) * 128, ch * C:(ch + 1) * C], ob[:],
                 reads=[ob], writes=["x_out:%d:%d" % (i, ch)], track_out=False)
    kb.phase_l0_ffn(xr_in, writer)
    kb.finish()
    return kb


def prep_l1attn_inputs(W_in, conv_w, conv_b, dt_bias, A_log, D_param, norm_w, W_out):
    return {
        "L1_attn_W_inT": np.ascontiguousarray(np.asarray(W_in).T),
        "L1_attn_W_outT": np.ascontiguousarray(np.asarray(W_out).T),
        "L1_attn_conv_w": np.ascontiguousarray(np.asarray(conv_w)),
        "L1_attn_conv_b": np.ascontiguousarray(np.asarray(conv_b)),
        "L1_attn_dt_bias": np.ascontiguousarray(np.asarray(dt_bias)),
        "L1_attn_A_log": np.ascontiguousarray(np.asarray(A_log)),
        "L1_attn_D_param": np.ascontiguousarray(np.asarray(D_param)),
        "L1_attn_norm_w": np.ascontiguousarray(np.asarray(norm_w)),
    }


def build_l1attn_program():
    kb = KB(blocks=(2,))
    xr_in = kb.din("x_in", [D, T])
    x_out = kb.dout("x_out", [D, T])

    def writer(i, cc, ob):
        kb.p.dma("y", x_out[i * 128:(i + 1) * 128, cc * 256:(cc + 1) * 256], ob[:],
                 reads=[ob], writes=["x_out:%d:%d" % (i, cc)], track_out=False)
    kb.phase_l1_attn(xr_in, writer)
    kb.finish()
    return kb


def build_l1ffn_program():
    kb = KB(blocks=(3,))
    xr_in = kb.din("x_in", [D, T])
    x_out = kb.dout("x_out", [D, T])

    def writer(i, ch, ob):
        kb.p.dma("y", x_out[i * 128:(i + 1) * 128, ch * C:(ch + 1) * C], ob[:],
                 reads=[ob], writes=["x_out:%d:%d" % (i, ch)], track_out=False)
    kb.phase_l1_ffn(xr_in, writer)
    kb.finish()
    return kb



def prep_l1ffn_inputs(L1_ffn_w_gate_up, L1_ffn_w_down):
    return {
        "L1_ffn_w_gate_upT": np.ascontiguousarray(np.asarray(L1_ffn_w_gate_up).T),
        "L1_ffn_w_downT": np.ascontiguousarray(np.asarray(L1_ffn_w_down).T),
    }


_PROGRAMS = {}


def _get_programs():
    if "p" not in _PROGRAMS:
        _PROGRAMS["p"] = [build_l0attn_program(), build_l0ffn_program(),
                          build_l1attn_program(), build_l1ffn_program()]
    return _PROGRAMS["p"]


def kernel(**inputs):
    x = np.asarray(inputs["L0_attn_x"])
    ncores = x.shape[0]
    xs = [np.ascontiguousarray(x[c].T) for c in range(ncores)]
    maps = [
        prep_l0attn_inputs(inputs["L0_attn_W_Q"], inputs["L0_attn_W_K"],
                           inputs["L0_attn_W_V"], inputs["L0_attn_W_O"]),
        prep_l0ffn_inputs(inputs["L0_ffn_gate_up_packed"], inputs["L0_ffn_gate_up_scales"],
                          inputs["L0_ffn_down_packed"], inputs["L0_ffn_down_scales"]),
        prep_l1attn_inputs(inputs["L1_attn_W_in"], inputs["L1_attn_conv_w"],
                           inputs["L1_attn_conv_b"], inputs["L1_attn_dt_bias"],
                           inputs["L1_attn_A_log"], inputs["L1_attn_D_param"],
                           inputs["L1_attn_norm_w"], inputs["L1_attn_W_out"]),
        prep_l1ffn_inputs(inputs["L1_ffn_w_gate_up"], inputs["L1_ffn_w_down"]),
    ]
    kbs = _get_programs()
    cur = xs
    for kb, wm in zip(kbs, maps):
        in_maps = [dict(wm, x_in=cur[c]) for c in range(ncores)]
        res = run_bass_kernel_spmd(kb.nc, in_maps, core_ids=list(range(ncores)))
        cur = [np.asarray(res.results[c]["x_out"]) for c in range(ncores)]
    out = np.stack([cur[c].T for c in range(ncores)], axis=0)
    return np.ascontiguousarray(out.astype(np.float32))


# version 04
# speedup vs baseline: 1.0303x; 1.0303x over previous
import numpy as np
import ml_dtypes
import concourse.bass as bass
import concourse.mybir as mybir
from concourse.bass_utils import run_bass_kernel_spmd

F32 = mybir.dt.float32
BF16 = mybir.dt.bfloat16
I32 = mybir.dt.int32
U8 = mybir.dt.uint8
U16 = mybir.dt.uint16
AF = mybir.ActivationFunctionType
ALU = mybir.AluOpType
AX = mybir.AxisListType
bfdt = ml_dtypes.bfloat16

T = 4096
D = 2048
C = 512
NCH = T // C
NDT = D // 128

QUEUES = ("t", "v", "s", "g", "y")


def _key(r):
    if isinstance(r, str):
        return r
    t = getattr(r, "tensor", None)
    if t is not None:
        return t.name
    return r.name


class Prog:

    def __init__(self, nc, same_engine_sync=True, dma_pool=8):
        self.nc = nc
        self.same_engine_sync = same_engine_sync
        self.instr = {q: [] for q in QUEUES}
        self.sems = {}
        self.cnt = {}
        self.hw = {q: {} for q in QUEUES}
        self.last_w = {}
        self.readers = {}
        self.dma_pool = dma_pool
        self._dma_rr = {q: 0 for q in QUEUES}
        self.eng_sem = {q: q for q in QUEUES}
        self._dma_tag = ""
        for q in QUEUES:
            self._get_sem(q)

    def _get_sem(self, name):
        if name not in self.sems:
            self.sems[name] = self.nc.alloc_semaphore(name="sem_" + name)
            self.cnt[name] = 0
        return self.sems[name]

    def _deps_for(self, reads, writes, extra):
        deps = list(extra)
        for r in reads:
            t = self.last_w.get(_key(r))
            if t is not None:
                deps.append(t)
        for w in writes:
            k = _key(w)
            t = self.last_w.get(k)
            if t is not None:
                deps.append(t)
            deps.extend(self.readers.get(k, ()))
        return deps

    def _register(self, token, reads, writes):
        if token is None:
            return
        for r in reads:
            self.readers.setdefault(_key(r), []).append(token)
        for w in writes:
            k = _key(w)
            self.last_w[k] = token
            self.readers[k] = []

    def _filter(self, q, alldeps):
        waits = []
        for d in alldeps:
            if d is None:
                continue
            s, v = d
            if not self.same_engine_sync and s == q:
                continue
            if self.hw[q].get(s, 0) < v:
                self.hw[q][s] = v
                waits.append((s, v))
        return waits

    def epoch(self, tag):
        for q in QUEUES:
            self.eng_sem[q] = q + tag
            self._get_sem(q + tag)
        self._dma_tag = tag

    def emit(self, q, fn, reads=(), writes=(), deps=(), sem=None, inc=1):
        waits = self._filter(q, self._deps_for(reads, writes, deps))
        token = None
        if inc:
            sname = sem if sem is not None else self.eng_sem[q]
            self._get_sem(sname)
            self.cnt[sname] += inc
            token = (sname, self.cnt[sname])
            if (not self.same_engine_sync) and sname == q \
                    and self.hw[q].get(sname, 0) < self.cnt[sname]:
                self.hw[q][sname] = self.cnt[sname]
            self.instr[q].append((waits, fn, sname, inc))
        else:
            self.instr[q].append((waits, fn, None, 0))
        self._register(token, reads, writes)
        return token

    def acc(self, q, fn, reads=(), bank=None, first=False, last=False, deps=()):
        if first:
            return self.emit(q, fn, reads=reads, writes=[bank], deps=deps)
        token = self.emit(q, fn, reads=reads, deps=deps)
        if last:
            k = _key(bank)
            self.last_w[k] = token
            self.readers[k] = []
        return token

    def dma(self, q, out, in_, reads=(), writes=(), deps=(), sem=None,
            track_out=True, track_in=True, **kw):
        if sem is None:
            sem = "d%s%d%s" % (q, self._dma_rr[q] % self.dma_pool, self._dma_tag)
            self._dma_rr[q] += 1
        self._get_sem(sem)
        deps = list(deps)
        if self.cnt[sem] > 0:
            deps.append((sem, self.cnt[sem]))
        reads = list(reads) + ([in_] if track_in else [])
        writes = list(writes) + ([out] if track_out else [])
        fn = lambda eng: eng.dma_start(out=out, in_=in_, **kw)
        return self.emit(q, fn, reads=reads, writes=writes, deps=deps, sem=sem, inc=16)

    def wait(self, q, deps):
        self.emit(q, None, deps=deps, inc=0)

    def tokens_now(self):
        return [(s, c) for s, c in self.cnt.items() if c > 0]

    def barrier(self, qs=QUEUES):
        toks = self.tokens_now()
        for q in qs:
            self.wait(q, toks)

    def build(self):
        nc = self.nc
        with nc.Block() as blk:
            def mk(q):
                def body(eng):
                    for waits, fn, sname, inc in self.instr[q]:
                        for s, v in waits:
                            eng.wait_ge(self.sems[s], v)
                        if fn is not None:
                            inst = fn(eng)
                            if sname is not None:
                                inst.then_inc(self.sems[sname], inc)
                return body
            blk.tensor(mk("t"))
            blk.vector(mk("v"))
            blk.scalar(mk("s"))
            blk.gpsimd(mk("g"))
            blk.sync(mk("y"))



def _rope_tables():
    half = 64
    inv_freq = 1.0 / 10000.0 ** (np.arange(half, dtype=np.float64) * 2.0 / 128.0)
    pos = np.arange(T, dtype=np.float64)
    ang = pos[None, :] * inv_freq[:, None]
    ang = np.concatenate([ang, ang], axis=0)
    return np.cos(ang).astype(np.float32), np.sin(ang).astype(np.float32)


class KB:

    def __init__(self, blocks=(0, 1, 2, 3)):
        self.nc = bass.Bass("TRN2", target_bir_lowering=False)
        self.p = Prog(self.nc, same_engine_sync=True, dma_pool=4)
        self.blocks = blocks
        self.inputs = {}
        self._consts = {}
        self._cur = None
        self._arena0 = None
        self.pb = [self.nc.alloc_psum_tensor("pb%d" % i, [128, 512], F32)
                   for i in range(8)]
        self.ones_col_bf = self.sb("ones_col_bf", [128, 1], BF16)
        self.ones_row_f = self.sb("ones_row_f", [1, 128], F32)
        self.p.emit("g", lambda e: e.memset(self.ones_col_bf[:], 1.0),
                    writes=[self.ones_col_bf])
        self.p.emit("g", lambda e: e.memset(self.ones_row_f[:], 1.0),
                    writes=[self.ones_row_f])
        self._ident_bf = None

    def prelude_shared(self):
        self.fconst(1e-5)
        self.fconst(1e-6)
        self.fconst(1.0)
        self.ident_bf()
        self._lnp = dict(
            xf=[self.sb("ln_xf%d" % i, [128, C], F32) for i in range(2)],
            xb=[self.sb("ln_xb%d" % i, [128, C], BF16) for i in range(2)],
            sq=[self.sb("ln_sq%d" % i, [128, C], BF16) for i in range(2)],
            tmp=[self.sb("ln_tmp%d" % i, [128, C], F32) for i in range(2)],
            m=self.sb("m_row", [1, C], F32), v=self.sb("v_row", [1, C], F32),
            a=self.sb("a_row", [1, C], F32), b=self.sb("b_row", [1, C], F32),
        )
        self._cvbuf = [self.sb("cvbuf%d" % i, [128, 2048], BF16) for i in range(2)]
        self._arena0 = self.nc.SBUF_PARTITION_SIZE_BYTES - self.nc.sbuf_bytes_remaining

    def phase_begin(self, tag):
        self.p.barrier()
        self.p.epoch(tag)
        self._cur = self._arena0

    def sb(self, name, shape, dtype):
        if self._cur is None:
            return self.nc.alloc_sbuf_tensor(name, shape, dtype)
        nbytes = int(np.prod(shape[1:])) * mybir.dt.size(dtype)
        off = self._cur
        self._cur = (off + nbytes + 31) & ~31
        assert self._cur <= self.nc.SBUF_PARTITION_SIZE_BYTES, \
            "arena overflow at %s (%d)" % (name, self._cur)
        return self.nc.alloc_sbuf_tensor_at(name, shape, dtype, offset=off)

    def din(self, name, shape, dtype=F32):
        self.inputs[name] = (tuple(shape), dtype)
        return self.nc.dram_tensor(name, list(shape), dtype, kind="ExternalInput").ap()

    def dout(self, name, shape, dtype=F32):
        return self.nc.dram_tensor(name, list(shape), dtype, kind="ExternalOutput").ap()

    def dscratch(self, name, shape, dtype):
        return self.nc.dram_tensor(name, list(shape), dtype, kind="Internal").ap()

    def const(self, name, arr):
        if name not in self._consts:
            self._consts[name] = self.nc.inline_tensor(np.ascontiguousarray(arr), name).ap()
        return self._consts[name]

    def fconst(self, val):
        key = "fc_%r" % (val,)
        t = self._consts.get(key)
        if t is None:
            t = self.sb("fc%d" % len(self._consts), [128, 1], F32)
            self.p.emit("g", lambda e: e.memset(t[:], float(val)), writes=[t])
            self._consts[key] = t
        return t

    def ident_bf(self):
        if self._ident_bf is None:
            self._ident_bf = self.sb("ident_bf", [128, 128], BF16)
            src = self.const("ident_c", np.eye(128, dtype=bfdt))
            self.p.dma("y", self._ident_bf[:], src[:])
        return self._ident_bf

    def convert_weight(self, src_ap, R, M, name, tmp_cols=2048):
        scr = self.dscratch(name, [R // 128, 128, M], BF16)
        ncols = min(tmp_cols, M)
        if not hasattr(self, "_cvbuf"):
            self._cvbuf = [self.sb("cvbuf%d" % i, [128, 2048], BF16) for i in range(2)]
        buf = [b if ncols == 2048 else None for b in self._cvbuf]
        buf = self._cvbuf
        for kt in range(R // 128):
            for ci, c0 in enumerate(range(0, M, ncols)):
                b = buf[(kt * ((M + ncols - 1) // ncols) + ci) % 2]
                nw = min(ncols, M - c0)
                self.p.dma("g", b[:, :nw], src_ap[kt * 128:(kt + 1) * 128, c0:c0 + nw],
                           writes=[b], track_out=False)
                self.p.dma("y", scr[kt, :, c0:c0 + nw], b[:, :nw], reads=[b],
                           writes=["%s:%d:%d" % (name, kt, ci)], track_out=False)
        return scr

    def ln_chunk(self, xr, ch, ln, eps=1e-5, nd=NDT,
                 pb_sx=4, pb_sq=5, pb_a=6, pb_b=7):
        p, pb = self.p, self.pb
        c0 = ch * C
        if not hasattr(self, "_lnp"):
            self._lnp = dict(
                xf=[self.sb("ln_xf%d" % i, [128, C], F32) for i in range(2)],
                xb=[self.sb("ln_xb%d" % i, [128, C], BF16) for i in range(2)],
                sq=[self.sb("ln_sq%d" % i, [128, C], BF16) for i in range(2)],
                tmp=[self.sb("ln_tmp%d" % i, [128, C], F32) for i in range(2)],
                m=self.sb("m_row", [1, C], F32), v=self.sb("v_row", [1, C], F32),
                a=self.sb("a_row", [1, C], F32), b=self.sb("b_row", [1, C], F32),
            )
        L = self._lnp
        m_row, v_row, a_row, b_row = L["m"], L["v"], L["a"], L["b"]
        for i in range(nd):
            xf, xb, sq = L["xf"][i % 2], L["xb"][i % 2], L["sq"][i % 2]
            p.dma("y", xf[:], xr[i * 128:(i + 1) * 128, c0:c0 + C])
            p.emit("g", lambda e, xf=xf, xb=xb: e.tensor_copy(xb[:], xf[:]),
                   reads=[xf], writes=[xb])
            p.emit("s", lambda e, xf=xf, sq=sq: e.activation(sq[:], xf[:], AF.Square),
                   reads=[xf], writes=[sq])
            p.acc("t", lambda e, xb=xb, i=i: e.matmul(pb[pb_sx][:1, :C], self.ones_col_bf[:],
                                                      xb[:], start=(i == 0), stop=(i == nd - 1)),
                  reads=[self.ones_col_bf, xb], bank=pb[pb_sx],
                  first=(i == 0), last=(i == nd - 1))
            p.acc("t", lambda e, sq=sq, i=i: e.matmul(pb[pb_sq][:1, :C], self.ones_col_bf[:],
                                                      sq[:], start=(i == 0), stop=(i == nd - 1)),
                  reads=[self.ones_col_bf, sq], bank=pb[pb_sq],
                  first=(i == 0), last=(i == nd - 1))
        dinv = 1.0 / (nd * 128)
        p.emit("v", lambda e: e.tensor_scalar(m_row[:], pb[pb_sx][:1, :C], dinv, None, ALU.mult),
               reads=[pb[pb_sx]], writes=[m_row])
        p.emit("v", lambda e: e.tensor_scalar(v_row[:], pb[pb_sq][:1, :C], dinv, None, ALU.mult),
               reads=[pb[pb_sq]], writes=[v_row])
        p.emit("v", lambda e: e.tensor_tensor(a_row[:], m_row[:], m_row[:], ALU.mult),
               reads=[m_row], writes=[a_row])
        p.emit("v", lambda e: e.tensor_tensor(v_row[:], v_row[:], a_row[:], ALU.subtract),
               reads=[v_row, a_row], writes=[v_row])
        epsb = self.fconst(eps)
        p.emit("s", lambda e: e.activation(a_row[:], v_row[:], AF.Ln, bias=epsb[:1, :]),
               reads=[v_row, epsb], writes=[a_row])
        p.emit("s", lambda e: e.activation(a_row[:], a_row[:], AF.Exp, scale=-0.5),
               reads=[a_row], writes=[a_row])
        p.emit("v", lambda e: e.scalar_tensor_tensor(b_row[:], m_row[:], -1.0, a_row[:],
                                                     ALU.mult, ALU.mult),
               reads=[m_row, a_row], writes=[b_row])
        p.emit("t", lambda e: e.matmul(pb[pb_a][:, :C], self.ones_row_f[:], a_row[:],
                                       start=True, stop=True),
               reads=[self.ones_row_f, a_row], writes=[pb[pb_a]])
        p.emit("t", lambda e: e.matmul(pb[pb_b][:, :C], self.ones_row_f[:], b_row[:],
                                       start=True, stop=True),
               reads=[self.ones_row_f, b_row], writes=[pb[pb_b]])
        for i in range(nd):
            xf, tmp = L["xf"][i % 2], L["tmp"][i % 2]
            p.dma("y", xf[:], xr[i * 128:(i + 1) * 128, c0:c0 + C])
            p.emit("v", lambda e, xf=xf, tmp=tmp: e.tensor_tensor(tmp[:], xf[:], pb[pb_a][:, :C], ALU.mult),
                   reads=[xf, pb[pb_a]], writes=[tmp])
            p.emit("v", lambda e, tmp=tmp, i=i: e.tensor_tensor(ln[i][:], tmp[:], pb[pb_b][:, :C], ALU.add),
                   reads=[tmp, pb[pb_b]], writes=[ln[i]])

    def phase_l1_ffn(self, xr_in, x_out_writer):
        p, pb, nc = self.p, self.pb, self.nc
        INTER = 8192
        NJ = INTER // 128
        NO = 2 * INTER // 128
        wgu = self.din("L1_ffn_w_gate_upT", [D, 2 * INTER])
        wdn = self.din("L1_ffn_w_downT", [INTER, D])
        wgu_s = self.convert_weight(wgu, D, 2 * INTER, "wgu_s")
        wdn_s = self.convert_weight(wdn, INTER, D, "wdn_s")
        p.barrier()

        ln = [self.sb("f4_ln%d" % i, [128, C], BF16) for i in range(NDT)]
        xres = [self.sb("f4_xr%d" % i, [128, C], F32) for i in range(2)]
        h2 = [self.sb("f4_h%d" % j, [128, C], BF16) for j in range(NJ)]
        wkg = [self.sb("f4_wkg%d" % i, [128, 1024], BF16) for i in range(4)]
        dk = [self.sb("f4_dk%d" % i, [128, 1024], BF16) for i in range(4)]
        stmp = [self.sb("f4_st%d" % i, [128, C], BF16) for i in range(2)]
        otile = [self.sb("f4_ot%d" % i, [128, C], F32) for i in range(2)]

        for ch in range(NCH):
            self.ln_chunk(xr_in, ch, ln)
            for g in range(16):
                for kt in range(16):
                    bg = wkg[kt % 4]
                    gb = 4 * g * 128
                    srcv = wgu_s[kt].rearrange("p (blk m) -> p blk m", m=8192)[:, :, gb:gb + 512]
                    p.dma("y" if kt % 2 == 0 else "g",
                          bg[:].rearrange("p (blk m) -> p blk m", m=512), srcv,
                          writes=[bg], track_out=False)
                    for oi in range(8):
                        col = (oi // 4) * 512 + (oi % 4) * 128
                        p.acc("t", lambda e, b=bg, col=col, oi=oi, kt=kt:
                              e.matmul(pb[oi][:, :C], b[:, col:col + 128], ln[kt][:],
                                       start=(kt == 0), stop=(kt == 15)),
                              reads=[bg, ln[kt]], bank=pb[oi],
                              first=(kt == 0), last=(kt == 15))
                for j in range(4):
                    st = stmp[j % 2]
                    p.emit("s", lambda e, j=j, st=st: e.activation(st[:], pb[j][:, :C], AF.Silu),
                           reads=[pb[j]], writes=[st])
                    jj = 4 * g + j
                    p.emit("v", lambda e, j=j, st=st, jj=jj:
                           e.tensor_tensor(h2[jj][:], st[:], pb[4 + j][:, :C], ALU.mult),
                           reads=[st, pb[4 + j]], writes=[h2[jj]])
            for g in range(2):
                for jt in range(NJ):
                    b = dk[jt % 4]
                    p.dma("y" if jt % 2 == 0 else "g", b[:],
                          wdn_s[jt, :, g * 1024:(g + 1) * 1024], writes=[b])
                    for oi in range(8):
                        p.acc("t", lambda e, b=b, oi=oi, jt=jt:
                              e.matmul(pb[oi][:, :C], b[:, oi * 128:(oi + 1) * 128], h2[jt][:],
                                       start=(jt == 0), stop=(jt == NJ - 1)),
                              reads=[b, h2[jt]], bank=pb[oi],
                              first=(jt == 0), last=(jt == NJ - 1))
                for oi in range(8):
                    i = g * 8 + oi
                    ob = otile[oi % 2]
                    xv = xres[oi % 2]
                    p.dma("y", xv[:], xr_in[i * 128:(i + 1) * 128, ch * C:(ch + 1) * C])
                    p.emit("v", lambda e, oi=oi, ob=ob, xv=xv:
                           e.tensor_tensor(ob[:], pb[oi][:, :C], xv[:], ALU.add),
                           reads=[pb[oi], xv], writes=[ob])
                    x_out_writer(i, ch, ob)

    def phase_l0_attn(self, xr_in, x_out_writer):
        p, pb, nc = self.p, self.pb, self.nc
        HEADS, KV = 16, 8
        SC = 1.0 / float(np.sqrt(2048.0))
        wqkv = self.din("L0_attn_WqkvT", [D, 4096])
        wot = self.din("L0_attn_WOT", [D, D])
        wqkv_s = self.convert_weight(wqkv, D, 4096, "wqkv_s")
        wot_s = self.convert_weight(wot, D, D, "wot_s")
        p.barrier()
        qs = self.dscratch("qs", [D, T], BF16)
        ks = self.dscratch("ks", [KV * 128, T], BF16)
        cxs = self.dscratch("cxs", [D, T], BF16)
        cosT, sinT = _rope_tables()
        cos_sb = self.sb("cos_sb", [128, T], BF16)
        sin_sb = self.sb("sin_sb", [128, T], BF16)
        p.dma("y", cos_sb[:], self.const("cos_c", cosT.astype(bfdt))[:])
        p.dma("y", sin_sb[:], self.const("sin_c", sinT.astype(bfdt))[:])
        ident = self.ident_bf()
        vT = [self.sb("vT%d" % kv, [128, T], BF16) for kv in range(KV)]
        Pt = self.sb("Pt", [128, 32 * C], BF16)
        self._Pt, self._vT = Pt, vT
        ln = [self.sb("a1_ln%d" % i, [128, C], BF16) for i in range(NDT)]
        slab = [self.sb("a1_wk%d" % i, [128, 1024], BF16) for i in range(4)]
        rot = self.sb("a1_rot", [128, C], F32)
        t1 = self.sb("a1_t1", [128, C], F32)
        sqb = [self.sb("a1_sq%d" % i, [128, C], BF16) for i in range(2)]
        vtmp = [self.sb("a1_vt%d" % i, [128, C], BF16) for i in range(2)]
        qkt = [self.sb("a1_qk%d" % i, [128, C], BF16) for i in range(2)]
        nqmax = self.sb("a1_nq", [1, 16], F32)
        nkmax = self.sb("a1_nk", [1, 16], F32)
        brow = self.sb("a1_brow", [1, 16], F32)
        tmp11 = self.sb("a1_t11", [1, 1], F32)
        bias_sb = self.sb("a1_bias", [128, 16], F32)
        kbuf = self.sb("a1_kb", [128, T], BF16)
        qbuf = [self.sb("a1_qb%d" % i, [128, C], BF16) for i in range(2)]
        zr_row = self.sb("a1_zr", [1, C], F32)
        zr_bc = self.sb("a1_zbc", [128, C], F32)
        ctile = [self.sb("a1_ct%d" % i, [128, C], BF16) for i in range(2)]
        cbuf = ln
        xres = [self.sb("a1_xr%d" % i, [128, C], F32) for i in range(2)]
        otile = [self.sb("a1_ot%d" % i, [128, C], F32) for i in range(2)]
        p.emit("g", lambda e: e.memset(nqmax[:], 0.0), writes=[nqmax])
        p.emit("g", lambda e: e.memset(nkmax[:], 0.0), writes=[nkmax])

        for ch in range(NCH):
            self.ln_chunk(xr_in, ch, ln)
            c0 = ch * C
            for g in range(4):
                ots = list(range(8 * g, 8 * g + 8))
                for kt in range(16):
                    b = slab[kt % 4]
                    p.dma("y" if kt % 2 == 0 else "g", b[:],
                          wqkv_s[kt, :, g * 1024:(g + 1) * 1024])
                    for oi in range(8):
                        p.acc("t", lambda e, b=b, oi=oi, kt=kt:
                              e.matmul(pb[oi][:, :C], b[:, oi * 128:(oi + 1) * 128],
                                       ln[kt][:], start=(kt == 0), stop=(kt == 15)),
                              reads=[b, ln[kt]], bank=pb[oi],
                              first=(kt == 0), last=(kt == 15))
                for oi in range(8):
                    ot = 8 * g + oi
                    if ot < 24:
                        p.emit("v", lambda e, oi=oi: e.tensor_scalar(
                            rot[0:64, :], pb[oi][64:128, :C], -1.0, None, ALU.mult),
                            reads=[pb[oi]], writes=[rot])
                        p.emit("v", lambda e, oi=oi: e.tensor_copy(
                            rot[64:128, :], pb[oi][0:64, :C]),
                            reads=[pb[oi]], writes=[rot])
                        p.emit("v", lambda e, oi=oi, c0=c0: e.tensor_tensor(
                            t1[:], pb[oi][:, :C], cos_sb[:, c0:c0 + C], ALU.mult),
                            reads=[pb[oi], cos_sb], writes=[t1])
                        p.emit("v", lambda e, c0=c0: e.tensor_tensor(
                            rot[:], rot[:], sin_sb[:, c0:c0 + C], ALU.mult),
                            reads=[rot, sin_sb], writes=[rot])
                        qk = qkt[oi % 2]
                        p.emit("v", lambda e, qk=qk: e.tensor_tensor(
                            qk[:], t1[:], rot[:], ALU.add),
                            reads=[t1, rot], writes=[qk])
                        if ot < 16:
                            dst, r0, tag = qs, ot * 128, "qs:%d:%d" % (ot, ch)
                        else:
                            dst, r0, tag = ks, (ot - 16) * 128, "ks:%d:%d" % (ot - 16, ch)
                        p.dma("y", dst[r0:r0 + 128, c0:c0 + C], qk[:], reads=[qk],
                              writes=[tag], track_out=False)
                        sq = sqb[oi % 2]
                        p.emit("v", lambda e, sq=sq, qk=qk: e.tensor_tensor(
                            sq[:], qk[:], qk[:], ALU.mult), reads=[qk], writes=[sq])
                        p.emit("t", lambda e, oi=oi, sq=sq: e.matmul(
                            pb[oi][:1, :C], self.ones_col_bf[:], sq[:], start=True, stop=True),
                            reads=[self.ones_col_bf, sq], writes=[pb[oi]])
                        p.emit("v", lambda e, oi=oi: e.reduce_max(
                            tmp11[:], pb[oi][:1, :C], AX.X), reads=[pb[oi]], writes=[tmp11])
                        if ot < 16:
                            p.emit("v", lambda e, ot=ot: e.tensor_tensor(
                                nqmax[:, ot:ot + 1], nqmax[:, ot:ot + 1], tmp11[:], ALU.max),
                                reads=[nqmax, tmp11], writes=[nqmax])
                        else:
                            kv = ot - 16
                            for hh in (2 * kv, 2 * kv + 1):
                                p.emit("v", lambda e, hh=hh: e.tensor_tensor(
                                    nkmax[:, hh:hh + 1], nkmax[:, hh:hh + 1], tmp11[:], ALU.max),
                                    reads=[nkmax, tmp11], writes=[nkmax])
                    else:
                        kv = ot - 24
                        vt = vtmp[oi % 2]
                        p.emit("v", lambda e, vt=vt, oi=oi: e.tensor_copy(vt[:], pb[oi][:, :C]),
                               reads=[pb[oi]], writes=[vt])
                        for j in range(4):
                            p.emit("t", lambda e, oi=oi, vt=vt, j=j: e.transpose(
                                pb[oi][:, :].bitcast(BF16)[:, :128],
                                vt[:, j * 128:(j + 1) * 128], ident[:]),
                                reads=[vt, ident], writes=[pb[oi]])
                            cb = (ch * 4 + j) * 128
                            p.emit("v", lambda e, kv=kv, oi=oi, cb=cb: e.tensor_copy(
                                vT[kv][:, cb:cb + 128], pb[oi][:, :].bitcast(BF16)[:, :128]),
                                reads=[pb[oi]], writes=["vT%d:%d" % (kv, cb)])
        p.emit("v", lambda e: e.tensor_tensor(brow[:], nqmax[:], nkmax[:], ALU.mult),
               reads=[nqmax, nkmax], writes=[brow])
        p.emit("s", lambda e: e.activation(brow[:], brow[:], AF.Ln),
               reads=[brow], writes=[brow])
        p.emit("s", lambda e: e.activation(brow[:], brow[:], AF.Exp, scale=0.5),
               reads=[brow], writes=[brow])
        p.emit("v", lambda e: e.tensor_scalar(brow[:], brow[:], -SC, None, ALU.mult),
               reads=[brow], writes=[brow])
        for h in range(16):
            bank = pb[h % 2]
            p.emit("t", lambda e, h=h, bank=bank: e.matmul(
                bank[:, :1], self.ones_row_f[:], brow[:, h:h + 1], start=True, stop=True),
                reads=[self.ones_row_f, brow], writes=[bank])
            p.emit("v", lambda e, h=h, bank=bank: e.tensor_copy(
                bias_sb[:, h:h + 1], bank[:, :1]), reads=[bank], writes=[bias_sb])
        p.barrier()

        for h in range(HEADS):
            kv = h // 2
            p.dma("y", kbuf[:], ks[kv * 128:(kv + 1) * 128, :])
            for qc in range(NCH):
                qb = qbuf[qc % 2]
                p.dma("y", qb[:], qs[h * 128:(h + 1) * 128, qc * C:(qc + 1) * C])
                pend = []
                for kt in range(32):
                    bank = pb[kt % 2]
                    p.emit("t", lambda e, bank=bank, kt=kt, qb=qb: e.matmul(
                        bank[:, :C], kbuf[:, kt * 128:(kt + 1) * 128], qb[:],
                        start=True, stop=True),
                        reads=[kbuf, qb], writes=[bank])
                    p.emit("s", lambda e, bank=bank, kt=kt, h=h: e.activation(
                        Pt[:, kt * C:(kt + 1) * C], bank[:, :C], AF.Exp,
                        bias=bias_sb[:, h:h + 1], scale=SC),
                        reads=[bank, bias_sb], writes=["Pt:%d" % kt])
                    pend.append(kt)
                    if len(pend) > 1:
                        self._attn_zc(pend.pop(0), kv)
                while pend:
                    self._attn_zc(pend.pop(0), kv)
                p.emit("s", lambda e: e.activation(zr_row[:], pb[2][:1, :C], AF.Ln),
                       reads=[pb[2]], writes=[zr_row])
                p.emit("s", lambda e: e.activation(zr_row[:], zr_row[:], AF.Exp, scale=-1.0),
                       reads=[zr_row], writes=[zr_row])
                p.emit("t", lambda e: e.matmul(pb[4][:, :C], self.ones_row_f[:],
                                               zr_row[:], start=True, stop=True),
                       reads=[self.ones_row_f, zr_row], writes=[pb[4]])
                p.emit("v", lambda e: e.tensor_copy(zr_bc[:], pb[4][:, :C]),
                       reads=[pb[4]], writes=[zr_bc])
                ct = ctile[qc % 2]
                p.emit("v", lambda e, ct=ct: e.tensor_tensor(ct[:], pb[3][:, :C],
                                                             zr_bc[:], ALU.mult),
                       reads=[pb[3], zr_bc], writes=[ct])
                p.dma("y", cxs[h * 128:(h + 1) * 128, qc * C:(qc + 1) * C], ct[:],
                      reads=[ct], writes=["cxs:%d:%d" % (h, qc)], track_out=False)
        p.barrier()

        for ch in range(NCH):
            c0 = ch * C
            for i in range(16):
                p.dma("y", cbuf[i][:], cxs[i * 128:(i + 1) * 128, c0:c0 + C])
            for g in range(2):
                for kt in range(16):
                    b = slab[kt % 4]
                    p.dma("y" if kt % 2 == 0 else "g", b[:],
                          wot_s[kt, :, g * 1024:(g + 1) * 1024])
                    for oi in range(8):
                        p.acc("t", lambda e, b=b, oi=oi, kt=kt: e.matmul(
                            pb[oi][:, :C], b[:, oi * 128:(oi + 1) * 128], cbuf[kt][:],
                            start=(kt == 0), stop=(kt == 15)),
                            reads=[b, cbuf[kt]], bank=pb[oi],
                            first=(kt == 0), last=(kt == 15))
                for oi in range(8):
                    i = g * 8 + oi
                    xv, ob = xres[oi % 2], otile[oi % 2]
                    p.dma("y", xv[:], xr_in[i * 128:(i + 1) * 128, c0:c0 + C])
                    p.emit("v", lambda e, oi=oi, xv=xv, ob=ob: e.tensor_tensor(
                        ob[:], pb[oi][:, :C], xv[:], ALU.add),
                        reads=[pb[oi], xv], writes=[ob])
                    x_out_writer(i, ch, ob)

    def _attn_zc(self, kt, kv):
        p, pb = self.p, self.pb
        Pt, vT = self._Pt, self._vT
        p.acc("t", lambda e, kt=kt: e.matmul(
            pb[2][:1, :C], self.ones_col_bf[:], Pt[:, kt * C:(kt + 1) * C],
            start=(kt == 0), stop=(kt == 31)),
            reads=[self.ones_col_bf, "Pt:%d" % kt], bank=pb[2],
            first=(kt == 0), last=(kt == 31))
        p.acc("t", lambda e, kt=kt, kv=kv: e.matmul(
            pb[3][:, :C], vT[kv][:, kt * 128:(kt + 1) * 128],
            Pt[:, kt * C:(kt + 1) * C], start=(kt == 0), stop=(kt == 31)),
            reads=[vT[kv], "Pt:%d" % kt], bank=pb[3],
            first=(kt == 0), last=(kt == 31))

    def _dequant(self, packed_ap, scalesT_ap, n_out_tiles, M_in, scr):
        p, pb = self.p, self.pb
        NW, NB, NG = M_in // 8, M_in // 2, M_in // 128
        if not hasattr(self, "_dq"):
            MX = 5632
            self._dq = dict(
                pk=self.sb("dq_pk", [128, MX // 8], I32),
                n16=self.sb("dq_n16", [128, MX // 2], U16),
                lo=self.sb("dq_lo", [128, MX // 2], U16),
                m=self.sb("dq_m", [128, MX // 2], U16),
                sb_=self.sb("dq_sb", [128, MX // 2], U16),
                bA=self.sb("dq_bA", [128, MX // 2], U16),
                bB=self.sb("dq_bB", [128, MX // 2], U16),
                mge=self.sb("dq_mge", [128, MX // 2], U16),
                bits=self.sb("dq_bits", [128, MX], U16),
                scl=self.sb("dq_scl", [128, MX // 128], mybir.dt.float16),
                sclf=self.sb("dq_sclf", [128, MX // 128], F32),
                cb=self.sb("dq_cb", [128, MX], BF16),
            )
        Q = self._dq
        pk, n16, lo = Q["pk"], Q["n16"], Q["lo"]
        m_, sb_, bA, bB, mge = Q["m"], Q["sb_"], Q["bA"], Q["bB"], Q["mge"]
        bits, scl, sclf, cb = Q["bits"], Q["scl"], Q["sclf"], Q["cb"]
        hi = n16
        ident = self.ident_bf()
        for o in range(n_out_tiles):
            r0 = o * 128
            p.dma("y", pk[:, :NW], packed_ap[r0:r0 + 128, :], writes=[pk], track_out=False)
            p.dma("y", scl[:, :NG], scalesT_ap[r0:r0 + 128, :], writes=[scl], track_out=False)
            p.emit("v", lambda e: e.tensor_copy(n16[:, :NB], pk[:, :NW].bitcast(U8)),
                   reads=[pk], writes=[n16])
            p.emit("v", lambda e: e.tensor_scalar(lo[:, :NB], n16[:, :NB], 15, None, ALU.bitwise_and),
                   reads=[n16], writes=[lo])
            p.emit("v", lambda e: e.tensor_scalar(n16[:, :NB], n16[:, :NB], 4, None, ALU.logical_shift_right),
                   reads=[n16], writes=[n16])
            for srcb, ph in ((lo, 0), (hi, 1)):
                p.emit("v", lambda e, s=srcb: e.tensor_scalar(m_[:, :NB], s[:, :NB], 7, None, ALU.bitwise_and),
                       reads=[srcb], writes=[m_])
                p.emit("v", lambda e, s=srcb: e.tensor_scalar(sb_[:, :NB], s[:, :NB], 8, 12,
                                                              ALU.bitwise_and, ALU.logical_shift_left),
                       reads=[srcb], writes=[sb_])
                p.emit("v", lambda e: e.tensor_single_scalar(bA[:, :NB], m_[:, :NB], 252, ALU.add),
                       reads=[m_], writes=[bA])
                p.emit("v", lambda e: e.tensor_single_scalar(bA[:, :NB], bA[:, :NB], 6, ALU.logical_shift_left),
                       reads=[bA], writes=[bA])
                p.emit("v", lambda e: e.tensor_scalar(bB[:, :NB], m_[:, :NB], 0x3F00, None, ALU.mult),
                       reads=[m_], writes=[bB])
                p.emit("v", lambda e: e.tensor_scalar(mge[:, :NB], m_[:, :NB], 2, None, ALU.is_ge),
                       reads=[m_], writes=[mge])
                p.emit("v", lambda e: e.copy_predicated(bB[:, :NB], mge[:, :NB], bA[:, :NB]),
                       reads=[mge, bA], writes=[bB])
                outv = bits[:, :M_in].rearrange("p (c two) -> p c two", two=2)[:, :, ph]
                p.emit("v", lambda e, o_=outv: e.tensor_tensor(o_, bB[:, :NB], sb_[:, :NB], ALU.bitwise_or),
                       reads=[bB, sb_], writes=["dqbits%d" % ph])
            p.emit("v", lambda e: e.tensor_copy(sclf[:, :NG], scl[:, :NG]),
                   reads=[scl], writes=[sclf])
            bview = bits[:, :M_in].bitcast(BF16).rearrange("p (g i) -> p g i", i=128)
            sview = sclf[:, :NG].unsqueeze(2).to_broadcast([128, NG, 128])
            p.emit("v", lambda e: e.tensor_tensor(bview, bview, sview, ALU.mult),
                   reads=["dqbits0", "dqbits1", sclf], writes=[bits])
            for j in range(NG):
                bank = pb[6 + (j % 2)]
                wqv = bits[:, :M_in].bitcast(BF16)
                p.emit("t", lambda e, j=j, bank=bank, wqv=wqv: e.transpose(
                    bank[:, :].bitcast(BF16)[:, :128], wqv[:, j * 128:(j + 1) * 128], ident[:]),
                    reads=[bits], writes=[bank])
                p.emit("v", lambda e, j=j, bank=bank: e.tensor_copy(
                    cb[:, j * 128:(j + 1) * 128], bank[:, :].bitcast(BF16)[:, :128]),
                    reads=[bank], writes=[cb])
            p.dma("y", scr[o].transpose([1, 0, 2]),
                  cb[:, :M_in].rearrange("p (j m) -> p j m", m=128),
                  reads=[cb], writes=["dqs" + scr.tensor.name + str(o)], track_out=False)

    def phase_l0_ffn(self, xr_in, x_out_writer):
        p, pb = self.p, self.pb
        I2, NJ, NO = 5632, 44, 88
        gup = self.din("L0_ffn_gate_up_packed", [2 * I2, 256], I32)
        gus = self.din("L0_ffn_gate_up_scalesT", [2 * I2, 16], mybir.dt.float16)
        dnp = self.din("L0_ffn_down_packed", [D, 704], I32)
        dns = self.din("L0_ffn_down_scalesT", [D, 44], mybir.dt.float16)
        wgu_scr = self.dscratch("wdq_gu", [NO, 16, 128, 128], BF16)
        wdn_scr = self.dscratch("wdq_dn", [16, NJ, 128, 128], BF16)
        self._dequant(gup, gus, NO, D, wgu_scr)
        self._dequant(dnp, dns, 16, I2, wdn_scr)
        p.barrier()

        ln = [self.sb("f2_ln%d" % i, [128, C], BF16) for i in range(NDT)]
        h2 = [self.sb("f2_h%d" % j, [128, C], BF16) for j in range(NJ)]
        ysb = [self.sb("f2_y%d" % i, [128, C], BF16) for i in range(16)]
        wkg = [self.sb("f2_wkg%d" % i, [128, 512], BF16) for i in range(4)]
        wku = [self.sb("f2_wku%d" % i, [128, 512], BF16) for i in range(4)]
        dk = [self.sb("f2_dk%d" % i, [128, 1024], BF16) for i in range(4)]
        stmp = [self.sb("f2_st%d" % i, [128, C], BF16) for i in range(2)]
        sqy = [self.sb("f2_sqy%d" % i, [128, C], BF16) for i in range(2)]
        rr = self.sb("f2_rr", [1, C], F32)
        rbc = self.sb("f2_rbc", [128, C], F32)
        Q = self._dq
        xres = [Q["bits"][:, :1024].bitcast(F32), Q["bits"][:, 1024:2048].bitcast(F32)]
        otile = [Q["cb"][:, :1024].bitcast(F32), Q["cb"][:, 1024:2048].bitcast(F32)]

        for ch in range(NCH):
            c0 = ch * C
            self.ln_chunk(xr_in, ch, ln)
            for g in range(11):
                for kt in range(16):
                    bg, bu = wkg[kt % 4], wku[kt % 4]
                    p.dma("y" if kt % 2 == 0 else "g",
                          bg[:].rearrange("p (o m) -> p o m", m=128),
                          wgu_scr[4 * g:4 * g + 4, kt].transpose([1, 0, 2]),
                          writes=[bg], track_out=False)
                    p.dma("g" if kt % 2 == 0 else "y",
                          bu[:].rearrange("p (o m) -> p o m", m=128),
                          wgu_scr[44 + 4 * g:44 + 4 * g + 4, kt].transpose([1, 0, 2]),
                          writes=[bu], track_out=False)
                    for oi in range(8):
                        b = bg if oi < 4 else bu
                        col = (oi % 4) * 128
                        p.acc("t", lambda e, b=b, col=col, oi=oi, kt=kt:
                              e.matmul(pb[oi][:, :C], b[:, col:col + 128], ln[kt][:],
                                       start=(kt == 0), stop=(kt == 15)),
                              reads=[b, ln[kt]], bank=pb[oi],
                              first=(kt == 0), last=(kt == 15))
                for j in range(4):
                    st = stmp[j % 2]
                    p.emit("s", lambda e, j=j, st=st: e.activation(st[:], pb[j][:, :C], AF.Silu),
                           reads=[pb[j]], writes=[st])
                    jj = 4 * g + j
                    p.emit("v", lambda e, j=j, st=st, jj=jj:
                           e.tensor_tensor(h2[jj][:], st[:], pb[4 + j][:, :C], ALU.mult),
                           reads=[st, pb[4 + j]], writes=[h2[jj]])
            for g in range(2):
                for jt in range(NJ):
                    b = dk[jt % 4]
                    p.dma("y" if jt % 2 == 0 else "g",
                          b[:].rearrange("p (o m) -> p o m", m=128),
                          wdn_scr[8 * g:8 * g + 8, jt].transpose([1, 0, 2]),
                          writes=[b], track_out=False)
                    for oi in range(8):
                        p.acc("t", lambda e, b=b, oi=oi, jt=jt:
                              e.matmul(pb[oi][:, :C], b[:, oi * 128:(oi + 1) * 128], h2[jt][:],
                                       start=(jt == 0), stop=(jt == NJ - 1)),
                              reads=[b, h2[jt]], bank=pb[oi],
                              first=(jt == 0), last=(jt == NJ - 1))
                for oi in range(8):
                    i = g * 8 + oi
                    p.emit("v", lambda e, oi=oi, i=i: e.tensor_copy(ysb[i][:], pb[oi][:, :C]),
                           reads=[pb[oi]], writes=[ysb[i]])
            for i in range(16):
                sq = sqy[i % 2]
                p.emit("s", lambda e, i=i, sq=sq: e.activation(sq[:], ysb[i][:], AF.Square),
                       reads=[ysb[i]], writes=[sq])
                p.acc("t", lambda e, sq=sq, i=i: e.matmul(pb[0][:1, :C], self.ones_col_bf[:],
                                                          sq[:], start=(i == 0), stop=(i == 15)),
                      reads=[self.ones_col_bf, sq], bank=pb[0],
                      first=(i == 0), last=(i == 15))
            epsb = self.fconst(1e-6)
            p.emit("s", lambda e: e.activation(rr[:], pb[0][:1, :C], AF.Ln,
                                               bias=epsb[:1, :], scale=1.0 / 2048.0),
                   reads=[pb[0], epsb], writes=[rr])
            p.emit("s", lambda e: e.activation(rr[:], rr[:], AF.Exp, scale=-0.5),
                   reads=[rr], writes=[rr])
            p.emit("t", lambda e: e.matmul(pb[1][:, :C], self.ones_row_f[:], rr[:],
                                           start=True, stop=True),
                   reads=[self.ones_row_f, rr], writes=[pb[1]])
            p.emit("v", lambda e: e.tensor_copy(rbc[:], pb[1][:, :C]),
                   reads=[pb[1]], writes=[rbc])
            for i in range(16):
                xv, ob = xres[i % 2], otile[i % 2]
                p.dma("y", xv[:], xr_in[i * 128:(i + 1) * 128, c0:c0 + C])
                p.emit("v", lambda e, i=i, ob=ob: e.tensor_tensor(ob[:], ysb[i][:], rbc[:], ALU.mult),
                       reads=[ysb[i], rbc], writes=[ob])
                p.emit("v", lambda e, ob=ob, xv=xv: e.tensor_tensor(ob[:], ob[:], xv[:], ALU.add),
                       reads=[ob, xv], writes=[ob])
                x_out_writer(i, ch, ob)

    def phase_l1_attn(self, xr_in, x_out_writer):
        p, pb = self.p, self.pb
        NH, HD, DST, QL, NCK = 64, 64, 128, 256, 16
        DI = 4096
        win = self.din("L1_attn_W_inT", [D, 8512])
        wout = self.din("L1_attn_W_outT", [DI, D])
        conv_w = self.din("L1_attn_conv_w", [4352, 4])
        conv_b = self.din("L1_attn_conv_b", [4352])
        dt_bias = self.din("L1_attn_dt_bias", [64])
        A_log = self.din("L1_attn_A_log", [64])
        D_param = self.din("L1_attn_D_param", [64])
        norm_w = self.din("L1_attn_norm_w", [4096])
        self._dtbuf = self.sb("m_dt", [64, T], F32)
        win_s = self.convert_weight(win, D, 8512, "win_s")
        wout_s = self.convert_weight(wout, DI, D, "wout_s")
        p.barrier()
        zs = self.dscratch("zs", [DI, T], BF16)
        xbcp = self.dscratch("xbcp", [4352, T], F32)
        xbcc = self.dscratch("xbcc", [4352, T], BF16)
        acum_scr = self.dscratch("acum_scr", [64, T], F32)
        csscr = self.dscratch("csscr", [NCK, 64], F32)
        yns = self.dscratch("yns", [DI, T], BF16)
        identb = self.ident_bf()
        identf = self.sb("ident_f", [128, 128], F32)
        p.dma("y", identf[:], self.const("identf_c", np.eye(128, dtype=np.float32))[:])

        ln = [self.sb("m_ln%d" % i, [128, C], BF16) for i in range(NDT)]
        slab = [self.sb("m_wk%d" % i, [128, 1024], BF16) for i in range(4)]
        ztile = [self.sb("m_zt%d" % i, [128, C], BF16) for i in range(2)]
        xbt = [self.sb("m_xbt%d" % i, [128, C], F32) for i in range(2)]

        groups = [list(range(8 * g, 8 * g + 8)) for g in range(8)] + [[64, 65, 66]]
        for ch in range(NCH):
            c0 = ch * C
            self.ln_chunk(xr_in, ch, ln)
            for ots in groups:
                g0 = ots[0] * 128
                gw = (ots[-1] * 128 + (64 if ots[-1] == 66 else 128)) - g0
                for kt in range(16):
                    b = slab[kt % 4]
                    p.dma("y" if kt % 2 == 0 else "g", b[:, :gw],
                          win_s[kt, :, g0:g0 + gw], writes=[b], track_out=False)
                    for oi, ot in enumerate(ots):
                        mw = 64 if ot == 66 else 128
                        p.acc("t", lambda e, b=b, oi=oi, kt=kt, mw=mw:
                              e.matmul(pb[oi][:mw, :C], b[:, oi * 128:oi * 128 + mw],
                                       ln[kt][:], start=(kt == 0), stop=(kt == 15)),
                              reads=[b, ln[kt]], bank=pb[oi],
                              first=(kt == 0), last=(kt == 15))
                for oi, ot in enumerate(ots):
                    if ot < 32:
                        zt = ztile[oi % 2]
                        p.emit("v", lambda e, oi=oi, zt=zt: e.tensor_copy(zt[:], pb[oi][:, :C]),
                               reads=[pb[oi]], writes=[zt])
                        p.dma("y", zs[ot * 128:(ot + 1) * 128, c0:c0 + C], zt[:],
                              reads=[zt], writes=["zs:%d:%d" % (ot, ch)], track_out=False)
                    elif ot < 66:
                        xt = xbt[oi % 2]
                        p.emit("v", lambda e, oi=oi, xt=xt: e.tensor_copy(xt[:], pb[oi][:, :C]),
                               reads=[pb[oi]], writes=[xt])
                        r0 = (ot - 32) * 128
                        p.dma("y", xbcp[r0:r0 + 128, c0:c0 + C], xt[:],
                              reads=[xt], writes=["xbcp:%d:%d" % (ot, ch)], track_out=False)
                    else:
                        p.emit("v", lambda e, oi=oi, c0=c0: e.tensor_copy(
                            self._dtbuf[:, c0:c0 + C], pb[oi][:64, :C]),
                            reads=[pb[oi]], writes=[self._dtbuf])
        p.barrier()

        cw_sb = self.sb("m_cw", [128, 34 * 4], F32)
        cb_sb = self.sb("m_cb", [128, 34], F32)
        p.dma("y", cw_sb[:].rearrange("p (a k) -> p a k", k=4),
              conv_w.rearrange("(a p) k -> p a k", p=128), writes=[cw_sb], track_out=False)
        p.dma("y", cb_sb[:], conv_b.rearrange("(a p) -> p a", p=128),
              writes=[cb_sb], track_out=False, allow_slow_non_contiguous=True)
        SEG = 1024
        xin = [self.sb("m_xin%d" % i, [128, SEG + 3], F32) for i in range(2)]
        cacc = self.sb("m_cacc", [128, SEG], F32)
        cvout = self.sb("m_cvout", [128, SEG], BF16)
        for a in range(34):
            r0 = a * 128
            for sg in range(T // SEG):
                xi = xin[sg % 2]
                s0 = sg * SEG
                if sg == 0:
                    p.emit("g", lambda e, xi=xi: e.memset(xi[:, :3], 0.0), writes=[xi])
                    p.dma("y", xi[:, 3:], xbcp[r0:r0 + 128, 0:SEG], writes=[xi],
                          track_out=False)
                else:
                    p.dma("y", xi[:], xbcp[r0:r0 + 128, s0 - 3:s0 + SEG], writes=[xi],
                          track_out=False)
                p.emit("v", lambda e, xi=xi, a=a: e.tensor_scalar(
                    cacc[:], xi[:, 0:SEG], cw_sb[:, 4 * a:4 * a + 1], cb_sb[:, a:a + 1],
                    ALU.mult, ALU.add), reads=[xi, cw_sb, cb_sb], writes=[cacc])
                for k in range(1, 4):
                    p.emit("v", lambda e, xi=xi, a=a, k=k: e.scalar_tensor_tensor(
                        cacc[:], xi[:, k:k + SEG], cw_sb[:, 4 * a + k:4 * a + k + 1],
                        cacc[:], ALU.mult, ALU.add),
                        reads=[xi, cw_sb, cacc], writes=[cacc])
                p.emit("s", lambda e: e.activation(cvout[:], cacc[:], AF.Silu),
                       reads=[cacc], writes=[cvout])
                p.dma("y", xbcc[r0:r0 + 128, s0:s0 + SEG], cvout[:],
                      reads=[cvout], writes=["xbcc:%d:%d" % (a, sg)], track_out=False)
        dtb_col = self.sb("m_dtb", [64, 1], F32)
        acol = self.sb("m_acol", [64, 1], F32)
        dcol = self.sb("m_dcol", [64, 1], F32)
        p.dma("y", dtb_col[:], dt_bias.unsqueeze(1))
        p.dma("y", acol[:], A_log.unsqueeze(1))
        p.dma("y", dcol[:], D_param.unsqueeze(1))
        p.emit("s", lambda e: e.activation(acol[:], acol[:], AF.Exp),
               reads=[acol], writes=[acol])
        p.emit("v", lambda e: e.tensor_scalar(acol[:], acol[:], -1.0, None, ALU.mult),
               reads=[acol], writes=[acol])
        dts = self._dtbuf
        stmp = self.sb("m_stmp", [64, 2048], F32)
        one64 = self.fconst(1.0)
        for sg in range(2):
            s0 = sg * 2048
            sl = slice(s0, s0 + 2048)
            p.emit("v", lambda e, sl=sl: e.tensor_scalar(
                dts[:, sl], dts[:, sl], dtb_col[:], None, ALU.add),
                reads=[dts, dtb_col], writes=[dts])
            p.emit("s", lambda e, sl=sl: e.activation(stmp[:], dts[:, sl], AF.Abs),
                   reads=[dts], writes=[stmp])
            p.emit("s", lambda e: e.activation(stmp[:], stmp[:], AF.Exp, scale=-1.0),
                   reads=[stmp], writes=[stmp])
            p.emit("s", lambda e: e.activation(stmp[:], stmp[:], AF.Ln, bias=one64[:64, :]),
                   reads=[stmp, one64], writes=[stmp])
            p.emit("v", lambda e, sl=sl: e.tensor_scalar(
                dts[:, sl], dts[:, sl], 0.0, None, ALU.max),
                reads=[dts], writes=[dts])
            p.emit("v", lambda e, sl=sl: e.tensor_tensor(
                dts[:, sl], dts[:, sl], stmp[:], ALU.add),
                reads=[dts, stmp], writes=[dts])
        m1 = np.zeros((64, 128), np.float32)
        for k in range(64):
            m1[k, (k % 2) * 64:(k % 2) * 64 + 64] = 1.0
        pat = np.zeros((64, 32), np.float32)
        for k in range(64):
            pat[k, k // 2] = 1.0
        m1_sb = self.sb("m_m1", [64, 128], F32)
        pat_sb = self.sb("m_pat", [64, 32], F32)
        p.dma("y", m1_sb[:], self.const("m1_c", m1)[:])
        p.dma("y", pat_sb[:], self.const("pat_c", pat)[:])
        m2 = self.sb("m_m2", [64, 32], F32)
        p.emit("v", lambda e: e.tensor_tensor(
            m2[:], pat_sb[:], dcol[:].to_broadcast([64, 32]), ALU.mult),
            reads=[pat_sb, dcol], writes=[m2])
        p.emit("t", lambda e: e.matmul(pb[0][:, :32], m1_sb[:], m2[:], start=True, stop=True),
               reads=[m1_sb, m2], writes=[pb[0]])
        dbc = self.sb("m_dbc", [128, 32], F32)
        p.emit("v", lambda e: e.tensor_copy(dbc[:], pb[0][:, :32]),
               reads=[pb[0]], writes=[dbc])
        nw_sb = self.sb("m_nw", [128, 32], F32)
        p.dma("y", nw_sb[:], norm_w.rearrange("(a p) -> p a", p=128),
              writes=[nw_sb], track_out=False, allow_slow_non_contiguous=True)
        p.barrier()

        mask = np.zeros((128, 512), np.float32)
        for st in range(2):
            for pp in range(128):
                s_idx = st * 128 + pp
                mask[pp, st * 256 + s_idx:st * 256 + 256] = 1.0
        mask_sb = self.sb("m_mask", [128, 512], BF16)
        p.dma("y", mask_sb[:], self.const("mask_c", mask.astype(bfdt))[:])
        acum = self.sb("m_acum", [64, QL], F32)
        dAc = self.sb("m_dAc", [64, QL], F32)
        onesQ = self.sb("m_onesQ", [64, QL], F32)
        p.emit("g", lambda e: e.memset(onesQ[:], 1.0), writes=[onesQ])
        dec = self.sb("m_dec", [64, QL], F32)
        decf = self.sb("m_decf", [64, QL], F32)
        dtcb = self.sb("m_dtcb", [64, QL], BF16)
        acT = self.sb("m_acT", [128, 128], F32)
        acTn = self.sb("m_acTn", [128, 128], F32)
        dcT = self.sb("m_dcT", [128, 128], F32)
        dtT = self.sb("m_dtT", [128, 128], BF16)
        Bsb = self.sb("m_B", [128, QL], BF16)
        Csb = self.sb("m_C", [128, QL], BF16)
        BT = self.sb("m_BT", [128, 2 * 128], BF16)
        msc = self.sb("m_msc", [128, 2 * QL], BF16)
        Xsb = [self.sb("m_X%d" % j, [128, QL], BF16) for j in range(32)]
        XT = self.sb("m_XT", [128, 8192], BF16)
        yc = [self.sb("m_y%d" % j, [128, QL], BF16) for j in range(32)]
        P = self.sb("m_P", [128, 4096], F32)
        Pbf = self.sb("m_Pbf", [128, 4096], BF16)
        eA = [self.sb("m_eA%d" % i, [128, QL], BF16) for i in range(2)]
        Csc = [self.sb("m_Csc%d" % i, [128, QL], BF16) for i in range(2)]
        mtmp = [self.sb("m_mtmp%d" % i, [128, 2 * QL], F32) for i in range(2)]
        me = [self.sb("m_me%d" % i, [128, 2 * QL], BF16) for i in range(2)]
        Mt = [[self.sb("m_Mt%d_%d" % (i, st), [128, QL], BF16) for st in range(2)]
              for i in range(2)]
        sBT = [[self.sb("m_sBT%d_%d" % (i, st), [128, 128], BF16) for st in range(2)]
               for i in range(2)]
        arow = [self.sb("m_ar%d" % i, [1, QL], F32) for i in range(2)]
        csrow = self.sb("m_csr", [1, 64], F32)
        excsT = self.sb("m_excsT", [128, 64], F32)
        zt2 = [self.sb("m_z2%d" % j, [128, QL], BF16) for j in range(2)]
        sqg = [self.sb("m_sqg%d" % i, [128, QL], BF16) for i in range(2)]
        gtmp = [self.sb("m_gt%d" % i, [128, QL], BF16) for i in range(2)]
        rr = self.sb("m_rr", [1, QL], F32)
        rbc = self.sb("m_rbc", [128, QL], F32)
        p.emit("g", lambda e: e.memset(P[:], 0.0), writes=[P])

        eps5 = self.fconst(1e-5)
        for c in range(NCK):
            c0 = c * QL
            p.emit("v", lambda e, c0=c0: e.tensor_scalar(
                dAc[:], dts[:, c0:c0 + QL], acol[:], None, ALU.mult),
                reads=[dts, acol], writes=[dAc])
            p.emit("v", lambda e: e.tensor_tensor_scan(
                acum[:], onesQ[:], dAc[:], 0.0, ALU.mult, ALU.add),
                reads=[onesQ, dAc], writes=[acum])
            p.dma("y", acum_scr[:, c0:c0 + QL], acum[:], reads=[acum],
                  writes=["acs:%d" % c], track_out=False)
            p.dma("y", csscr[c].unsqueeze(1), acum[:, QL - 1:QL], reads=[acum],
                  writes=["css:%d" % c], track_out=False)
            p.emit("v", lambda e: e.tensor_scalar(
                dec[:], acum[:], acum[:, QL - 1:QL], None, ALU.subtract),
                reads=[acum], writes=[dec])
            p.emit("s", lambda e: e.activation(decf[:], dec[:], AF.Exp, scale=-1.0),
                   reads=[dec], writes=[decf])
            p.emit("v", lambda e, c0=c0: e.tensor_copy(dtcb[:], dts[:, c0:c0 + QL]),
                   reads=[dts], writes=[dtcb])
            for sub in range(2):
                p.emit("t", lambda e, sub=sub: e.transpose(
                    pb[7][:, :64], acum[:, sub * 128:(sub + 1) * 128], identf[:64, :64]),
                    reads=[acum, identf], writes=[pb[7]])
                p.emit("v", lambda e, sub=sub: e.tensor_copy(
                    acT[:, sub * 64:(sub + 1) * 64], pb[7][:, :64]),
                    reads=[pb[7]], writes=[acT])
                p.emit("t", lambda e, sub=sub: e.transpose(
                    pb[6][:, :64], decf[:, sub * 128:(sub + 1) * 128],
                    identf[:64, :64]), reads=[decf, identf], writes=[pb[6]])
                p.emit("v", lambda e, sub=sub: e.tensor_copy(
                    dcT[:, sub * 64:(sub + 1) * 64], pb[6][:, :64]),
                    reads=[pb[6]], writes=[dcT])
                p.emit("t", lambda e, sub=sub: e.transpose(
                    pb[6][:, :].bitcast(BF16)[:, :64], dtcb[:, sub * 128:(sub + 1) * 128],
                    identb[:64, :64]), reads=[dtcb, identb], writes=[pb[6]])
                p.emit("v", lambda e, sub=sub: e.tensor_copy(
                    dtT[:, sub * 64:(sub + 1) * 64], pb[6][:, :].bitcast(BF16)[:, :64]),
                    reads=[pb[6]], writes=[dtT])
            p.emit("v", lambda e: e.tensor_scalar(acTn[:], acT[:], -1.0, None, ALU.mult),
                   reads=[acT], writes=[acTn])
            p.dma("y", Bsb[:], xbcc[4096:4224, c0:c0 + QL], writes=[Bsb], track_out=False)
            p.dma("y", Csb[:], xbcc[4224:4352, c0:c0 + QL], writes=[Csb], track_out=False)
            for st in range(2):
                p.emit("t", lambda e, st=st: e.transpose(
                    pb[5][:, :].bitcast(BF16)[:, :128], Bsb[:, st * 128:(st + 1) * 128],
                    identb[:]), reads=[Bsb, identb], writes=[pb[5]])
                p.emit("v", lambda e, st=st: e.tensor_copy(
                    BT[:, st * 128:(st + 1) * 128], pb[5][:, :].bitcast(BF16)[:, :128]),
                    reads=[pb[5]], writes=[BT])
                p.emit("t", lambda e, st=st: e.matmul(
                    pb[st][:, :QL], Bsb[:, st * 128:(st + 1) * 128], Csb[:],
                    start=True, stop=True), reads=[Bsb, Csb], writes=[pb[st]])
                p.emit("v", lambda e, st=st: e.tensor_tensor(
                    msc[:, st * QL:(st + 1) * QL], pb[st][:, :QL],
                    mask_sb[:, st * QL:(st + 1) * QL], ALU.mult),
                    reads=[pb[st], mask_sb], writes=[msc])
            for j in range(32):
                p.dma("y", Xsb[j][:], xbcc[j * 128:(j + 1) * 128, c0:c0 + QL],
                      writes=[Xsb[j]], track_out=False)
                for sub in range(2):
                    bank = pb[6 + (j % 2)]
                    p.emit("t", lambda e, j=j, sub=sub, bank=bank: e.transpose(
                        bank[:, :].bitcast(BF16)[:, :128], Xsb[j][:, sub * 128:(sub + 1) * 128],
                        identb[:]), reads=[Xsb[j], identb], writes=[bank])
                    p.emit("v", lambda e, j=j, sub=sub, bank=bank: e.tensor_copy(
                        XT[:, sub * 4096 + j * 128:sub * 4096 + (j + 1) * 128],
                        bank[:, :].bitcast(BF16)[:, :128]),
                        reads=[bank], writes=["XTs%d" % sub])
            for sub in range(2):
                xv = XT[:, sub * 4096:(sub + 1) * 4096].rearrange("p (h i) -> p h i", i=64)
                dv = dtT[:, sub * 64:(sub + 1) * 64].unsqueeze(2).to_broadcast([128, 64, 64])
                p.emit("v", lambda e, xv=xv, dv=dv: e.tensor_tensor(xv, xv, dv, ALU.mult),
                       reads=[dtT], writes=["XTs%d" % sub])
            p.emit("v", lambda e: e.tensor_copy(Pbf[:], P[:]), reads=[P], writes=[Pbf])
            p.dma("g", csrow[:], csscr[c].unsqueeze(0), writes=[csrow],
                  track_out=False, reads=["css:%d" % c])
            p.emit("s", lambda e: e.activation(csrow[:], csrow[:], AF.Exp),
                   reads=[csrow], writes=[csrow])
            p.emit("t", lambda e: e.matmul(pb[2][:, :64], self.ones_row_f[:], csrow[:],
                                           start=True, stop=True),
                   reads=[self.ones_row_f, csrow], writes=[pb[2]])
            p.emit("v", lambda e: e.tensor_copy(excsT[:], pb[2][:, :64]),
                   reads=[pb[2]], writes=[excsT])
            pv = P[:].rearrange("p (h i) -> p h i", i=64)
            ev = excsT[:].unsqueeze(2).to_broadcast([128, 64, 64])
            p.emit("v", lambda e, pv=pv, ev=ev: e.tensor_tensor(pv, pv, ev, ALU.mult),
                   reads=[excsT, P], writes=[P])
            for h in range(NH):
                par = h % 2
                bA, bY, bS = pb[0 + par], pb[2 + par], pb[4 + par]
                ar = arow[par]
                eAp, Cscp, mtp, mep = eA[par], Csc[par], mtmp[par], me[par]
                Mtp, sBTp = Mt[par], sBT[par]
                p.dma("g", ar[:], acum_scr[h:h + 1, c0:c0 + QL], writes=[ar],
                      track_out=False, reads=["acs:%d" % c])
                p.emit("t", lambda e, ar=ar, bA=bA: e.matmul(
                    bA[:, :QL], self.ones_row_f[:], ar[:], start=True, stop=True),
                    reads=[self.ones_row_f, ar], writes=[bA])
                p.emit("s", lambda e, bA=bA, eAp=eAp: e.activation(eAp[:], bA[:, :QL], AF.Exp),
                       reads=[bA], writes=[eAp])
                p.emit("v", lambda e, eAp=eAp, Cscp=Cscp: e.tensor_tensor(
                    Cscp[:], Csb[:], eAp[:], ALU.mult),
                    reads=[Csb, eAp], writes=[Cscp])
                for st in range(2):
                    p.emit("v", lambda e, st=st, h=h, bA=bA, mtp=mtp: e.tensor_scalar(
                        mtp[:, st * QL:(st + 1) * QL], bA[:, :QL],
                        acTn[:, st * 64 + h:st * 64 + h + 1],
                        0.0, ALU.add, ALU.min),
                        reads=[bA, acTn], writes=[mtp])
                p.emit("s", lambda e, mtp=mtp, mep=mep: e.activation(mep[:], mtp[:], AF.Exp),
                       reads=[mtp], writes=[mep])
                for st in range(2):
                    p.emit("v", lambda e, st=st, mep=mep, Mtp=Mtp: e.tensor_tensor(
                        Mtp[st][:], mep[:, st * QL:(st + 1) * QL],
                        msc[:, st * QL:(st + 1) * QL], ALU.mult),
                        reads=[mep, msc], writes=[Mtp[st]])
                hb = (h % 2) * 64
                p.emit("t", lambda e, h=h, hb=hb, bY=bY, Cscp=Cscp: e.matmul(
                    bY[hb:hb + 64, :QL], Pbf[:, h * 64:(h + 1) * 64], Cscp[:],
                    start=True, stop=False), reads=[Pbf, Cscp], writes=[bY])
                for st in range(2):
                    p.acc("t", lambda e, st=st, h=h, hb=hb, bY=bY, Mtp=Mtp: e.matmul(
                        bY[hb:hb + 64, :QL], XT[:, st * 4096 + h * 64:st * 4096 + h * 64 + 64],
                        Mtp[st][:], start=False, stop=(st == 1)),
                        reads=["XTs%d" % st, Mtp[st]], bank=bY,
                        first=False, last=(st == 1))
                xj, half = h // 2, (h % 2) * 64
                p.emit("v", lambda e, h=h, xj=xj, half=half, bY=bY: e.scalar_tensor_tensor(
                    yc[xj][half:half + 64, :], Xsb[xj][half:half + 64, :],
                    dbc[half:half + 64, xj:xj + 1], bY[half:half + 64, :QL],
                    ALU.mult, ALU.add),
                    reads=[Xsb[xj], dbc, bY], writes=["yck%d" % xj])
                for st in range(2):
                    p.emit("v", lambda e, st=st, h=h, sBTp=sBTp: e.tensor_scalar(
                        sBTp[st][:], BT[:, st * 128:(st + 1) * 128],
                        dcT[:, st * 64 + h:st * 64 + h + 1], None, ALU.mult),
                        reads=[BT, dcT], writes=[sBTp[st]])
                    p.acc("t", lambda e, st=st, h=h, bS=bS, sBTp=sBTp: e.matmul(
                        bS[:, :64], sBTp[st][:], XT[:, st * 4096 + h * 64:st * 4096 + h * 64 + 64],
                        start=(st == 0), stop=(st == 1)),
                        reads=[sBTp[st], "XTs%d" % st], bank=bS,
                        first=(st == 0), last=(st == 1))
                p.emit("v", lambda e, h=h, bS=bS: e.tensor_tensor(
                    P[:, h * 64:(h + 1) * 64], P[:, h * 64:(h + 1) * 64],
                    bS[:, :64], ALU.add), reads=[P, bS], writes=[P])
            for j in range(32):
                z2 = zt2[j % 2]
                p.dma("y", z2[:], zs[j * 128:(j + 1) * 128, c0:c0 + QL],
                      writes=[z2], track_out=False)
                gt = gtmp[j % 2]
                p.emit("s", lambda e, z2=z2, gt=gt: e.activation(gt[:], z2[:], AF.Silu),
                       reads=[z2], writes=[gt])
                p.emit("v", lambda e, j=j, gt=gt: e.tensor_tensor(
                    yc[j][:], yc[j][:], gt[:], ALU.mult),
                    reads=[gt], writes=["yck%d" % j])
                sq = sqg[j % 2]
                p.emit("s", lambda e, j=j, sq=sq: e.activation(sq[:], yc[j][:], AF.Square),
                       reads=["yck%d" % j], writes=[sq])
                p.acc("t", lambda e, sq=sq, j=j: e.matmul(
                    pb[7][:1, :QL], self.ones_col_bf[:], sq[:],
                    start=(j == 0), stop=(j == 31)),
                    reads=[self.ones_col_bf, sq], bank=pb[7],
                    first=(j == 0), last=(j == 31))
            p.emit("s", lambda e: e.activation(rr[:], pb[7][:1, :QL], AF.Ln,
                                               bias=eps5[:1, :], scale=1.0 / 4096.0),
                   reads=[pb[7], eps5], writes=[rr])
            p.emit("s", lambda e: e.activation(rr[:], rr[:], AF.Exp, scale=-0.5),
                   reads=[rr], writes=[rr])
            p.emit("t", lambda e: e.matmul(pb[7][:, :QL], self.ones_row_f[:], rr[:],
                                           start=True, stop=True),
                   reads=[self.ones_row_f, rr], writes=[pb[7]])
            p.emit("v", lambda e: e.tensor_copy(rbc[:], pb[7][:, :QL]),
                   reads=[pb[7]], writes=[rbc])
            for j in range(32):
                p.emit("v", lambda e, j=j: e.scalar_tensor_tensor(
                    yc[j][:], yc[j][:], nw_sb[:, j:j + 1], rbc[:], ALU.mult, ALU.mult),
                    reads=[nw_sb, rbc], writes=["yck%d" % j])
                p.dma("y", yns[j * 128:(j + 1) * 128, c0:c0 + QL], yc[j][:],
                      reads=["yck%d" % j], writes=["yns:%d:%d" % (j, c)],
                      track_out=False)
        p.barrier()

        ynb = Xsb
        xres = xbt
        otile = [xin[0], xin[1]]
        for cc in range(T // QL):
            c0 = cc * QL
            for j in range(32):
                p.dma("y", ynb[j][:], yns[j * 128:(j + 1) * 128, c0:c0 + QL],
                      writes=[ynb[j]], track_out=False)
            for g in range(2):
                for kt in range(32):
                    b = slab[kt % 4]
                    p.dma("y" if kt % 2 == 0 else "g", b[:],
                          wout_s[kt, :, g * 1024:(g + 1) * 1024],
                          writes=[b], track_out=False)
                    for oi in range(8):
                        p.acc("t", lambda e, b=b, oi=oi, kt=kt: e.matmul(
                            pb[oi][:, :QL], b[:, oi * 128:(oi + 1) * 128], ynb[kt][:],
                            start=(kt == 0), stop=(kt == 31)),
                            reads=[b, ynb[kt]], bank=pb[oi],
                            first=(kt == 0), last=(kt == 31))
                for oi in range(8):
                    i = g * 8 + oi
                    xv, ob = xres[oi % 2][:, :QL], otile[oi % 2][:, :QL]
                    xk, ok = xres[oi % 2], otile[oi % 2]
                    p.dma("y", xv, xr_in[i * 128:(i + 1) * 128, c0:c0 + QL],
                          writes=[xk], track_out=False)
                    p.emit("v", lambda e, oi=oi, xv=xv, ob=ob: e.tensor_tensor(
                        ob, pb[oi][:, :QL], xv, ALU.add),
                        reads=[pb[oi], xk], writes=[ok])
                    x_out_writer(i, cc, ob)

    def finish(self):
        self.p.barrier()
        self.p.build()



def _qkv_perm():
    i = np.arange(2048)
    permq = (i % 128) * 16 + i // 128
    j = np.arange(1024)
    permkv = (j % 128) * 8 + j // 128
    return permq, permkv


def prep_l0attn_inputs(W_Q, W_K, W_V, W_O):
    permq, permkv = _qkv_perm()
    W_Q, W_K = np.asarray(W_Q), np.asarray(W_K)
    W_V, W_O = np.asarray(W_V), np.asarray(W_O)
    wq = W_Q[permq].T
    wk = W_K[permkv].T
    wv = W_V[permkv].T
    return {
        "L0_attn_WqkvT": np.ascontiguousarray(np.concatenate([wq, wk, wv], axis=1)),
        "L0_attn_WOT": np.ascontiguousarray(W_O.T),
    }


def build_l0attn_program():
    kb = KB(blocks=(0,))
    xr_in = kb.din("x_in", [D, T])
    x_out = kb.dout("x_out", [D, T])

    def writer(i, ch, ob):
        kb.p.dma("y", x_out[i * 128:(i + 1) * 128, ch * C:(ch + 1) * C], ob[:],
                 reads=[ob], writes=["x_out:%d:%d" % (i, ch)], track_out=False)
    kb.prelude_shared()
    kb.phase_begin('a')
    kb.phase_l0_attn(xr_in, writer)
    kb.finish()
    return kb


def prep_l0ffn_inputs(gup, gus, dnp, dns):
    return {
        "L0_ffn_gate_up_packed": np.ascontiguousarray(np.asarray(gup)),
        "L0_ffn_gate_up_scalesT": np.ascontiguousarray(np.asarray(gus).T),
        "L0_ffn_down_packed": np.ascontiguousarray(np.asarray(dnp)),
        "L0_ffn_down_scalesT": np.ascontiguousarray(np.asarray(dns).T),
    }


def build_l0ffn_program():
    kb = KB(blocks=(1,))
    xr_in = kb.din("x_in", [D, T])
    x_out = kb.dout("x_out", [D, T])

    def writer(i, ch, ob):
        kb.p.dma("y", x_out[i * 128:(i + 1) * 128, ch * C:(ch + 1) * C], ob[:],
                 reads=[ob], writes=["x_out:%d:%d" % (i, ch)], track_out=False)
    kb.prelude_shared()
    kb.phase_begin('b')
    kb.phase_l0_ffn(xr_in, writer)
    kb.finish()
    return kb


def prep_l1attn_inputs(W_in, conv_w, conv_b, dt_bias, A_log, D_param, norm_w, W_out):
    return {
        "L1_attn_W_inT": np.ascontiguousarray(np.asarray(W_in).T),
        "L1_attn_W_outT": np.ascontiguousarray(np.asarray(W_out).T),
        "L1_attn_conv_w": np.ascontiguousarray(np.asarray(conv_w)),
        "L1_attn_conv_b": np.ascontiguousarray(np.asarray(conv_b)),
        "L1_attn_dt_bias": np.ascontiguousarray(np.asarray(dt_bias)),
        "L1_attn_A_log": np.ascontiguousarray(np.asarray(A_log)),
        "L1_attn_D_param": np.ascontiguousarray(np.asarray(D_param)),
        "L1_attn_norm_w": np.ascontiguousarray(np.asarray(norm_w)),
    }


def build_l1attn_program():
    kb = KB(blocks=(2,))
    xr_in = kb.din("x_in", [D, T])
    x_out = kb.dout("x_out", [D, T])

    def writer(i, cc, ob):
        kb.p.dma("y", x_out[i * 128:(i + 1) * 128, cc * 256:(cc + 1) * 256], ob[:],
                 reads=[ob], writes=["x_out:%d:%d" % (i, cc)], track_out=False)
    kb.prelude_shared()
    kb.phase_begin('c')
    kb.phase_l1_attn(xr_in, writer)
    kb.finish()
    return kb


def build_l1ffn_program():
    kb = KB(blocks=(3,))
    xr_in = kb.din("x_in", [D, T])
    x_out = kb.dout("x_out", [D, T])

    def writer(i, ch, ob):
        kb.p.dma("y", x_out[i * 128:(i + 1) * 128, ch * C:(ch + 1) * C], ob[:],
                 reads=[ob], writes=["x_out:%d:%d" % (i, ch)], track_out=False)
    kb.prelude_shared()
    kb.phase_begin('d')
    kb.phase_l1_ffn(xr_in, writer)
    kb.finish()
    return kb



def prep_l1ffn_inputs(L1_ffn_w_gate_up, L1_ffn_w_down):
    return {
        "L1_ffn_w_gate_upT": np.ascontiguousarray(np.asarray(L1_ffn_w_gate_up).T),
        "L1_ffn_w_downT": np.ascontiguousarray(np.asarray(L1_ffn_w_down).T),
    }


def build_fused_program():
    kb = KB()
    x_in = kb.din("x_in", [D, T])
    x_out = kb.dout("x_out", [D, T])
    xr1 = kb.dscratch("xr1", [D, T], F32)
    xr2 = kb.dscratch("xr2", [D, T], F32)
    xr3 = kb.dscratch("xr3", [D, T], F32)
    kb.prelude_shared()

    def mkw(dst, csz):
        def w(i, ch, ob):
            kb.p.dma("y", dst[i * 128:(i + 1) * 128, ch * csz:(ch + 1) * csz],
                     ob[:], reads=[ob],
                     writes=["%s:%d:%d" % (dst.tensor.name, i, ch)],
                     track_out=False)
        return w
    kb.phase_begin("a")
    kb.phase_l0_attn(x_in, mkw(xr1, C))
    kb.phase_begin("b")
    kb.phase_l0_ffn(xr1, mkw(xr2, C))
    kb.phase_begin("c")
    kb.phase_l1_attn(xr2, mkw(xr3, 256))
    kb.phase_begin("d")
    kb.phase_l1_ffn(xr3, mkw(x_out, C))
    kb.finish()
    return kb


_PROGRAMS = {}


def _get_fused():
    if "f" not in _PROGRAMS:
        _PROGRAMS["f"] = build_fused_program()
    return _PROGRAMS["f"]


def kernel(**inputs):
    x = np.asarray(inputs["L0_attn_x"])
    ncores = x.shape[0]
    wm = {}
    wm.update(prep_l0attn_inputs(inputs["L0_attn_W_Q"], inputs["L0_attn_W_K"],
                                 inputs["L0_attn_W_V"], inputs["L0_attn_W_O"]))
    wm.update(prep_l0ffn_inputs(inputs["L0_ffn_gate_up_packed"],
                                inputs["L0_ffn_gate_up_scales"],
                                inputs["L0_ffn_down_packed"],
                                inputs["L0_ffn_down_scales"]))
    wm.update(prep_l1attn_inputs(inputs["L1_attn_W_in"], inputs["L1_attn_conv_w"],
                                 inputs["L1_attn_conv_b"], inputs["L1_attn_dt_bias"],
                                 inputs["L1_attn_A_log"], inputs["L1_attn_D_param"],
                                 inputs["L1_attn_norm_w"], inputs["L1_attn_W_out"]))
    wm.update(prep_l1ffn_inputs(inputs["L1_ffn_w_gate_up"], inputs["L1_ffn_w_down"]))
    kb = _get_fused()
    in_maps = [dict(wm, x_in=np.ascontiguousarray(x[c].T)) for c in range(ncores)]
    res = run_bass_kernel_spmd(kb.nc, in_maps, core_ids=list(range(ncores)))
    out = np.stack([np.asarray(res.results[c]["x_out"]).T for c in range(ncores)], axis=0)
    return np.ascontiguousarray(out.astype(np.float32))


# version 07
# speedup vs baseline: 1.1291x; 1.0960x over previous
import numpy as np
import ml_dtypes
import concourse.bass as bass
import concourse.mybir as mybir
from concourse.bass_utils import run_bass_kernel_spmd

F32 = mybir.dt.float32
BF16 = mybir.dt.bfloat16
I32 = mybir.dt.int32
U8 = mybir.dt.uint8
U16 = mybir.dt.uint16
AF = mybir.ActivationFunctionType
ALU = mybir.AluOpType
AX = mybir.AxisListType
bfdt = ml_dtypes.bfloat16

T = 4096
D = 2048
C = 512
NCH = T // C
NDT = D // 128

QUEUES = ("t", "v", "s", "g", "y")


def _key(r):
    if isinstance(r, str):
        return r
    t = getattr(r, "tensor", None)
    if t is not None:
        return t.name
    return r.name


class Prog:

    def __init__(self, nc, same_engine_sync=True, dma_pool=8):
        self.nc = nc
        self.same_engine_sync = same_engine_sync
        self.instr = {q: [] for q in QUEUES}
        self.sems = {}
        self.cnt = {}
        self.hw = {q: {} for q in QUEUES}
        self.last_w = {}
        self.readers = {}
        self.dma_pool = dma_pool
        self._dma_rr = {q: 0 for q in QUEUES}
        self.eng_sem = {q: q for q in QUEUES}
        self._dma_tag = ""
        for q in QUEUES:
            self._get_sem(q)

    def _get_sem(self, name):
        if name not in self.sems:
            self.sems[name] = self.nc.alloc_semaphore(name="sem_" + name)
            self.cnt[name] = 0
        return self.sems[name]

    def _deps_for(self, reads, writes, extra):
        deps = list(extra)
        for r in reads:
            t = self.last_w.get(_key(r))
            if t is not None:
                deps.append(t)
        for w in writes:
            k = _key(w)
            t = self.last_w.get(k)
            if t is not None:
                deps.append(t)
            deps.extend(self.readers.get(k, ()))
        return deps

    def _register(self, token, reads, writes):
        if token is None:
            return
        for r in reads:
            self.readers.setdefault(_key(r), []).append(token)
        for w in writes:
            k = _key(w)
            self.last_w[k] = token
            self.readers[k] = []

    def _filter(self, q, alldeps):
        waits = []
        for d in alldeps:
            if d is None:
                continue
            s, v = d
            if not self.same_engine_sync and s == q:
                continue
            if self.hw[q].get(s, 0) < v:
                self.hw[q][s] = v
                waits.append((s, v))
        return waits

    def epoch(self, tag):
        for q in QUEUES:
            self.eng_sem[q] = q + tag
            self._get_sem(q + tag)
        self._dma_tag = tag

    def emit(self, q, fn, reads=(), writes=(), deps=(), sem=None, inc=1):
        waits = self._filter(q, self._deps_for(reads, writes, deps))
        token = None
        if inc:
            sname = sem if sem is not None else self.eng_sem[q]
            self._get_sem(sname)
            self.cnt[sname] += inc
            token = (sname, self.cnt[sname])
            if (not self.same_engine_sync) and sname == q \
                    and self.hw[q].get(sname, 0) < self.cnt[sname]:
                self.hw[q][sname] = self.cnt[sname]
            self.instr[q].append((waits, fn, sname, inc))
        else:
            self.instr[q].append((waits, fn, None, 0))
        self._register(token, reads, writes)
        return token

    def acc(self, q, fn, reads=(), bank=None, first=False, last=False, deps=()):
        if first:
            return self.emit(q, fn, reads=reads, writes=[bank], deps=deps)
        token = self.emit(q, fn, reads=reads, deps=deps)
        if last:
            k = _key(bank)
            self.last_w[k] = token
            self.readers[k] = []
        return token

    def dma(self, q, out, in_, reads=(), writes=(), deps=(), sem=None,
            track_out=True, track_in=True, **kw):
        if sem is None:
            sem = "d%s%d%s" % (q, self._dma_rr[q] % self.dma_pool, self._dma_tag)
            self._dma_rr[q] += 1
        self._get_sem(sem)
        deps = list(deps)
        if self.cnt[sem] > 0:
            deps.append((sem, self.cnt[sem]))
        reads = list(reads) + ([in_] if track_in else [])
        writes = list(writes) + ([out] if track_out else [])
        fn = lambda eng: eng.dma_start(out=out, in_=in_, **kw)
        return self.emit(q, fn, reads=reads, writes=writes, deps=deps, sem=sem, inc=16)

    def wait(self, q, deps):
        self.emit(q, None, deps=deps, inc=0)

    def tokens_now(self):
        return [(s, c) for s, c in self.cnt.items() if c > 0]

    def barrier(self, qs=QUEUES, exclude_prefix=None):
        toks = [(s, c) for s, c in self.cnt.items()
                if c > 0 and not (exclude_prefix and s.startswith(exclude_prefix))]
        for q in qs:
            self.wait(q, toks)

    def build(self):
        nc = self.nc
        with nc.Block() as blk:
            def mk(q):
                def body(eng):
                    for waits, fn, sname, inc in self.instr[q]:
                        for s, v in waits:
                            eng.wait_ge(self.sems[s], v)
                        if fn is not None:
                            inst = fn(eng)
                            if sname is not None:
                                inst.then_inc(self.sems[sname], inc)
                return body
            blk.tensor(mk("t"))
            blk.vector(mk("v"))
            blk.scalar(mk("s"))
            blk.gpsimd(mk("g"))
            blk.sync(mk("y"))



def _rope_tables():
    half = 64
    inv_freq = 1.0 / 10000.0 ** (np.arange(half, dtype=np.float64) * 2.0 / 128.0)
    pos = np.arange(T, dtype=np.float64)
    ang = pos[None, :] * inv_freq[:, None]
    ang = np.concatenate([ang, ang], axis=0)
    return np.cos(ang).astype(np.float32), np.sin(ang).astype(np.float32)


class KB:

    def __init__(self, blocks=(0, 1, 2, 3)):
        self.nc = bass.Bass("TRN2", target_bir_lowering=False)
        self.p = Prog(self.nc, same_engine_sync=True, dma_pool=4)
        self.blocks = blocks
        self.inputs = {}
        self._consts = {}
        self._cur = None
        self._arena0 = None
        self.pb = [self.nc.alloc_psum_tensor("pb%d" % i, [128, 512], F32)
                   for i in range(8)]
        self.ones_col_bf = self.sb("ones_col_bf", [128, 1], BF16)
        self.ones_row_f = self.sb("ones_row_f", [1, 128], F32)
        self.p.emit("g", lambda e: e.memset(self.ones_col_bf[:], 1.0),
                    writes=[self.ones_col_bf])
        self.p.emit("g", lambda e: e.memset(self.ones_row_f[:], 1.0),
                    writes=[self.ones_row_f])
        self._ident_bf = None

    def prelude_shared(self):
        self.fconst(1e-5)
        self.fconst(1e-6)
        self.fconst(1.0)
        self.ident_bf()
        self._lnp = dict(
            xf=[self.sb("ln_xf%d" % i, [128, C], F32) for i in range(2)],
            xb=[self.sb("ln_xb%d" % i, [128, C], BF16) for i in range(2)],
            sq=[self.sb("ln_sq%d" % i, [128, C], BF16) for i in range(2)],
            tmp=[self.sb("ln_tmp%d" % i, [128, C], F32) for i in range(2)],
            m=self.sb("m_row", [1, C], F32), v=self.sb("v_row", [1, C], F32),
            a=self.sb("a_row", [1, C], F32), b=self.sb("b_row", [1, C], F32),
        )
        self._cvbuf = [self.sb("cvbuf%d" % i, [128, 2048], BF16) for i in range(2)]
        self._arena0 = self.nc.SBUF_PARTITION_SIZE_BYTES - self.nc.sbuf_bytes_remaining

    def phase_begin(self, tag):
        self.p.barrier(exclude_prefix="cw")
        self.p.epoch(tag)
        self._cur = self._arena0

    def sb(self, name, shape, dtype):
        if self._cur is None:
            return self.nc.alloc_sbuf_tensor(name, shape, dtype)
        nbytes = int(np.prod(shape[1:])) * mybir.dt.size(dtype)
        off = self._cur
        self._cur = (off + nbytes + 31) & ~31
        assert self._cur <= self.nc.SBUF_PARTITION_SIZE_BYTES, \
            "arena overflow at %s (%d)" % (name, self._cur)
        return self.nc.alloc_sbuf_tensor_at(name, shape, dtype, offset=off)

    def din(self, name, shape, dtype=F32):
        if name in self.inputs:
            return self.inputs[name][2]
        ap = self.nc.dram_tensor(name, list(shape), dtype, kind="ExternalInput").ap()
        self.inputs[name] = (tuple(shape), dtype, ap)
        return ap

    def get_ws(self, iname, shape, R, M, sname):
        if not hasattr(self, "_ws"):
            self._ws = {}
        if sname not in self._ws:
            ap = self.din(iname, shape)
            self._ws[sname] = self.convert_weight(ap, R, M, sname)
        return self._ws[sname]

    def dout(self, name, shape, dtype=F32):
        return self.nc.dram_tensor(name, list(shape), dtype, kind="ExternalOutput").ap()

    def dscratch(self, name, shape, dtype):
        return self.nc.dram_tensor(name, list(shape), dtype, kind="Internal").ap()

    def const(self, name, arr):
        if name not in self._consts:
            self._consts[name] = self.nc.inline_tensor(np.ascontiguousarray(arr), name).ap()
        return self._consts[name]

    def fconst(self, val):
        key = "fc_%r" % (val,)
        t = self._consts.get(key)
        if t is None:
            t = self.sb("fc%d" % len(self._consts), [128, 1], F32)
            self.p.emit("g", lambda e: e.memset(t[:], float(val)), writes=[t])
            self._consts[key] = t
        return t

    def ident_bf(self):
        if self._ident_bf is None:
            self._ident_bf = self.sb("ident_bf", [128, 128], BF16)
            src = self.const("ident_c", np.eye(128, dtype=bfdt))
            self.p.dma("y", self._ident_bf[:], src[:])
        return self._ident_bf

    def convert_weight(self, src_ap, R, M, name, tmp_cols=2048):
        scr = self.dscratch(name, [R // 128, 128, M], BF16)
        ncols = min(tmp_cols, M)
        if not hasattr(self, "_cvbuf"):
            self._cvbuf = [self.sb("cvbuf%d" % i, [128, 2048], BF16) for i in range(2)]
        buf = self._cvbuf
        if not hasattr(self, "wready"):
            self.wready = {}
        toks = []
        n = 0
        for kt in range(R // 128):
            for ci, c0 in enumerate(range(0, M, ncols)):
                b = buf[(kt * ((M + ncols - 1) // ncols) + ci) % 2]
                nw = min(ncols, M - c0)
                self.p.dma("g", b[:, :nw], src_ap[kt * 128:(kt + 1) * 128, c0:c0 + nw],
                           writes=[b], track_out=False, sem="cwg%d" % (n % 4))
                toks.append(self.p.dma("y", scr[kt, :, c0:c0 + nw], b[:, :nw], reads=[b],
                                       writes=["%s:%d:%d" % (name, kt, ci)],
                                       track_out=False, sem="cwy%d" % (n % 4)))
                n += 1
        self.wready[name] = toks
        return scr

    def ln_chunk(self, xr, ch, ln, eps=1e-5, nd=NDT,
                 pb_sx=4, pb_sq=5, pb_a=6, pb_b=7):
        p, pb = self.p, self.pb
        c0 = ch * C
        if not hasattr(self, "_lnp"):
            self._lnp = dict(
                xf=[self.sb("ln_xf%d" % i, [128, C], F32) for i in range(2)],
                xb=[self.sb("ln_xb%d" % i, [128, C], BF16) for i in range(2)],
                sq=[self.sb("ln_sq%d" % i, [128, C], BF16) for i in range(2)],
                tmp=[self.sb("ln_tmp%d" % i, [128, C], F32) for i in range(2)],
                m=self.sb("m_row", [1, C], F32), v=self.sb("v_row", [1, C], F32),
                a=self.sb("a_row", [1, C], F32), b=self.sb("b_row", [1, C], F32),
            )
        L = self._lnp
        m_row, v_row, a_row, b_row = L["m"], L["v"], L["a"], L["b"]
        for i in range(nd):
            xf, xb, sq = L["xf"][i % 2], L["xb"][i % 2], L["sq"][i % 2]
            p.dma("y", xf[:], xr[i * 128:(i + 1) * 128, c0:c0 + C])
            p.emit("g", lambda e, xf=xf, xb=xb: e.tensor_copy(xb[:], xf[:]),
                   reads=[xf], writes=[xb])
            p.emit("s", lambda e, xf=xf, sq=sq: e.activation(sq[:], xf[:], AF.Square),
                   reads=[xf], writes=[sq])
            p.acc("t", lambda e, xb=xb, i=i: e.matmul(pb[pb_sx][:1, :C], self.ones_col_bf[:],
                                                      xb[:], start=(i == 0), stop=(i == nd - 1)),
                  reads=[self.ones_col_bf, xb], bank=pb[pb_sx],
                  first=(i == 0), last=(i == nd - 1))
            p.acc("t", lambda e, sq=sq, i=i: e.matmul(pb[pb_sq][:1, :C], self.ones_col_bf[:],
                                                      sq[:], start=(i == 0), stop=(i == nd - 1)),
                  reads=[self.ones_col_bf, sq], bank=pb[pb_sq],
                  first=(i == 0), last=(i == nd - 1))
        dinv = 1.0 / (nd * 128)
        p.emit("v", lambda e: e.tensor_scalar(m_row[:], pb[pb_sx][:1, :C], dinv, None, ALU.mult),
               reads=[pb[pb_sx]], writes=[m_row])
        p.emit("v", lambda e: e.tensor_scalar(v_row[:], pb[pb_sq][:1, :C], dinv, None, ALU.mult),
               reads=[pb[pb_sq]], writes=[v_row])
        p.emit("v", lambda e: e.tensor_tensor(a_row[:], m_row[:], m_row[:], ALU.mult),
               reads=[m_row], writes=[a_row])
        p.emit("v", lambda e: e.tensor_tensor(v_row[:], v_row[:], a_row[:], ALU.subtract),
               reads=[v_row, a_row], writes=[v_row])
        epsb = self.fconst(eps)
        p.emit("s", lambda e: e.activation(a_row[:], v_row[:], AF.Ln, bias=epsb[:1, :]),
               reads=[v_row, epsb], writes=[a_row])
        p.emit("s", lambda e: e.activation(a_row[:], a_row[:], AF.Exp, scale=-0.5),
               reads=[a_row], writes=[a_row])
        p.emit("v", lambda e: e.scalar_tensor_tensor(b_row[:], m_row[:], -1.0, a_row[:],
                                                     ALU.mult, ALU.mult),
               reads=[m_row, a_row], writes=[b_row])
        p.emit("t", lambda e: e.matmul(pb[pb_a][:, :C], self.ones_row_f[:], a_row[:],
                                       start=True, stop=True),
               reads=[self.ones_row_f, a_row], writes=[pb[pb_a]])
        p.emit("t", lambda e: e.matmul(pb[pb_b][:, :C], self.ones_row_f[:], b_row[:],
                                       start=True, stop=True),
               reads=[self.ones_row_f, b_row], writes=[pb[pb_b]])
        for i in range(nd):
            xf, tmp = L["xf"][i % 2], L["tmp"][i % 2]
            p.dma("y", xf[:], xr[i * 128:(i + 1) * 128, c0:c0 + C])
            p.emit("v", lambda e, xf=xf, tmp=tmp: e.tensor_tensor(tmp[:], xf[:], pb[pb_a][:, :C], ALU.mult),
                   reads=[xf, pb[pb_a]], writes=[tmp])
            p.emit("v", lambda e, tmp=tmp, i=i: e.tensor_tensor(ln[i][:], tmp[:], pb[pb_b][:, :C], ALU.add),
                   reads=[tmp, pb[pb_b]], writes=[ln[i]])

    def phase_l1_ffn(self, xr_in, x_out_writer):
        p, pb, nc = self.p, self.pb, self.nc
        INTER = 8192
        NJ = INTER // 128
        NO = 2 * INTER // 128
        wgu_s = self.get_ws("L1_ffn_w_gate_upT", [D, 2 * INTER], D, 2 * INTER, "wgu_s")
        wdn_s = self.get_ws("L1_ffn_w_downT", [INTER, D], INTER, D, "wdn_s")

        ln = [self.sb("f4_ln%d" % i, [128, C], BF16) for i in range(NDT)]
        xres = [self.sb("f4_xr%d" % i, [128, C], F32) for i in range(2)]
        h2 = [self.sb("f4_h%d" % j, [128, C], BF16) for j in range(NJ)]
        wkg = [self.sb("f4_wkg%d" % i, [128, 1024], BF16) for i in range(4)]
        dk = [self.sb("f4_dk%d" % i, [128, 1024], BF16) for i in range(4)]
        stmp = [self.sb("f4_st%d" % i, [128, C], BF16) for i in range(2)]
        otile = [self.sb("f4_ot%d" % i, [128, C], F32) for i in range(2)]

        for ch in range(NCH):
            self.ln_chunk(xr_in, ch, ln)
            for g in range(16):
                for kt in range(16):
                    bg = wkg[kt % 4]
                    gb = 4 * g * 128
                    srcv = wgu_s[kt].rearrange("p (blk m) -> p blk m", m=8192)[:, :, gb:gb + 512]
                    p.dma("y" if kt % 2 == 0 else "g",
                          bg[:].rearrange("p (blk m) -> p blk m", m=512), srcv,
                          writes=[bg], track_out=False, deps=self.wready["wgu_s"])
                    for oi in range(8):
                        col = (oi // 4) * 512 + (oi % 4) * 128
                        p.acc("t", lambda e, b=bg, col=col, oi=oi, kt=kt:
                              e.matmul(pb[oi][:, :C], b[:, col:col + 128], ln[kt][:],
                                       start=(kt == 0), stop=(kt == 15)),
                              reads=[bg, ln[kt]], bank=pb[oi],
                              first=(kt == 0), last=(kt == 15))
                for j in range(4):
                    st = stmp[j % 2]
                    p.emit("s", lambda e, j=j, st=st: e.activation(st[:], pb[j][:, :C], AF.Silu),
                           reads=[pb[j]], writes=[st])
                    jj = 4 * g + j
                    p.emit("v", lambda e, j=j, st=st, jj=jj:
                           e.tensor_tensor(h2[jj][:], st[:], pb[4 + j][:, :C], ALU.mult),
                           reads=[st, pb[4 + j]], writes=[h2[jj]])
            for g in range(2):
                for jt in range(NJ):
                    b = dk[jt % 4]
                    p.dma("y" if jt % 2 == 0 else "g", b[:],
                          wdn_s[jt, :, g * 1024:(g + 1) * 1024], writes=[b],
                          deps=self.wready["wdn_s"])
                    for oi in range(8):
                        p.acc("t", lambda e, b=b, oi=oi, jt=jt:
                              e.matmul(pb[oi][:, :C], b[:, oi * 128:(oi + 1) * 128], h2[jt][:],
                                       start=(jt == 0), stop=(jt == NJ - 1)),
                              reads=[b, h2[jt]], bank=pb[oi],
                              first=(jt == 0), last=(jt == NJ - 1))
                for oi in range(8):
                    i = g * 8 + oi
                    ob = otile[oi % 2]
                    xv = xres[oi % 2]
                    p.dma("y", xv[:], xr_in[i * 128:(i + 1) * 128, ch * C:(ch + 1) * C])
                    p.emit("v", lambda e, oi=oi, ob=ob, xv=xv:
                           e.tensor_tensor(ob[:], pb[oi][:, :C], xv[:], ALU.add),
                           reads=[pb[oi], xv], writes=[ob])
                    x_out_writer(i, ch, ob)

    def phase_l0_attn(self, xr_in, x_out_writer):
        p, pb, nc = self.p, self.pb, self.nc
        HEADS, KV = 16, 8
        SC = 1.0 / float(np.sqrt(2048.0))
        wqkv_s = self.get_ws("L0_attn_WqkvT", [D, 4096], D, 4096, "wqkv_s")
        wot_s = self.get_ws("L0_attn_WOT", [D, D], D, D, "wot_s")
        qs = self.dscratch("qs", [D, T], BF16)
        ks = self.dscratch("ks", [KV * 128, T], BF16)
        cxs = self.dscratch("cxs", [D, T], BF16)
        cosT, sinT = _rope_tables()
        cos_sb = self.sb("cos_sb", [128, T], BF16)
        sin_sb = self.sb("sin_sb", [128, T], BF16)
        p.dma("y", cos_sb[:], self.const("cos_c", cosT.astype(bfdt))[:])
        p.dma("y", sin_sb[:], self.const("sin_c", sinT.astype(bfdt))[:])
        ident = self.ident_bf()
        vT = [self.sb("vT%d" % kv, [128, T], BF16) for kv in range(KV)]
        Pt = self.sb("Pt", [128, 32 * C], BF16)
        self._Pt, self._vT = Pt, vT
        ln = [self.sb("a1_ln%d" % i, [128, C], BF16) for i in range(NDT)]
        slab = [self.sb("a1_wk%d" % i, [128, 1024], BF16) for i in range(4)]
        rot = self.sb("a1_rot", [128, C], F32)
        t1 = self.sb("a1_t1", [128, C], F32)
        sqb = [self.sb("a1_sq%d" % i, [128, C], BF16) for i in range(2)]
        vtmp = [self.sb("a1_vt%d" % i, [128, C], BF16) for i in range(2)]
        qkt = [self.sb("a1_qk%d" % i, [128, C], BF16) for i in range(2)]
        nqmax = self.sb("a1_nq", [1, 16], F32)
        nkmax = self.sb("a1_nk", [1, 16], F32)
        brow = self.sb("a1_brow", [1, 16], F32)
        tmp11 = self.sb("a1_t11", [1, 1], F32)
        bias_sb = self.sb("a1_bias", [128, 16], F32)
        kbuf = [self.sb("a1_kb%d" % i, [128, T], BF16) for i in range(2)]
        qbuf = [self.sb("a1_qb%d" % i, [128, C], BF16) for i in range(2)]
        zr_row = self.sb("a1_zr", [1, C], F32)
        zr_bc = self.sb("a1_zbc", [128, C], F32)
        ctile = [self.sb("a1_ct%d" % i, [128, C], BF16) for i in range(2)]
        cbuf = ln
        xres = [self.sb("a1_xr%d" % i, [128, C], F32) for i in range(2)]
        otile = [self.sb("a1_ot%d" % i, [128, C], F32) for i in range(2)]
        p.emit("g", lambda e: e.memset(nqmax[:], 0.0), writes=[nqmax])
        p.emit("g", lambda e: e.memset(nkmax[:], 0.0), writes=[nkmax])

        for ch in range(NCH):
            self.ln_chunk(xr_in, ch, ln)
            c0 = ch * C
            for g in range(4):
                ots = list(range(8 * g, 8 * g + 8))
                for kt in range(16):
                    b = slab[kt % 4]
                    p.dma("y" if kt % 2 == 0 else "g", b[:],
                          wqkv_s[kt, :, g * 1024:(g + 1) * 1024],
                          deps=self.wready["wqkv_s"])
                    for oi in range(8):
                        p.acc("t", lambda e, b=b, oi=oi, kt=kt:
                              e.matmul(pb[oi][:, :C], b[:, oi * 128:(oi + 1) * 128],
                                       ln[kt][:], start=(kt == 0), stop=(kt == 15)),
                              reads=[b, ln[kt]], bank=pb[oi],
                              first=(kt == 0), last=(kt == 15))
                for oi in range(8):
                    ot = 8 * g + oi
                    if ot < 24:
                        p.emit("v", lambda e, oi=oi: e.tensor_scalar(
                            rot[0:64, :], pb[oi][64:128, :C], -1.0, None, ALU.mult),
                            reads=[pb[oi]], writes=[rot])
                        p.emit("v", lambda e, oi=oi: e.tensor_copy(
                            rot[64:128, :], pb[oi][0:64, :C]),
                            reads=[pb[oi]], writes=[rot])
                        p.emit("v", lambda e, oi=oi, c0=c0: e.tensor_tensor(
                            t1[:], pb[oi][:, :C], cos_sb[:, c0:c0 + C], ALU.mult),
                            reads=[pb[oi], cos_sb], writes=[t1])
                        p.emit("v", lambda e, c0=c0: e.tensor_tensor(
                            rot[:], rot[:], sin_sb[:, c0:c0 + C], ALU.mult),
                            reads=[rot, sin_sb], writes=[rot])
                        qk = qkt[oi % 2]
                        p.emit("v", lambda e, qk=qk: e.tensor_tensor(
                            qk[:], t1[:], rot[:], ALU.add),
                            reads=[t1, rot], writes=[qk])
                        if ot < 16:
                            dst, r0, tag = qs, ot * 128, "qs:%d:%d" % (ot, ch)
                        else:
                            dst, r0, tag = ks, (ot - 16) * 128, "ks:%d:%d" % (ot - 16, ch)
                        p.dma("y", dst[r0:r0 + 128, c0:c0 + C], qk[:], reads=[qk],
                              writes=[tag], track_out=False)
                        sq = sqb[oi % 2]
                        p.emit("v", lambda e, sq=sq, qk=qk: e.tensor_tensor(
                            sq[:], qk[:], qk[:], ALU.mult), reads=[qk], writes=[sq])
                        p.emit("t", lambda e, oi=oi, sq=sq: e.matmul(
                            pb[oi][:1, :C], self.ones_col_bf[:], sq[:], start=True, stop=True),
                            reads=[self.ones_col_bf, sq], writes=[pb[oi]])
                        p.emit("v", lambda e, oi=oi: e.reduce_max(
                            tmp11[:], pb[oi][:1, :C], AX.X), reads=[pb[oi]], writes=[tmp11])
                        if ot < 16:
                            p.emit("v", lambda e, ot=ot: e.tensor_tensor(
                                nqmax[:, ot:ot + 1], nqmax[:, ot:ot + 1], tmp11[:], ALU.max),
                                reads=[nqmax, tmp11], writes=[nqmax])
                        else:
                            kv = ot - 16
                            for hh in (2 * kv, 2 * kv + 1):
                                p.emit("v", lambda e, hh=hh: e.tensor_tensor(
                                    nkmax[:, hh:hh + 1], nkmax[:, hh:hh + 1], tmp11[:], ALU.max),
                                    reads=[nkmax, tmp11], writes=[nkmax])
                    else:
                        kv = ot - 24
                        vt = vtmp[oi % 2]
                        p.emit("v", lambda e, vt=vt, oi=oi: e.tensor_copy(vt[:], pb[oi][:, :C]),
                               reads=[pb[oi]], writes=[vt])
                        for j in range(4):
                            cb = (ch * 4 + j) * 128
                            p.dma("y", vT[kv][:, cb:cb + 128],
                                  vt[:, j * 128:(j + 1) * 128],
                                  reads=[vt], writes=["vT%d:%d" % (kv, cb)],
                                  track_out=False, transpose=True)
        p.emit("v", lambda e: e.tensor_tensor(brow[:], nqmax[:], nkmax[:], ALU.mult),
               reads=[nqmax, nkmax], writes=[brow])
        p.emit("s", lambda e: e.activation(brow[:], brow[:], AF.Ln),
               reads=[brow], writes=[brow])
        p.emit("s", lambda e: e.activation(brow[:], brow[:], AF.Exp, scale=0.5),
               reads=[brow], writes=[brow])
        p.emit("v", lambda e: e.tensor_scalar(brow[:], brow[:], -SC, None, ALU.mult),
               reads=[brow], writes=[brow])
        for h in range(16):
            bank = pb[h % 2]
            p.emit("t", lambda e, h=h, bank=bank: e.matmul(
                bank[:, :1], self.ones_row_f[:], brow[:, h:h + 1], start=True, stop=True),
                reads=[self.ones_row_f, brow], writes=[bank])
            p.emit("v", lambda e, h=h, bank=bank: e.tensor_copy(
                bias_sb[:, h:h + 1], bank[:, :1]), reads=[bank], writes=[bias_sb])
        p.barrier()

        p.dma("y", kbuf[0][:], ks[0:128, :])
        for h in range(HEADS):
            kv = h // 2
            kb_ = kbuf[h % 2]
            if h + 1 < HEADS:
                p.dma("y", kbuf[(h + 1) % 2][:],
                      ks[((h + 1) // 2) * 128:((h + 1) // 2 + 1) * 128, :])
            for qc in range(NCH):
                qb = qbuf[qc % 2]
                p.dma("y", qb[:], qs[h * 128:(h + 1) * 128, qc * C:(qc + 1) * C])
                pend = []
                for kt in range(32):
                    bank = pb[kt % 2]
                    p.emit("t", lambda e, bank=bank, kt=kt, qb=qb, kb_=kb_: e.matmul(
                        bank[:, :C], kb_[:, kt * 128:(kt + 1) * 128], qb[:],
                        start=True, stop=True),
                        reads=[kb_, qb], writes=[bank])
                    p.emit("s", lambda e, bank=bank, kt=kt, h=h: e.activation(
                        Pt[:, kt * C:(kt + 1) * C], bank[:, :C], AF.Exp,
                        bias=bias_sb[:, h:h + 1], scale=SC),
                        reads=[bank, bias_sb], writes=["Pt:%d" % kt])
                    pend.append(kt)
                    if len(pend) > 1:
                        self._attn_zc(pend.pop(0), kv)
                while pend:
                    self._attn_zc(pend.pop(0), kv)
                p.emit("s", lambda e: e.activation(zr_row[:], pb[2][:1, :C], AF.Ln),
                       reads=[pb[2]], writes=[zr_row])
                p.emit("s", lambda e: e.activation(zr_row[:], zr_row[:], AF.Exp, scale=-1.0),
                       reads=[zr_row], writes=[zr_row])
                p.emit("t", lambda e: e.matmul(pb[4][:, :C], self.ones_row_f[:],
                                               zr_row[:], start=True, stop=True),
                       reads=[self.ones_row_f, zr_row], writes=[pb[4]])
                p.emit("v", lambda e: e.tensor_copy(zr_bc[:], pb[4][:, :C]),
                       reads=[pb[4]], writes=[zr_bc])
                ct = ctile[qc % 2]
                p.emit("v", lambda e, ct=ct: e.tensor_tensor(ct[:], pb[3][:, :C],
                                                             zr_bc[:], ALU.mult),
                       reads=[pb[3], zr_bc], writes=[ct])
                p.dma("y", cxs[h * 128:(h + 1) * 128, qc * C:(qc + 1) * C], ct[:],
                      reads=[ct], writes=["cxs:%d:%d" % (h, qc)], track_out=False)
        p.barrier()

        for ch in range(NCH):
            c0 = ch * C
            for i in range(16):
                p.dma("y", cbuf[i][:], cxs[i * 128:(i + 1) * 128, c0:c0 + C])
            for g in range(2):
                for kt in range(16):
                    b = slab[kt % 4]
                    p.dma("y" if kt % 2 == 0 else "g", b[:],
                          wot_s[kt, :, g * 1024:(g + 1) * 1024],
                          deps=self.wready["wot_s"])
                    for oi in range(8):
                        p.acc("t", lambda e, b=b, oi=oi, kt=kt: e.matmul(
                            pb[oi][:, :C], b[:, oi * 128:(oi + 1) * 128], cbuf[kt][:],
                            start=(kt == 0), stop=(kt == 15)),
                            reads=[b, cbuf[kt]], bank=pb[oi],
                            first=(kt == 0), last=(kt == 15))
                for oi in range(8):
                    i = g * 8 + oi
                    xv, ob = xres[oi % 2], otile[oi % 2]
                    p.dma("y", xv[:], xr_in[i * 128:(i + 1) * 128, c0:c0 + C])
                    p.emit("v", lambda e, oi=oi, xv=xv, ob=ob: e.tensor_tensor(
                        ob[:], pb[oi][:, :C], xv[:], ALU.add),
                        reads=[pb[oi], xv], writes=[ob])
                    x_out_writer(i, ch, ob)

    def _attn_zc(self, kt, kv):
        p, pb = self.p, self.pb
        Pt, vT = self._Pt, self._vT
        p.acc("t", lambda e, kt=kt: e.matmul(
            pb[2][:1, :C], self.ones_col_bf[:], Pt[:, kt * C:(kt + 1) * C],
            start=(kt == 0), stop=(kt == 31)),
            reads=[self.ones_col_bf, "Pt:%d" % kt], bank=pb[2],
            first=(kt == 0), last=(kt == 31))
        p.acc("t", lambda e, kt=kt, kv=kv: e.matmul(
            pb[3][:, :C], vT[kv][:, kt * 128:(kt + 1) * 128],
            Pt[:, kt * C:(kt + 1) * C], start=(kt == 0), stop=(kt == 31)),
            reads=[vT[kv], "Pt:%d" % kt], bank=pb[3],
            first=(kt == 0), last=(kt == 31))

    def _dequant(self, packed_ap, scalesT_ap, n_out_tiles, M_in, scr):
        p, pb = self.p, self.pb
        NW, NB, NG = M_in // 8, M_in // 2, M_in // 128
        if not hasattr(self, "_dq"):
            MX = 5632
            self._dq = dict(
                pk=self.sb("dq_pk", [128, MX // 8], I32),
                n16=self.sb("dq_n16", [128, MX // 2], U16),
                lo=self.sb("dq_lo", [128, MX // 2], U16),
                m=self.sb("dq_m", [128, MX // 2], U16),
                sb_=self.sb("dq_sb", [128, MX // 2], U16),
                bA=self.sb("dq_bA", [128, MX // 2], U16),
                bB=self.sb("dq_bB", [128, MX // 2], U16),
                mge=self.sb("dq_mge", [128, MX // 2], U16),
                bits=self.sb("dq_bits", [128, MX], U16),
                scl=self.sb("dq_scl", [128, MX // 128], mybir.dt.float16),
                sclf=self.sb("dq_sclf", [128, MX // 128], F32),
            )
        Q = self._dq
        pk, n16, lo = Q["pk"], Q["n16"], Q["lo"]
        m_, sb_, bA, bB, mge = Q["m"], Q["sb_"], Q["bA"], Q["bB"], Q["mge"]
        bits, scl, sclf = Q["bits"], Q["scl"], Q["sclf"]
        hi = n16
        ident = self.ident_bf()
        for o in range(n_out_tiles):
            r0 = o * 128
            p.dma("y", pk[:, :NW], packed_ap[r0:r0 + 128, :], writes=[pk], track_out=False)
            p.dma("y", scl[:, :NG], scalesT_ap[r0:r0 + 128, :], writes=[scl], track_out=False)
            p.emit("v", lambda e: e.tensor_copy(n16[:, :NB], pk[:, :NW].bitcast(U8)),
                   reads=[pk], writes=[n16])
            p.emit("v", lambda e: e.tensor_scalar(lo[:, :NB], n16[:, :NB], 15, None, ALU.bitwise_and),
                   reads=[n16], writes=[lo])
            p.emit("v", lambda e: e.tensor_scalar(n16[:, :NB], n16[:, :NB], 4, None, ALU.logical_shift_right),
                   reads=[n16], writes=[n16])
            for srcb, ph in ((lo, 0), (hi, 1)):
                p.emit("v", lambda e, s=srcb: e.tensor_scalar(m_[:, :NB], s[:, :NB], 7, None, ALU.bitwise_and),
                       reads=[srcb], writes=[m_])
                p.emit("v", lambda e, s=srcb: e.tensor_scalar(sb_[:, :NB], s[:, :NB], 8, 12,
                                                              ALU.bitwise_and, ALU.logical_shift_left),
                       reads=[srcb], writes=[sb_])
                p.emit("v", lambda e: e.tensor_single_scalar(bA[:, :NB], m_[:, :NB], 252, ALU.add),
                       reads=[m_], writes=[bA])
                p.emit("v", lambda e: e.tensor_single_scalar(bA[:, :NB], bA[:, :NB], 6, ALU.logical_shift_left),
                       reads=[bA], writes=[bA])
                p.emit("v", lambda e: e.tensor_scalar(bB[:, :NB], m_[:, :NB], 0x3F00, None, ALU.mult),
                       reads=[m_], writes=[bB])
                p.emit("v", lambda e: e.tensor_scalar(mge[:, :NB], m_[:, :NB], 2, None, ALU.is_ge),
                       reads=[m_], writes=[mge])
                p.emit("v", lambda e: e.copy_predicated(bB[:, :NB], mge[:, :NB], bA[:, :NB]),
                       reads=[mge, bA], writes=[bB])
                outv = bits[:, :M_in].rearrange("p (c two) -> p c two", two=2)[:, :, ph]
                p.emit("v", lambda e, o_=outv: e.tensor_tensor(o_, bB[:, :NB], sb_[:, :NB], ALU.bitwise_or),
                       reads=[bB, sb_], writes=[bits])
            p.emit("v", lambda e: e.tensor_copy(sclf[:, :NG], scl[:, :NG]),
                   reads=[scl], writes=[sclf])
            bview = bits[:, :M_in].bitcast(BF16).rearrange("p (g i) -> p g i", i=128)
            sview = sclf[:, :NG].unsqueeze(2).to_broadcast([128, NG, 128])
            p.emit("v", lambda e: e.tensor_tensor(bview, bview, sview, ALU.mult),
                   reads=[sclf], writes=[bits])
            p.dma("y", scr[o * 128:(o + 1) * 128, :], bits[:, :M_in].bitcast(BF16),
                  reads=[bits], writes=["dqs" + scr.tensor.name + str(o)],
                  track_out=False)

    def phase_l0_ffn(self, xr_in, x_out_writer):
        p, pb = self.p, self.pb
        I2, NJ, NO = 5632, 44, 88
        gup = self.din("L0_ffn_gate_up_packed", [2 * I2, 256], I32)
        gus = self.din("L0_ffn_gate_up_scalesT", [2 * I2, 16], mybir.dt.float16)
        dnp = self.din("L0_ffn_down_packed", [D, 704], I32)
        dns = self.din("L0_ffn_down_scalesT", [D, 44], mybir.dt.float16)
        wgu_scr = self.dscratch("wdq_gu", [NO * 128, D], BF16)
        wdn_scr = self.dscratch("wdq_dn", [16 * 128, I2], BF16)
        self._dequant(gup, gus, NO, D, wgu_scr)
        self._dequant(dnp, dns, 16, I2, wdn_scr)
        p.barrier()

        ln = [self.sb("f2_ln%d" % i, [128, C], BF16) for i in range(NDT)]
        h2 = [self.sb("f2_h%d" % j, [128, C], BF16) for j in range(NJ)]
        ysb = [self.sb("f2_y%d" % i, [128, C], BF16) for i in range(16)]
        wkg = [self.sb("f2_wkg%d" % i, [128, 512], BF16) for i in range(4)]
        wku = [self.sb("f2_wku%d" % i, [128, 512], BF16) for i in range(4)]
        dk = [self.sb("f2_dk%d" % i, [128, 1024], BF16) for i in range(4)]
        stmp = [self.sb("f2_st%d" % i, [128, C], BF16) for i in range(2)]
        sqy = [self.sb("f2_sqy%d" % i, [128, C], BF16) for i in range(2)]
        rr = self.sb("f2_rr", [1, C], F32)
        rbc = self.sb("f2_rbc", [128, C], F32)
        xres = [self.sb("f2_xr%d" % i, [128, C], F32) for i in range(2)]
        otile = [self.sb("f2_ot%d" % i, [128, C], F32) for i in range(2)]

        for ch in range(NCH):
            c0 = ch * C
            self.ln_chunk(xr_in, ch, ln)
            for g in range(11):
                for kt in range(16):
                    bg, bu = wkg[kt % 4], wku[kt % 4]
                    p.dma("y", bg[:],
                          wgu_scr[4 * g * 128:4 * g * 128 + 512,
                                  kt * 128:(kt + 1) * 128],
                          writes=[bg], track_out=False, transpose=True)
                    p.dma("y", bu[:],
                          wgu_scr[(44 + 4 * g) * 128:(44 + 4 * g) * 128 + 512,
                                  kt * 128:(kt + 1) * 128],
                          writes=[bu], track_out=False, transpose=True)
                    for oi in range(8):
                        b = bg if oi < 4 else bu
                        col = (oi % 4) * 128
                        p.acc("t", lambda e, b=b, col=col, oi=oi, kt=kt:
                              e.matmul(pb[oi][:, :C], b[:, col:col + 128], ln[kt][:],
                                       start=(kt == 0), stop=(kt == 15)),
                              reads=[b, ln[kt]], bank=pb[oi],
                              first=(kt == 0), last=(kt == 15))
                for j in range(4):
                    st = stmp[j % 2]
                    p.emit("s", lambda e, j=j, st=st: e.activation(st[:], pb[j][:, :C], AF.Silu),
                           reads=[pb[j]], writes=[st])
                    jj = 4 * g + j
                    p.emit("v", lambda e, j=j, st=st, jj=jj:
                           e.tensor_tensor(h2[jj][:], st[:], pb[4 + j][:, :C], ALU.mult),
                           reads=[st, pb[4 + j]], writes=[h2[jj]])
            for g in range(2):
                for jt in range(NJ):
                    b = dk[jt % 4]
                    p.dma("y", b[:],
                          wdn_scr[g * 1024:(g + 1) * 1024, jt * 128:(jt + 1) * 128],
                          writes=[b], track_out=False, transpose=True)
                    for oi in range(8):
                        p.acc("t", lambda e, b=b, oi=oi, jt=jt:
                              e.matmul(pb[oi][:, :C], b[:, oi * 128:(oi + 1) * 128], h2[jt][:],
                                       start=(jt == 0), stop=(jt == NJ - 1)),
                              reads=[b, h2[jt]], bank=pb[oi],
                              first=(jt == 0), last=(jt == NJ - 1))
                for oi in range(8):
                    i = g * 8 + oi
                    p.emit("v", lambda e, oi=oi, i=i: e.tensor_copy(ysb[i][:], pb[oi][:, :C]),
                           reads=[pb[oi]], writes=[ysb[i]])
            for i in range(16):
                sq = sqy[i % 2]
                p.emit("s", lambda e, i=i, sq=sq: e.activation(sq[:], ysb[i][:], AF.Square),
                       reads=[ysb[i]], writes=[sq])
                p.acc("t", lambda e, sq=sq, i=i: e.matmul(pb[0][:1, :C], self.ones_col_bf[:],
                                                          sq[:], start=(i == 0), stop=(i == 15)),
                      reads=[self.ones_col_bf, sq], bank=pb[0],
                      first=(i == 0), last=(i == 15))
            epsb = self.fconst(1e-6)
            p.emit("s", lambda e: e.activation(rr[:], pb[0][:1, :C], AF.Ln,
                                               bias=epsb[:1, :], scale=1.0 / 2048.0),
                   reads=[pb[0], epsb], writes=[rr])
            p.emit("s", lambda e: e.activation(rr[:], rr[:], AF.Exp, scale=-0.5),
                   reads=[rr], writes=[rr])
            p.emit("t", lambda e: e.matmul(pb[1][:, :C], self.ones_row_f[:], rr[:],
                                           start=True, stop=True),
                   reads=[self.ones_row_f, rr], writes=[pb[1]])
            p.emit("v", lambda e: e.tensor_copy(rbc[:], pb[1][:, :C]),
                   reads=[pb[1]], writes=[rbc])
            for i in range(16):
                xv, ob = xres[i % 2], otile[i % 2]
                p.dma("y", xv[:], xr_in[i * 128:(i + 1) * 128, c0:c0 + C])
                p.emit("v", lambda e, i=i, ob=ob: e.tensor_tensor(ob[:], ysb[i][:], rbc[:], ALU.mult),
                       reads=[ysb[i], rbc], writes=[ob])
                p.emit("v", lambda e, ob=ob, xv=xv: e.tensor_tensor(ob[:], ob[:], xv[:], ALU.add),
                       reads=[ob, xv], writes=[ob])
                x_out_writer(i, ch, ob)

    def phase_l1_attn(self, xr_in, x_out_writer):
        p, pb = self.p, self.pb
        NH, HD, DST, QL, NCK = 64, 64, 128, 256, 16
        DI = 4096
        conv_w = self.din("L1_attn_conv_w", [4352, 4])
        conv_b = self.din("L1_attn_conv_b", [4352])
        dt_bias = self.din("L1_attn_dt_bias", [64])
        A_log = self.din("L1_attn_A_log", [64])
        D_param = self.din("L1_attn_D_param", [64])
        norm_w = self.din("L1_attn_norm_w", [4096])
        self._dtbuf = self.sb("m_dt", [64, T], F32)
        win_s = self.get_ws("L1_attn_W_inT", [D, 8512], D, 8512, "win_s")
        wout_s = self.get_ws("L1_attn_W_outT", [DI, D], DI, D, "wout_s")
        zs = self.dscratch("zs", [DI, T], BF16)
        xbcp = self.dscratch("xbcp", [4352, T], F32)
        xbcc = self.dscratch("xbcc", [4352, T], BF16)
        acum_scr = self.dscratch("acum_scr", [64, T], F32)
        csscr = self.dscratch("csscr", [NCK, 64], F32)
        yns = self.dscratch("yns", [DI, T], BF16)
        identb = self.ident_bf()
        identf = self.sb("ident_f", [128, 128], F32)
        p.dma("y", identf[:], self.const("identf_c", np.eye(128, dtype=np.float32))[:])

        ln = [self.sb("m_ln%d" % i, [128, C], BF16) for i in range(NDT)]
        slab = [self.sb("m_wk%d" % i, [128, 1024], BF16) for i in range(4)]
        ztile = [self.sb("m_zt%d" % i, [128, C], BF16) for i in range(2)]
        xbt = [self.sb("m_xbt%d" % i, [128, C], F32) for i in range(2)]

        groups = [list(range(8 * g, 8 * g + 8)) for g in range(8)] + [[64, 65, 66]]
        for ch in range(NCH):
            c0 = ch * C
            self.ln_chunk(xr_in, ch, ln)
            for ots in groups:
                g0 = ots[0] * 128
                gw = (ots[-1] * 128 + (64 if ots[-1] == 66 else 128)) - g0
                for kt in range(16):
                    b = slab[kt % 4]
                    p.dma("y" if kt % 2 == 0 else "g", b[:, :gw],
                          win_s[kt, :, g0:g0 + gw], writes=[b], track_out=False,
                          deps=self.wready["win_s"])
                    for oi, ot in enumerate(ots):
                        mw = 64 if ot == 66 else 128
                        p.acc("t", lambda e, b=b, oi=oi, kt=kt, mw=mw:
                              e.matmul(pb[oi][:mw, :C], b[:, oi * 128:oi * 128 + mw],
                                       ln[kt][:], start=(kt == 0), stop=(kt == 15)),
                              reads=[b, ln[kt]], bank=pb[oi],
                              first=(kt == 0), last=(kt == 15))
                for oi, ot in enumerate(ots):
                    if ot < 32:
                        zt = ztile[oi % 2]
                        p.emit("v", lambda e, oi=oi, zt=zt: e.tensor_copy(zt[:], pb[oi][:, :C]),
                               reads=[pb[oi]], writes=[zt])
                        p.dma("y", zs[ot * 128:(ot + 1) * 128, c0:c0 + C], zt[:],
                              reads=[zt], writes=["zs:%d:%d" % (ot, ch)], track_out=False)
                    elif ot < 66:
                        xt = xbt[oi % 2]
                        p.emit("v", lambda e, oi=oi, xt=xt: e.tensor_copy(xt[:], pb[oi][:, :C]),
                               reads=[pb[oi]], writes=[xt])
                        r0 = (ot - 32) * 128
                        p.dma("y", xbcp[r0:r0 + 128, c0:c0 + C], xt[:],
                              reads=[xt], writes=["xbcp:%d:%d" % (ot, ch)], track_out=False)
                    else:
                        p.emit("v", lambda e, oi=oi, c0=c0: e.tensor_copy(
                            self._dtbuf[:, c0:c0 + C], pb[oi][:64, :C]),
                            reads=[pb[oi]], writes=[self._dtbuf])
        p.barrier()

        cw_sb = self.sb("m_cw", [128, 34 * 4], F32)
        cb_sb = self.sb("m_cb", [128, 34], F32)
        p.dma("y", cw_sb[:].rearrange("p (a k) -> p a k", k=4),
              conv_w.rearrange("(a p) k -> p a k", p=128), writes=[cw_sb], track_out=False)
        p.dma("y", cb_sb[:], conv_b.rearrange("(a p) -> p a", p=128),
              writes=[cb_sb], track_out=False, allow_slow_non_contiguous=True)
        SEG = 1024
        xin = [self.sb("m_xin%d" % i, [128, SEG + 3], F32) for i in range(2)]
        cacc = self.sb("m_cacc", [128, SEG], F32)
        cvout = self.sb("m_cvout", [128, SEG], BF16)
        for a in range(34):
            r0 = a * 128
            for sg in range(T // SEG):
                xi = xin[sg % 2]
                s0 = sg * SEG
                if sg == 0:
                    p.emit("g", lambda e, xi=xi: e.memset(xi[:, :3], 0.0), writes=[xi])
                    p.dma("y", xi[:, 3:], xbcp[r0:r0 + 128, 0:SEG], writes=[xi],
                          track_out=False)
                else:
                    p.dma("y", xi[:], xbcp[r0:r0 + 128, s0 - 3:s0 + SEG], writes=[xi],
                          track_out=False)
                p.emit("v", lambda e, xi=xi, a=a: e.tensor_scalar(
                    cacc[:], xi[:, 0:SEG], cw_sb[:, 4 * a:4 * a + 1], cb_sb[:, a:a + 1],
                    ALU.mult, ALU.add), reads=[xi, cw_sb, cb_sb], writes=[cacc])
                for k in range(1, 4):
                    p.emit("v", lambda e, xi=xi, a=a, k=k: e.scalar_tensor_tensor(
                        cacc[:], xi[:, k:k + SEG], cw_sb[:, 4 * a + k:4 * a + k + 1],
                        cacc[:], ALU.mult, ALU.add),
                        reads=[xi, cw_sb, cacc], writes=[cacc])
                p.emit("s", lambda e: e.activation(cvout[:], cacc[:], AF.Silu),
                       reads=[cacc], writes=[cvout])
                p.dma("y", xbcc[r0:r0 + 128, s0:s0 + SEG], cvout[:],
                      reads=[cvout], writes=["xbcc:%d:%d" % (a, sg)], track_out=False)
        dtb_col = self.sb("m_dtb", [64, 1], F32)
        acol = self.sb("m_acol", [64, 1], F32)
        dcol = self.sb("m_dcol", [64, 1], F32)
        p.dma("y", dtb_col[:], dt_bias.unsqueeze(1))
        p.dma("y", acol[:], A_log.unsqueeze(1))
        p.dma("y", dcol[:], D_param.unsqueeze(1))
        p.emit("s", lambda e: e.activation(acol[:], acol[:], AF.Exp),
               reads=[acol], writes=[acol])
        p.emit("v", lambda e: e.tensor_scalar(acol[:], acol[:], -1.0, None, ALU.mult),
               reads=[acol], writes=[acol])
        dts = self._dtbuf
        stmp = self.sb("m_stmp", [64, 2048], F32)
        one64 = self.fconst(1.0)
        for sg in range(2):
            s0 = sg * 2048
            sl = slice(s0, s0 + 2048)
            p.emit("v", lambda e, sl=sl: e.tensor_scalar(
                dts[:, sl], dts[:, sl], dtb_col[:], None, ALU.add),
                reads=[dts, dtb_col], writes=[dts])
            p.emit("s", lambda e, sl=sl: e.activation(stmp[:], dts[:, sl], AF.Abs),
                   reads=[dts], writes=[stmp])
            p.emit("s", lambda e: e.activation(stmp[:], stmp[:], AF.Exp, scale=-1.0),
                   reads=[stmp], writes=[stmp])
            p.emit("s", lambda e: e.activation(stmp[:], stmp[:], AF.Ln, bias=one64[:64, :]),
                   reads=[stmp, one64], writes=[stmp])
            p.emit("v", lambda e, sl=sl: e.tensor_scalar(
                dts[:, sl], dts[:, sl], 0.0, None, ALU.max),
                reads=[dts], writes=[dts])
            p.emit("v", lambda e, sl=sl: e.tensor_tensor(
                dts[:, sl], dts[:, sl], stmp[:], ALU.add),
                reads=[dts, stmp], writes=[dts])
        m1 = np.zeros((64, 128), np.float32)
        for k in range(64):
            m1[k, (k % 2) * 64:(k % 2) * 64 + 64] = 1.0
        pat = np.zeros((64, 32), np.float32)
        for k in range(64):
            pat[k, k // 2] = 1.0
        m1_sb = self.sb("m_m1", [64, 128], F32)
        pat_sb = self.sb("m_pat", [64, 32], F32)
        p.dma("y", m1_sb[:], self.const("m1_c", m1)[:])
        p.dma("y", pat_sb[:], self.const("pat_c", pat)[:])
        m2 = self.sb("m_m2", [64, 32], F32)
        p.emit("v", lambda e: e.tensor_tensor(
            m2[:], pat_sb[:], dcol[:].to_broadcast([64, 32]), ALU.mult),
            reads=[pat_sb, dcol], writes=[m2])
        p.emit("t", lambda e: e.matmul(pb[0][:, :32], m1_sb[:], m2[:], start=True, stop=True),
               reads=[m1_sb, m2], writes=[pb[0]])
        dbc = self.sb("m_dbc", [128, 32], F32)
        p.emit("v", lambda e: e.tensor_copy(dbc[:], pb[0][:, :32]),
               reads=[pb[0]], writes=[dbc])
        nw_sb = self.sb("m_nw", [128, 32], F32)
        p.dma("y", nw_sb[:], norm_w.rearrange("(a p) -> p a", p=128),
              writes=[nw_sb], track_out=False, allow_slow_non_contiguous=True)
        p.barrier()

        mask = np.zeros((128, 512), np.float32)
        for st in range(2):
            for pp in range(128):
                s_idx = st * 128 + pp
                mask[pp, st * 256 + s_idx:st * 256 + 256] = 1.0
        mask_sb = self.sb("m_mask", [128, 512], BF16)
        p.dma("y", mask_sb[:], self.const("mask_c", mask.astype(bfdt))[:])
        acum = self.sb("m_acum", [64, QL], F32)
        dAc = self.sb("m_dAc", [64, QL], F32)
        onesQ = self.sb("m_onesQ", [64, QL], F32)
        p.emit("g", lambda e: e.memset(onesQ[:], 1.0), writes=[onesQ])
        dec = self.sb("m_dec", [64, QL], F32)
        decf = self.sb("m_decf", [64, QL], F32)
        dtcb = self.sb("m_dtcb", [64, QL], BF16)
        acT = self.sb("m_acT", [128, 128], F32)
        acTn = self.sb("m_acTn", [128, 128], F32)
        dcT = self.sb("m_dcT", [128, 128], F32)
        dtT = self.sb("m_dtT", [128, 128], BF16)
        Bsb = self.sb("m_B", [128, QL], BF16)
        Csb = self.sb("m_C", [128, QL], BF16)
        BT = self.sb("m_BT", [128, 2 * 128], BF16)
        msc = self.sb("m_msc", [128, 2 * QL], BF16)
        Xsb = [self.sb("m_X%d" % j, [128, QL], BF16) for j in range(32)]
        XT = self.sb("m_XT", [128, 8192], BF16)
        yc = [self.sb("m_y%d" % j, [128, QL], BF16) for j in range(32)]
        P = self.sb("m_P", [128, 4096], F32)
        Pbf = self.sb("m_Pbf", [128, 4096], BF16)
        eA = [self.sb("m_eA%d" % i, [128, QL], BF16) for i in range(2)]
        Csc = [self.sb("m_Csc%d" % i, [128, QL], BF16) for i in range(2)]
        mtmp = [self.sb("m_mtmp%d" % i, [128, 2 * QL], F32) for i in range(2)]
        me = [self.sb("m_me%d" % i, [128, 2 * QL], BF16) for i in range(2)]
        Mt = [[self.sb("m_Mt%d_%d" % (i, st), [128, QL], BF16) for st in range(2)]
              for i in range(2)]
        sBT = [[self.sb("m_sBT%d_%d" % (i, st), [128, 128], BF16) for st in range(2)]
               for i in range(2)]
        arow = [self.sb("m_ar%d" % i, [1, QL], F32) for i in range(2)]
        csrow = self.sb("m_csr", [1, 64], F32)
        excsT = self.sb("m_excsT", [128, 64], F32)
        zt2 = [self.sb("m_z2%d" % j, [128, QL], BF16) for j in range(2)]
        sqg = [self.sb("m_sqg%d" % i, [128, QL], BF16) for i in range(2)]
        gtmp = [self.sb("m_gt%d" % i, [128, QL], BF16) for i in range(2)]
        rr = self.sb("m_rr", [1, QL], F32)
        rbc = self.sb("m_rbc", [128, QL], F32)
        p.emit("g", lambda e: e.memset(P[:], 0.0), writes=[P])

        eps5 = self.fconst(1e-5)
        for c in range(NCK):
            c0 = c * QL
            p.emit("v", lambda e, c0=c0: e.tensor_scalar(
                dAc[:], dts[:, c0:c0 + QL], acol[:], None, ALU.mult),
                reads=[dts, acol], writes=[dAc])
            p.emit("v", lambda e: e.tensor_tensor_scan(
                acum[:], onesQ[:], dAc[:], 0.0, ALU.mult, ALU.add),
                reads=[onesQ, dAc], writes=[acum])
            p.dma("y", acum_scr[:, c0:c0 + QL], acum[:], reads=[acum],
                  writes=["acs:%d" % c], track_out=False)
            p.dma("y", csscr[c].unsqueeze(1), acum[:, QL - 1:QL], reads=[acum],
                  writes=["css:%d" % c], track_out=False)
            p.emit("v", lambda e: e.tensor_scalar(
                dec[:], acum[:], acum[:, QL - 1:QL], None, ALU.subtract),
                reads=[acum], writes=[dec])
            p.emit("s", lambda e: e.activation(decf[:], dec[:], AF.Exp, scale=-1.0),
                   reads=[dec], writes=[decf])
            p.emit("v", lambda e, c0=c0: e.tensor_copy(dtcb[:], dts[:, c0:c0 + QL]),
                   reads=[dts], writes=[dtcb])
            for sub in range(2):
                p.emit("t", lambda e, sub=sub: e.transpose(
                    pb[7][:, :64], acum[:, sub * 128:(sub + 1) * 128], identf[:64, :64]),
                    reads=[acum, identf], writes=[pb[7]])
                p.emit("v", lambda e, sub=sub: e.tensor_copy(
                    acT[:, sub * 64:(sub + 1) * 64], pb[7][:, :64]),
                    reads=[pb[7]], writes=[acT])
                p.emit("t", lambda e, sub=sub: e.transpose(
                    pb[6][:, :64], decf[:, sub * 128:(sub + 1) * 128],
                    identf[:64, :64]), reads=[decf, identf], writes=[pb[6]])
                p.emit("v", lambda e, sub=sub: e.tensor_copy(
                    dcT[:, sub * 64:(sub + 1) * 64], pb[6][:, :64]),
                    reads=[pb[6]], writes=[dcT])
                p.emit("t", lambda e, sub=sub: e.transpose(
                    pb[6][:, :].bitcast(BF16)[:, :64], dtcb[:, sub * 128:(sub + 1) * 128],
                    identb[:64, :64]), reads=[dtcb, identb], writes=[pb[6]])
                p.emit("v", lambda e, sub=sub: e.tensor_copy(
                    dtT[:, sub * 64:(sub + 1) * 64], pb[6][:, :].bitcast(BF16)[:, :64]),
                    reads=[pb[6]], writes=[dtT])
            p.emit("v", lambda e: e.tensor_scalar(acTn[:], acT[:], -1.0, None, ALU.mult),
                   reads=[acT], writes=[acTn])
            p.dma("y", Bsb[:], xbcc[4096:4224, c0:c0 + QL], writes=[Bsb], track_out=False)
            p.dma("y", Csb[:], xbcc[4224:4352, c0:c0 + QL], writes=[Csb], track_out=False)
            for st in range(2):
                p.emit("t", lambda e, st=st: e.transpose(
                    pb[5][:, :].bitcast(BF16)[:, :128], Bsb[:, st * 128:(st + 1) * 128],
                    identb[:]), reads=[Bsb, identb], writes=[pb[5]])
                p.emit("v", lambda e, st=st: e.tensor_copy(
                    BT[:, st * 128:(st + 1) * 128], pb[5][:, :].bitcast(BF16)[:, :128]),
                    reads=[pb[5]], writes=[BT])
                p.emit("t", lambda e, st=st: e.matmul(
                    pb[st][:, :QL], Bsb[:, st * 128:(st + 1) * 128], Csb[:],
                    start=True, stop=True), reads=[Bsb, Csb], writes=[pb[st]])
                p.emit("v", lambda e, st=st: e.tensor_tensor(
                    msc[:, st * QL:(st + 1) * QL], pb[st][:, :QL],
                    mask_sb[:, st * QL:(st + 1) * QL], ALU.mult),
                    reads=[pb[st], mask_sb], writes=[msc])
            for j in range(32):
                p.dma("y", Xsb[j][:], xbcc[j * 128:(j + 1) * 128, c0:c0 + QL],
                      writes=[Xsb[j]], track_out=False)
                for sub in range(2):
                    bank = pb[6 + (j % 2)]
                    p.emit("t", lambda e, j=j, sub=sub, bank=bank: e.transpose(
                        bank[:, :].bitcast(BF16)[:, :128], Xsb[j][:, sub * 128:(sub + 1) * 128],
                        identb[:]), reads=[Xsb[j], identb], writes=[bank])
                    p.emit("v", lambda e, j=j, sub=sub, bank=bank: e.tensor_copy(
                        XT[:, sub * 4096 + j * 128:sub * 4096 + (j + 1) * 128],
                        bank[:, :].bitcast(BF16)[:, :128]),
                        reads=[bank], writes=["XTs%d" % sub])
            for sub in range(2):
                xv = XT[:, sub * 4096:(sub + 1) * 4096].rearrange("p (h i) -> p h i", i=64)
                dv = dtT[:, sub * 64:(sub + 1) * 64].unsqueeze(2).to_broadcast([128, 64, 64])
                p.emit("v", lambda e, xv=xv, dv=dv: e.tensor_tensor(xv, xv, dv, ALU.mult),
                       reads=[dtT], writes=["XTs%d" % sub])
            p.emit("v", lambda e: e.tensor_copy(Pbf[:], P[:]), reads=[P], writes=[Pbf])
            p.dma("g", csrow[:], csscr[c].unsqueeze(0), writes=[csrow],
                  track_out=False, reads=["css:%d" % c])
            p.emit("s", lambda e: e.activation(csrow[:], csrow[:], AF.Exp),
                   reads=[csrow], writes=[csrow])
            p.emit("t", lambda e: e.matmul(pb[2][:, :64], self.ones_row_f[:], csrow[:],
                                           start=True, stop=True),
                   reads=[self.ones_row_f, csrow], writes=[pb[2]])
            p.emit("v", lambda e: e.tensor_copy(excsT[:], pb[2][:, :64]),
                   reads=[pb[2]], writes=[excsT])
            pv = P[:].rearrange("p (h i) -> p h i", i=64)
            ev = excsT[:].unsqueeze(2).to_broadcast([128, 64, 64])
            p.emit("v", lambda e, pv=pv, ev=ev: e.tensor_tensor(pv, pv, ev, ALU.mult),
                   reads=[excsT, P], writes=[P])
            def prep(h):
                par = h % 2
                bA = pb[0 + par]
                ar = arow[par]
                eAp, Cscp, mtp, mep = eA[par], Csc[par], mtmp[par], me[par]
                Mtp, sBTp = Mt[par], sBT[par]
                p.dma("g", ar[:], acum_scr[h:h + 1, c0:c0 + QL], writes=[ar],
                      track_out=False, reads=["acs:%d" % c])
                p.emit("t", lambda e, ar=ar, bA=bA: e.matmul(
                    bA[:, :QL], self.ones_row_f[:], ar[:], start=True, stop=True),
                    reads=[self.ones_row_f, ar], writes=[bA])
                p.emit("s", lambda e, bA=bA, eAp=eAp: e.activation(eAp[:], bA[:, :QL], AF.Exp),
                       reads=[bA], writes=[eAp])
                p.emit("v", lambda e, eAp=eAp, Cscp=Cscp: e.tensor_tensor(
                    Cscp[:], Csb[:], eAp[:], ALU.mult),
                    reads=[Csb, eAp], writes=[Cscp])
                for st in range(2):
                    p.emit("v", lambda e, st=st, h=h, bA=bA, mtp=mtp: e.tensor_scalar(
                        mtp[:, st * QL:(st + 1) * QL], bA[:, :QL],
                        acTn[:, st * 64 + h:st * 64 + h + 1],
                        0.0, ALU.add, ALU.min),
                        reads=[bA, acTn], writes=[mtp])
                p.emit("s", lambda e, mtp=mtp, mep=mep: e.activation(mep[:], mtp[:], AF.Exp),
                       reads=[mtp], writes=[mep])
                for st in range(2):
                    p.emit("v", lambda e, st=st, mep=mep, Mtp=Mtp: e.tensor_tensor(
                        Mtp[st][:], mep[:, st * QL:(st + 1) * QL],
                        msc[:, st * QL:(st + 1) * QL], ALU.mult),
                        reads=[mep, msc], writes=[Mtp[st]])
                for st in range(2):
                    p.emit("v", lambda e, st=st, h=h, sBTp=sBTp: e.tensor_scalar(
                        sBTp[st][:], BT[:, st * 128:(st + 1) * 128],
                        dcT[:, st * 64 + h:st * 64 + h + 1], None, ALU.mult),
                        reads=[BT, dcT], writes=[sBTp[st]])

            def consume(h):
                par = h % 2
                bY, bS = pb[2 + par], pb[4 + par]
                Cscp, Mtp, sBTp = Csc[par], Mt[par], sBT[par]
                hb = (h % 2) * 64
                p.emit("t", lambda e, h=h, hb=hb, bY=bY, Cscp=Cscp: e.matmul(
                    bY[hb:hb + 64, :QL], Pbf[:, h * 64:(h + 1) * 64], Cscp[:],
                    start=True, stop=False), reads=[Pbf, Cscp], writes=[bY])
                for st in range(2):
                    p.acc("t", lambda e, st=st, h=h, hb=hb, bY=bY, Mtp=Mtp: e.matmul(
                        bY[hb:hb + 64, :QL], XT[:, st * 4096 + h * 64:st * 4096 + h * 64 + 64],
                        Mtp[st][:], start=False, stop=(st == 1)),
                        reads=["XTs%d" % st, Mtp[st]], bank=bY,
                        first=False, last=(st == 1))
                for st in range(2):
                    p.acc("t", lambda e, st=st, h=h, bS=bS, sBTp=sBTp: e.matmul(
                        bS[:, :64], sBTp[st][:], XT[:, st * 4096 + h * 64:st * 4096 + h * 64 + 64],
                        start=(st == 0), stop=(st == 1)),
                        reads=[sBTp[st], "XTs%d" % st], bank=bS,
                        first=(st == 0), last=(st == 1))
                xj, half = h // 2, (h % 2) * 64
                p.emit("v", lambda e, h=h, xj=xj, half=half, bY=bY: e.scalar_tensor_tensor(
                    yc[xj][half:half + 64, :], Xsb[xj][half:half + 64, :],
                    dbc[half:half + 64, xj:xj + 1], bY[half:half + 64, :QL],
                    ALU.mult, ALU.add),
                    reads=[Xsb[xj], dbc, bY], writes=["yck%d" % xj])
                p.emit("v", lambda e, h=h, bS=bS: e.tensor_tensor(
                    P[:, h * 64:(h + 1) * 64], P[:, h * 64:(h + 1) * 64],
                    bS[:, :64], ALU.add), reads=[P, bS], writes=[P])

            prep(0)
            for h in range(NH):
                if h + 1 < NH:
                    prep(h + 1)
                consume(h)
            for j in range(32):
                z2 = zt2[j % 2]
                p.dma("y", z2[:], zs[j * 128:(j + 1) * 128, c0:c0 + QL],
                      writes=[z2], track_out=False)
                gt = gtmp[j % 2]
                p.emit("s", lambda e, z2=z2, gt=gt: e.activation(gt[:], z2[:], AF.Silu),
                       reads=[z2], writes=[gt])
                p.emit("v", lambda e, j=j, gt=gt: e.tensor_tensor(
                    yc[j][:], yc[j][:], gt[:], ALU.mult),
                    reads=[gt], writes=["yck%d" % j])
                sq = sqg[j % 2]
                p.emit("s", lambda e, j=j, sq=sq: e.activation(sq[:], yc[j][:], AF.Square),
                       reads=["yck%d" % j], writes=[sq])
                p.acc("t", lambda e, sq=sq, j=j: e.matmul(
                    pb[7][:1, :QL], self.ones_col_bf[:], sq[:],
                    start=(j == 0), stop=(j == 31)),
                    reads=[self.ones_col_bf, sq], bank=pb[7],
                    first=(j == 0), last=(j == 31))
            p.emit("s", lambda e: e.activation(rr[:], pb[7][:1, :QL], AF.Ln,
                                               bias=eps5[:1, :], scale=1.0 / 4096.0),
                   reads=[pb[7], eps5], writes=[rr])
            p.emit("s", lambda e: e.activation(rr[:], rr[:], AF.Exp, scale=-0.5),
                   reads=[rr], writes=[rr])
            p.emit("t", lambda e: e.matmul(pb[7][:, :QL], self.ones_row_f[:], rr[:],
                                           start=True, stop=True),
                   reads=[self.ones_row_f, rr], writes=[pb[7]])
            p.emit("v", lambda e: e.tensor_copy(rbc[:], pb[7][:, :QL]),
                   reads=[pb[7]], writes=[rbc])
            for j in range(32):
                p.emit("v", lambda e, j=j: e.scalar_tensor_tensor(
                    yc[j][:], yc[j][:], nw_sb[:, j:j + 1], rbc[:], ALU.mult, ALU.mult),
                    reads=[nw_sb, rbc], writes=["yck%d" % j])
                p.dma("y", yns[j * 128:(j + 1) * 128, c0:c0 + QL], yc[j][:],
                      reads=["yck%d" % j], writes=["yns:%d:%d" % (j, c)],
                      track_out=False)
        p.barrier()

        ynb = Xsb
        xres = xbt
        otile = [xin[0], xin[1]]
        for cc in range(T // QL):
            c0 = cc * QL
            for j in range(32):
                p.dma("y", ynb[j][:], yns[j * 128:(j + 1) * 128, c0:c0 + QL],
                      writes=[ynb[j]], track_out=False)
            for g in range(2):
                for kt in range(32):
                    b = slab[kt % 4]
                    p.dma("y" if kt % 2 == 0 else "g", b[:],
                          wout_s[kt, :, g * 1024:(g + 1) * 1024],
                          writes=[b], track_out=False, deps=self.wready["wout_s"])
                    for oi in range(8):
                        p.acc("t", lambda e, b=b, oi=oi, kt=kt: e.matmul(
                            pb[oi][:, :QL], b[:, oi * 128:(oi + 1) * 128], ynb[kt][:],
                            start=(kt == 0), stop=(kt == 31)),
                            reads=[b, ynb[kt]], bank=pb[oi],
                            first=(kt == 0), last=(kt == 31))
                for oi in range(8):
                    i = g * 8 + oi
                    xv, ob = xres[oi % 2][:, :QL], otile[oi % 2][:, :QL]
                    xk, ok = xres[oi % 2], otile[oi % 2]
                    p.dma("y", xv, xr_in[i * 128:(i + 1) * 128, c0:c0 + QL],
                          writes=[xk], track_out=False)
                    p.emit("v", lambda e, oi=oi, xv=xv, ob=ob: e.tensor_tensor(
                        ob, pb[oi][:, :QL], xv, ALU.add),
                        reads=[pb[oi], xk], writes=[ok])
                    x_out_writer(i, cc, ob)

    def finish(self):
        self.p.barrier()
        self.p.build()



def _qkv_perm():
    i = np.arange(2048)
    permq = (i % 128) * 16 + i // 128
    j = np.arange(1024)
    permkv = (j % 128) * 8 + j // 128
    return permq, permkv


def prep_l0attn_inputs(W_Q, W_K, W_V, W_O):
    permq, permkv = _qkv_perm()
    W_Q, W_K = np.asarray(W_Q), np.asarray(W_K)
    W_V, W_O = np.asarray(W_V), np.asarray(W_O)
    wq = W_Q[permq].T
    wk = W_K[permkv].T
    wv = W_V[permkv].T
    return {
        "L0_attn_WqkvT": np.ascontiguousarray(np.concatenate([wq, wk, wv], axis=1)),
        "L0_attn_WOT": np.ascontiguousarray(W_O.T),
    }


def build_l0attn_program():
    kb = KB(blocks=(0,))
    xr_in = kb.din("x_in", [D, T])
    x_out = kb.dout("x_out", [D, T])

    def writer(i, ch, ob):
        kb.p.dma("y", x_out[i * 128:(i + 1) * 128, ch * C:(ch + 1) * C], ob[:],
                 reads=[ob], writes=["x_out:%d:%d" % (i, ch)], track_out=False)
    kb.prelude_shared()
    kb.phase_begin('a')
    kb.phase_l0_attn(xr_in, writer)
    kb.finish()
    return kb


def prep_l0ffn_inputs(gup, gus, dnp, dns):
    return {
        "L0_ffn_gate_up_packed": np.ascontiguousarray(np.asarray(gup)),
        "L0_ffn_gate_up_scalesT": np.ascontiguousarray(np.asarray(gus).T),
        "L0_ffn_down_packed": np.ascontiguousarray(np.asarray(dnp)),
        "L0_ffn_down_scalesT": np.ascontiguousarray(np.asarray(dns).T),
    }


def build_l0ffn_program():
    kb = KB(blocks=(1,))
    xr_in = kb.din("x_in", [D, T])
    x_out = kb.dout("x_out", [D, T])

    def writer(i, ch, ob):
        kb.p.dma("y", x_out[i * 128:(i + 1) * 128, ch * C:(ch + 1) * C], ob[:],
                 reads=[ob], writes=["x_out:%d:%d" % (i, ch)], track_out=False)
    kb.prelude_shared()
    kb.phase_begin('b')
    kb.phase_l0_ffn(xr_in, writer)
    kb.finish()
    return kb


def prep_l1attn_inputs(W_in, conv_w, conv_b, dt_bias, A_log, D_param, norm_w, W_out):
    return {
        "L1_attn_W_inT": np.ascontiguousarray(np.asarray(W_in).T),
        "L1_attn_W_outT": np.ascontiguousarray(np.asarray(W_out).T),
        "L1_attn_conv_w": np.ascontiguousarray(np.asarray(conv_w)),
        "L1_attn_conv_b": np.ascontiguousarray(np.asarray(conv_b)),
        "L1_attn_dt_bias": np.ascontiguousarray(np.asarray(dt_bias)),
        "L1_attn_A_log": np.ascontiguousarray(np.asarray(A_log)),
        "L1_attn_D_param": np.ascontiguousarray(np.asarray(D_param)),
        "L1_attn_norm_w": np.ascontiguousarray(np.asarray(norm_w)),
    }


def build_l1attn_program():
    kb = KB(blocks=(2,))
    xr_in = kb.din("x_in", [D, T])
    x_out = kb.dout("x_out", [D, T])

    def writer(i, cc, ob):
        kb.p.dma("y", x_out[i * 128:(i + 1) * 128, cc * 256:(cc + 1) * 256], ob[:],
                 reads=[ob], writes=["x_out:%d:%d" % (i, cc)], track_out=False)
    kb.prelude_shared()
    kb.phase_begin('c')
    kb.phase_l1_attn(xr_in, writer)
    kb.finish()
    return kb


def build_l1ffn_program():
    kb = KB(blocks=(3,))
    xr_in = kb.din("x_in", [D, T])
    x_out = kb.dout("x_out", [D, T])

    def writer(i, ch, ob):
        kb.p.dma("y", x_out[i * 128:(i + 1) * 128, ch * C:(ch + 1) * C], ob[:],
                 reads=[ob], writes=["x_out:%d:%d" % (i, ch)], track_out=False)
    kb.prelude_shared()
    kb.phase_begin('d')
    kb.phase_l1_ffn(xr_in, writer)
    kb.finish()
    return kb



def prep_l1ffn_inputs(L1_ffn_w_gate_up, L1_ffn_w_down):
    return {
        "L1_ffn_w_gate_upT": np.ascontiguousarray(np.asarray(L1_ffn_w_gate_up).T),
        "L1_ffn_w_downT": np.ascontiguousarray(np.asarray(L1_ffn_w_down).T),
    }


def build_fused_program():
    kb = KB()
    x_in = kb.din("x_in", [D, T])
    x_out = kb.dout("x_out", [D, T])
    xr1 = kb.dscratch("xr1", [D, T], F32)
    xr2 = kb.dscratch("xr2", [D, T], F32)
    xr3 = kb.dscratch("xr3", [D, T], F32)
    kb.prelude_shared()
    kb.get_ws("L0_attn_WqkvT", [D, 4096], D, 4096, "wqkv_s")
    kb.get_ws("L0_attn_WOT", [D, D], D, D, "wot_s")
    kb.get_ws("L1_attn_W_inT", [D, 8512], D, 8512, "win_s")
    kb.get_ws("L1_attn_W_outT", [4096, D], 4096, D, "wout_s")
    kb.get_ws("L1_ffn_w_gate_upT", [D, 16384], D, 16384, "wgu_s")
    kb.get_ws("L1_ffn_w_downT", [8192, D], 8192, D, "wdn_s")

    def mkw(dst, csz):
        def w(i, ch, ob):
            kb.p.dma("y", dst[i * 128:(i + 1) * 128, ch * csz:(ch + 1) * csz],
                     ob[:], reads=[ob],
                     writes=["%s:%d:%d" % (dst.tensor.name, i, ch)],
                     track_out=False)
        return w
    kb.phase_begin("a")
    kb.phase_l0_attn(x_in, mkw(xr1, C))
    kb.phase_begin("b")
    kb.phase_l0_ffn(xr1, mkw(xr2, C))
    kb.phase_begin("c")
    kb.phase_l1_attn(xr2, mkw(xr3, 256))
    kb.phase_begin("d")
    kb.phase_l1_ffn(xr3, mkw(x_out, C))
    kb.finish()
    return kb


_PROGRAMS = {}


def _get_fused():
    if "f" not in _PROGRAMS:
        _PROGRAMS["f"] = build_fused_program()
    return _PROGRAMS["f"]


def kernel(**inputs):
    x = np.asarray(inputs["L0_attn_x"])
    ncores = x.shape[0]
    wm = {}
    wm.update(prep_l0attn_inputs(inputs["L0_attn_W_Q"], inputs["L0_attn_W_K"],
                                 inputs["L0_attn_W_V"], inputs["L0_attn_W_O"]))
    wm.update(prep_l0ffn_inputs(inputs["L0_ffn_gate_up_packed"],
                                inputs["L0_ffn_gate_up_scales"],
                                inputs["L0_ffn_down_packed"],
                                inputs["L0_ffn_down_scales"]))
    wm.update(prep_l1attn_inputs(inputs["L1_attn_W_in"], inputs["L1_attn_conv_w"],
                                 inputs["L1_attn_conv_b"], inputs["L1_attn_dt_bias"],
                                 inputs["L1_attn_A_log"], inputs["L1_attn_D_param"],
                                 inputs["L1_attn_norm_w"], inputs["L1_attn_W_out"]))
    wm.update(prep_l1ffn_inputs(inputs["L1_ffn_w_gate_up"], inputs["L1_ffn_w_down"]))
    kb = _get_fused()
    in_maps = [dict(wm, x_in=np.ascontiguousarray(x[c].T)) for c in range(ncores)]
    res = run_bass_kernel_spmd(kb.nc, in_maps, core_ids=list(range(ncores)))
    out = np.stack([np.asarray(res.results[c]["x_out"]).T for c in range(ncores)], axis=0)
    return np.ascontiguousarray(out.astype(np.float32))


# version 08 (final)
# speedup vs baseline: 1.1403x; 1.0099x over previous
import numpy as np
import ml_dtypes
import concourse.bass as bass
import concourse.mybir as mybir
from concourse.bass_utils import run_bass_kernel_spmd

F32 = mybir.dt.float32
BF16 = mybir.dt.bfloat16
I32 = mybir.dt.int32
U8 = mybir.dt.uint8
U16 = mybir.dt.uint16
AF = mybir.ActivationFunctionType
ALU = mybir.AluOpType
AX = mybir.AxisListType
bfdt = ml_dtypes.bfloat16

T = 4096
D = 2048
C = 512
NCH = T // C
NDT = D // 128

QUEUES = ("t", "v", "s", "g", "y")


def _key(r):
    if isinstance(r, str):
        return r
    t = getattr(r, "tensor", None)
    if t is not None:
        return t.name
    return r.name


class Prog:

    def __init__(self, nc, same_engine_sync=True, dma_pool=8):
        self.nc = nc
        self.same_engine_sync = same_engine_sync
        self.instr = {q: [] for q in QUEUES}
        self.sems = {}
        self.cnt = {}
        self.hw = {q: {} for q in QUEUES}
        self.last_w = {}
        self.readers = {}
        self.dma_pool = dma_pool
        self._dma_rr = {q: 0 for q in QUEUES}
        self.eng_sem = {q: q for q in QUEUES}
        self._dma_tag = ""
        for q in QUEUES:
            self._get_sem(q)

    def _get_sem(self, name):
        if name not in self.sems:
            self.sems[name] = self.nc.alloc_semaphore(name="sem_" + name)
            self.cnt[name] = 0
        return self.sems[name]

    def _deps_for(self, reads, writes, extra):
        deps = list(extra)
        for r in reads:
            t = self.last_w.get(_key(r))
            if t is not None:
                deps.append(t)
        for w in writes:
            k = _key(w)
            t = self.last_w.get(k)
            if t is not None:
                deps.append(t)
            deps.extend(self.readers.get(k, ()))
        return deps

    def _register(self, token, reads, writes):
        if token is None:
            return
        for r in reads:
            self.readers.setdefault(_key(r), []).append(token)
        for w in writes:
            k = _key(w)
            self.last_w[k] = token
            self.readers[k] = []

    def _filter(self, q, alldeps):
        waits = []
        for d in alldeps:
            if d is None:
                continue
            s, v = d
            if not self.same_engine_sync and s == q:
                continue
            if self.hw[q].get(s, 0) < v:
                self.hw[q][s] = v
                waits.append((s, v))
        return waits

    def epoch(self, tag):
        for q in QUEUES:
            self.eng_sem[q] = q + tag
            self._get_sem(q + tag)
        self._dma_tag = tag

    def emit(self, q, fn, reads=(), writes=(), deps=(), sem=None, inc=1):
        waits = self._filter(q, self._deps_for(reads, writes, deps))
        token = None
        if inc:
            sname = sem if sem is not None else self.eng_sem[q]
            self._get_sem(sname)
            self.cnt[sname] += inc
            token = (sname, self.cnt[sname])
            if (not self.same_engine_sync) and sname == q \
                    and self.hw[q].get(sname, 0) < self.cnt[sname]:
                self.hw[q][sname] = self.cnt[sname]
            self.instr[q].append((waits, fn, sname, inc))
        else:
            self.instr[q].append((waits, fn, None, 0))
        self._register(token, reads, writes)
        return token

    def acc(self, q, fn, reads=(), bank=None, first=False, last=False, deps=()):
        if first:
            return self.emit(q, fn, reads=reads, writes=[bank], deps=deps)
        token = self.emit(q, fn, reads=reads, deps=deps)
        if last:
            k = _key(bank)
            self.last_w[k] = token
            self.readers[k] = []
        return token

    def dma(self, q, out, in_, reads=(), writes=(), deps=(), sem=None,
            track_out=True, track_in=True, **kw):
        if sem is None:
            sem = "d%s%d%s" % (q, self._dma_rr[q] % self.dma_pool, self._dma_tag)
            self._dma_rr[q] += 1
        self._get_sem(sem)
        deps = list(deps)
        if self.cnt[sem] > 0:
            deps.append((sem, self.cnt[sem]))
        reads = list(reads) + ([in_] if track_in else [])
        writes = list(writes) + ([out] if track_out else [])
        fn = lambda eng: eng.dma_start(out=out, in_=in_, **kw)
        return self.emit(q, fn, reads=reads, writes=writes, deps=deps, sem=sem, inc=16)

    def wait(self, q, deps):
        self.emit(q, None, deps=deps, inc=0)

    def tokens_now(self):
        return [(s, c) for s, c in self.cnt.items() if c > 0]

    def barrier(self, qs=QUEUES, exclude_prefix=None):
        toks = [(s, c) for s, c in self.cnt.items()
                if c > 0 and not (exclude_prefix and s.startswith(exclude_prefix))]
        for q in qs:
            self.wait(q, toks)

    def build(self):
        nc = self.nc
        with nc.Block() as blk:
            def mk(q):
                def body(eng):
                    for waits, fn, sname, inc in self.instr[q]:
                        for s, v in waits:
                            eng.wait_ge(self.sems[s], v)
                        if fn is not None:
                            inst = fn(eng)
                            if sname is not None:
                                inst.then_inc(self.sems[sname], inc)
                return body
            blk.tensor(mk("t"))
            blk.vector(mk("v"))
            blk.scalar(mk("s"))
            blk.gpsimd(mk("g"))
            blk.sync(mk("y"))



def _rope_tables():
    half = 64
    inv_freq = 1.0 / 10000.0 ** (np.arange(half, dtype=np.float64) * 2.0 / 128.0)
    pos = np.arange(T, dtype=np.float64)
    ang = pos[None, :] * inv_freq[:, None]
    ang = np.concatenate([ang, ang], axis=0)
    return np.cos(ang).astype(np.float32), np.sin(ang).astype(np.float32)


class KB:

    def __init__(self, blocks=(0, 1, 2, 3)):
        self.nc = bass.Bass("TRN2", target_bir_lowering=False)
        self.p = Prog(self.nc, same_engine_sync=True, dma_pool=4)
        self.blocks = blocks
        self.inputs = {}
        self._consts = {}
        self._cur = None
        self._arena0 = None
        self.pb = [self.nc.alloc_psum_tensor("pb%d" % i, [128, 512], F32)
                   for i in range(8)]
        self.ones_col_bf = self.sb("ones_col_bf", [128, 1], BF16)
        self.ones_row_f = self.sb("ones_row_f", [1, 128], F32)
        self.p.emit("g", lambda e: e.memset(self.ones_col_bf[:], 1.0),
                    writes=[self.ones_col_bf])
        self.p.emit("g", lambda e: e.memset(self.ones_row_f[:], 1.0),
                    writes=[self.ones_row_f])
        self._ident_bf = None

    def prelude_shared(self):
        self.fconst(1e-5)
        self.fconst(1e-6)
        self.fconst(1.0)
        self.ident_bf()
        self._lnp = dict(
            xf=[self.sb("ln_xf%d" % i, [128, C], F32) for i in range(2)],
            xb=[self.sb("ln_xb%d" % i, [128, C], BF16) for i in range(2)],
            sq=[self.sb("ln_sq%d" % i, [128, C], BF16) for i in range(2)],
            tmp=[self.sb("ln_tmp%d" % i, [128, C], F32) for i in range(2)],
            m=self.sb("m_row", [1, C], F32), v=self.sb("v_row", [1, C], F32),
            a=self.sb("a_row", [1, C], F32), b=self.sb("b_row", [1, C], F32),
        )
        self._cvbuf = [self.sb("cvbuf%d" % i, [128, 2048], BF16) for i in range(2)]
        self._arena0 = self.nc.SBUF_PARTITION_SIZE_BYTES - self.nc.sbuf_bytes_remaining

    def phase_begin(self, tag):
        self.p.barrier(exclude_prefix="cw")
        self.p.epoch(tag)
        self._cur = self._arena0

    def sb(self, name, shape, dtype):
        if self._cur is None:
            return self.nc.alloc_sbuf_tensor(name, shape, dtype)
        nbytes = int(np.prod(shape[1:])) * mybir.dt.size(dtype)
        off = self._cur
        self._cur = (off + nbytes + 31) & ~31
        assert self._cur <= self.nc.SBUF_PARTITION_SIZE_BYTES, \
            "arena overflow at %s (%d)" % (name, self._cur)
        return self.nc.alloc_sbuf_tensor_at(name, shape, dtype, offset=off)

    def din(self, name, shape, dtype=F32):
        if name in self.inputs:
            return self.inputs[name][2]
        ap = self.nc.dram_tensor(name, list(shape), dtype, kind="ExternalInput").ap()
        self.inputs[name] = (tuple(shape), dtype, ap)
        return ap

    def get_ws(self, iname, shape, R, M, sname):
        if not hasattr(self, "_ws"):
            self._ws = {}
        if sname not in self._ws:
            ap = self.din(iname, shape)
            self._ws[sname] = self.convert_weight(ap, R, M, sname)
        return self._ws[sname]

    def dout(self, name, shape, dtype=F32):
        return self.nc.dram_tensor(name, list(shape), dtype, kind="ExternalOutput").ap()

    def dscratch(self, name, shape, dtype):
        return self.nc.dram_tensor(name, list(shape), dtype, kind="Internal").ap()

    def const(self, name, arr):
        if name not in self._consts:
            self._consts[name] = self.nc.inline_tensor(np.ascontiguousarray(arr), name).ap()
        return self._consts[name]

    def fconst(self, val):
        key = "fc_%r" % (val,)
        t = self._consts.get(key)
        if t is None:
            t = self.sb("fc%d" % len(self._consts), [128, 1], F32)
            self.p.emit("g", lambda e: e.memset(t[:], float(val)), writes=[t])
            self._consts[key] = t
        return t

    def ident_bf(self):
        if self._ident_bf is None:
            self._ident_bf = self.sb("ident_bf", [128, 128], BF16)
            src = self.const("ident_c", np.eye(128, dtype=bfdt))
            self.p.dma("y", self._ident_bf[:], src[:])
        return self._ident_bf

    def convert_weight(self, src_ap, R, M, name, tmp_cols=2048):
        scr = self.dscratch(name, [R // 128, 128, M], BF16)
        ncols = min(tmp_cols, M)
        if not hasattr(self, "_cvbuf"):
            self._cvbuf = [self.sb("cvbuf%d" % i, [128, 2048], BF16) for i in range(2)]
        buf = self._cvbuf
        if not hasattr(self, "wready"):
            self.wready = {}
        toks = []
        n = 0
        for kt in range(R // 128):
            for ci, c0 in enumerate(range(0, M, ncols)):
                b = buf[(kt * ((M + ncols - 1) // ncols) + ci) % 2]
                nw = min(ncols, M - c0)
                self.p.dma("g", b[:, :nw], src_ap[kt * 128:(kt + 1) * 128, c0:c0 + nw],
                           writes=[b], track_out=False, sem="cwg%d" % (n % 4))
                toks.append(self.p.dma("y", scr[kt, :, c0:c0 + nw], b[:, :nw], reads=[b],
                                       writes=["%s:%d:%d" % (name, kt, ci)],
                                       track_out=False, sem="cwy%d" % (n % 4)))
                n += 1
        self.wready[name] = toks
        return scr

    def ln_chunk(self, xr, ch, ln, eps=1e-5, nd=NDT,
                 pb_sx=4, pb_sq=5, pb_a=6, pb_b=7):
        p, pb = self.p, self.pb
        c0 = ch * C
        if not hasattr(self, "_lnp"):
            self._lnp = dict(
                xf=[self.sb("ln_xf%d" % i, [128, C], F32) for i in range(2)],
                xb=[self.sb("ln_xb%d" % i, [128, C], BF16) for i in range(2)],
                sq=[self.sb("ln_sq%d" % i, [128, C], BF16) for i in range(2)],
                tmp=[self.sb("ln_tmp%d" % i, [128, C], F32) for i in range(2)],
                m=self.sb("m_row", [1, C], F32), v=self.sb("v_row", [1, C], F32),
                a=self.sb("a_row", [1, C], F32), b=self.sb("b_row", [1, C], F32),
            )
        L = self._lnp
        m_row, v_row, a_row, b_row = L["m"], L["v"], L["a"], L["b"]
        for i in range(nd):
            xf, xb, sq = L["xf"][i % 2], L["xb"][i % 2], L["sq"][i % 2]
            p.dma("y", xf[:], xr[i * 128:(i + 1) * 128, c0:c0 + C])
            p.emit("g", lambda e, xf=xf, xb=xb: e.tensor_copy(xb[:], xf[:]),
                   reads=[xf], writes=[xb])
            p.emit("s", lambda e, xf=xf, sq=sq: e.activation(sq[:], xf[:], AF.Square),
                   reads=[xf], writes=[sq])
            p.acc("t", lambda e, xb=xb, i=i: e.matmul(pb[pb_sx][:1, :C], self.ones_col_bf[:],
                                                      xb[:], start=(i == 0), stop=(i == nd - 1)),
                  reads=[self.ones_col_bf, xb], bank=pb[pb_sx],
                  first=(i == 0), last=(i == nd - 1))
            p.acc("t", lambda e, sq=sq, i=i: e.matmul(pb[pb_sq][:1, :C], self.ones_col_bf[:],
                                                      sq[:], start=(i == 0), stop=(i == nd - 1)),
                  reads=[self.ones_col_bf, sq], bank=pb[pb_sq],
                  first=(i == 0), last=(i == nd - 1))
        dinv = 1.0 / (nd * 128)
        p.emit("v", lambda e: e.tensor_scalar(m_row[:], pb[pb_sx][:1, :C], dinv, None, ALU.mult),
               reads=[pb[pb_sx]], writes=[m_row])
        p.emit("v", lambda e: e.tensor_scalar(v_row[:], pb[pb_sq][:1, :C], dinv, None, ALU.mult),
               reads=[pb[pb_sq]], writes=[v_row])
        p.emit("v", lambda e: e.tensor_tensor(a_row[:], m_row[:], m_row[:], ALU.mult),
               reads=[m_row], writes=[a_row])
        p.emit("v", lambda e: e.tensor_tensor(v_row[:], v_row[:], a_row[:], ALU.subtract),
               reads=[v_row, a_row], writes=[v_row])
        epsb = self.fconst(eps)
        p.emit("s", lambda e: e.activation(a_row[:], v_row[:], AF.Ln, bias=epsb[:1, :]),
               reads=[v_row, epsb], writes=[a_row])
        p.emit("s", lambda e: e.activation(a_row[:], a_row[:], AF.Exp, scale=-0.5),
               reads=[a_row], writes=[a_row])
        p.emit("v", lambda e: e.scalar_tensor_tensor(b_row[:], m_row[:], -1.0, a_row[:],
                                                     ALU.mult, ALU.mult),
               reads=[m_row, a_row], writes=[b_row])
        p.emit("t", lambda e: e.matmul(pb[pb_a][:, :C], self.ones_row_f[:], a_row[:],
                                       start=True, stop=True),
               reads=[self.ones_row_f, a_row], writes=[pb[pb_a]])
        p.emit("t", lambda e: e.matmul(pb[pb_b][:, :C], self.ones_row_f[:], b_row[:],
                                       start=True, stop=True),
               reads=[self.ones_row_f, b_row], writes=[pb[pb_b]])
        for i in range(nd):
            xf, tmp = L["xf"][i % 2], L["tmp"][i % 2]
            p.dma("y", xf[:], xr[i * 128:(i + 1) * 128, c0:c0 + C])
            p.emit("v", lambda e, xf=xf, tmp=tmp: e.tensor_tensor(tmp[:], xf[:], pb[pb_a][:, :C], ALU.mult),
                   reads=[xf, pb[pb_a]], writes=[tmp])
            p.emit("v", lambda e, tmp=tmp, i=i: e.tensor_tensor(ln[i][:], tmp[:], pb[pb_b][:, :C], ALU.add),
                   reads=[tmp, pb[pb_b]], writes=[ln[i]])

    def phase_l1_ffn(self, xr_in, x_out_writer):
        p, pb, nc = self.p, self.pb, self.nc
        INTER = 8192
        NJ = INTER // 128
        NO = 2 * INTER // 128
        wgu_s = self.get_ws("L1_ffn_w_gate_upT", [D, 2 * INTER], D, 2 * INTER, "wgu_s")
        wdn_s = self.get_ws("L1_ffn_w_downT", [INTER, D], INTER, D, "wdn_s")

        ln = [self.sb("f4_ln%d" % i, [128, C], BF16) for i in range(NDT)]
        xres = [self.sb("f4_xr%d" % i, [128, C], F32) for i in range(2)]
        h2 = [self.sb("f4_h%d" % j, [128, C], BF16) for j in range(NJ)]
        wkg = [self.sb("f4_wkg%d" % i, [128, 1024], BF16) for i in range(4)]
        dk = [self.sb("f4_dk%d" % i, [128, 1024], BF16) for i in range(4)]
        stmp = [self.sb("f4_st%d" % i, [128, C], BF16) for i in range(2)]
        otile = [self.sb("f4_ot%d" % i, [128, C], F32) for i in range(2)]

        for ch in range(NCH):
            self.ln_chunk(xr_in, ch, ln)
            for g in range(16):
                for kt in range(16):
                    bg = wkg[kt % 4]
                    gb = 4 * g * 128
                    srcv = wgu_s[kt].rearrange("p (blk m) -> p blk m", m=8192)[:, :, gb:gb + 512]
                    p.dma("y" if kt % 2 == 0 else "g",
                          bg[:].rearrange("p (blk m) -> p blk m", m=512), srcv,
                          writes=[bg], track_out=False, deps=self.wready["wgu_s"])
                    for oi in range(8):
                        col = (oi // 4) * 512 + (oi % 4) * 128
                        p.acc("t", lambda e, b=bg, col=col, oi=oi, kt=kt:
                              e.matmul(pb[oi][:, :C], b[:, col:col + 128], ln[kt][:],
                                       start=(kt == 0), stop=(kt == 15)),
                              reads=[bg, ln[kt]], bank=pb[oi],
                              first=(kt == 0), last=(kt == 15))
                for j in range(4):
                    st = stmp[j % 2]
                    p.emit("s", lambda e, j=j, st=st: e.activation(st[:], pb[j][:, :C], AF.Silu),
                           reads=[pb[j]], writes=[st])
                    jj = 4 * g + j
                    p.emit("v", lambda e, j=j, st=st, jj=jj:
                           e.tensor_tensor(h2[jj][:], st[:], pb[4 + j][:, :C], ALU.mult),
                           reads=[st, pb[4 + j]], writes=[h2[jj]])
            for g in range(2):
                for jt in range(NJ):
                    b = dk[jt % 4]
                    p.dma("y" if jt % 2 == 0 else "g", b[:],
                          wdn_s[jt, :, g * 1024:(g + 1) * 1024], writes=[b],
                          deps=self.wready["wdn_s"])
                    for oi in range(8):
                        p.acc("t", lambda e, b=b, oi=oi, jt=jt:
                              e.matmul(pb[oi][:, :C], b[:, oi * 128:(oi + 1) * 128], h2[jt][:],
                                       start=(jt == 0), stop=(jt == NJ - 1)),
                              reads=[b, h2[jt]], bank=pb[oi],
                              first=(jt == 0), last=(jt == NJ - 1))
                for oi in range(8):
                    i = g * 8 + oi
                    ob = otile[oi % 2]
                    xv = xres[oi % 2]
                    p.dma("y", xv[:], xr_in[i * 128:(i + 1) * 128, ch * C:(ch + 1) * C])
                    p.emit("v", lambda e, oi=oi, ob=ob, xv=xv:
                           e.tensor_tensor(ob[:], pb[oi][:, :C], xv[:], ALU.add),
                           reads=[pb[oi], xv], writes=[ob])
                    x_out_writer(i, ch, ob)

    def phase_l0_attn(self, xr_in, x_out_writer):
        def x_out_writer2(i, ch, ob, key):
            self._writer_key = key
            x_out_writer(i, ch, ob)
        p, pb, nc = self.p, self.pb, self.nc
        HEADS, KV = 16, 8
        SC = 1.0 / float(np.sqrt(2048.0))
        wqkv_s = self.get_ws("L0_attn_WqkvT", [D, 4096], D, 4096, "wqkv_s")
        wot_s = self.get_ws("L0_attn_WOT", [D, D], D, D, "wot_s")
        qs = self.dscratch("qs", [D, T], BF16)
        ks = self.dscratch("ks", [KV * 128, T], BF16)
        cxs = self.dscratch("cxs", [D, T], BF16)
        cosT, sinT = _rope_tables()
        cos_sb = self.sb("cos_sb", [128, T], BF16)
        sin_sb = self.sb("sin_sb", [128, T], BF16)
        p.dma("y", cos_sb[:], self.const("cos_c", cosT.astype(bfdt))[:])
        p.dma("y", sin_sb[:], self.const("sin_c", sinT.astype(bfdt))[:])
        ident = self.ident_bf()
        vT = [self.sb("vT%d" % kv, [128, T], BF16) for kv in range(KV)]
        Pt = self.sb("Pt", [128, 32 * C], BF16)
        self._Pt, self._vT = Pt, vT
        ln = [self.sb("a1_ln%d" % i, [128, C], BF16) for i in range(NDT)]
        slab = [self.sb("a1_wk%d" % i, [128, 1024], BF16) for i in range(3)]
        rot = self.sb("a1_rot", [128, C], BF16)
        t1 = self.sb("a1_t1", [128, C], BF16)
        vtmp = [self.sb("a1_vt%d" % i, [128, C], BF16) for i in range(2)]
        sqb = vtmp
        qkt = [self.sb("a1_qk%d" % i, [128, C], BF16) for i in range(2)]
        nqmax = self.sb("a1_nq", [1, 16], F32)
        nkmax = self.sb("a1_nk", [1, 16], F32)
        brow = self.sb("a1_brow", [1, 16], F32)
        tmp11 = self.sb("a1_t11", [1, 1], F32)
        bias_sb = self.sb("a1_bias", [128, 16], F32)
        kbuf = [self.sb("a1_kb%d" % i, [128, T], BF16) for i in range(2)]
        qbuf = [self.sb("a1_qb%d" % i, [128, T], BF16) for i in range(2)]
        zr_row = self.sb("a1_zr", [1, C], F32)
        zr_bc = self.sb("a1_zbc", [128, C], F32)
        ctile = [self.sb("a1_ct%d" % i, [128, C], BF16) for i in range(2)]
        cbuf = [Pt[:, i * C:(i + 1) * C] for i in range(16)]
        xres = [Pt[:, 8192 + i * 1024:8192 + (i + 1) * 1024].bitcast(F32)
                for i in range(2)]
        otile = [Pt[:, 10240 + i * 1024:10240 + (i + 1) * 1024].bitcast(F32)
                 for i in range(2)]
        p.emit("g", lambda e: e.memset(nqmax[:], 0.0), writes=[nqmax])
        p.emit("g", lambda e: e.memset(nkmax[:], 0.0), writes=[nkmax])

        for ch in range(NCH):
            self.ln_chunk(xr_in, ch, ln)
            c0 = ch * C
            for g in range(4):
                ots = list(range(8 * g, 8 * g + 8))
                for kt in range(16):
                    b = slab[kt % 3]
                    p.dma("y" if kt % 2 == 0 else "g", b[:],
                          wqkv_s[kt, :, g * 1024:(g + 1) * 1024],
                          deps=self.wready["wqkv_s"])
                    for oi in range(8):
                        p.acc("t", lambda e, b=b, oi=oi, kt=kt:
                              e.matmul(pb[oi][:, :C], b[:, oi * 128:(oi + 1) * 128],
                                       ln[kt][:], start=(kt == 0), stop=(kt == 15)),
                              reads=[b, ln[kt]], bank=pb[oi],
                              first=(kt == 0), last=(kt == 15))
                for oi in range(8):
                    ot = 8 * g + oi
                    if ot < 24:
                        p.emit("v", lambda e, oi=oi: e.tensor_scalar(
                            rot[0:64, :], pb[oi][64:128, :C], -1.0, None, ALU.mult),
                            reads=[pb[oi]], writes=[rot])
                        p.emit("v", lambda e, oi=oi: e.tensor_copy(
                            rot[64:128, :], pb[oi][0:64, :C]),
                            reads=[pb[oi]], writes=[rot])
                        p.emit("v", lambda e, oi=oi, c0=c0: e.tensor_tensor(
                            t1[:], pb[oi][:, :C], cos_sb[:, c0:c0 + C], ALU.mult),
                            reads=[pb[oi], cos_sb], writes=[t1])
                        p.emit("v", lambda e, c0=c0: e.tensor_tensor(
                            rot[:], rot[:], sin_sb[:, c0:c0 + C], ALU.mult),
                            reads=[rot, sin_sb], writes=[rot])
                        qk = qkt[oi % 2]
                        p.emit("v", lambda e, qk=qk: e.tensor_tensor(
                            qk[:], t1[:], rot[:], ALU.add),
                            reads=[t1, rot], writes=[qk])
                        if ot < 16:
                            dst, r0, tag = qs, ot * 128, "qs:%d:%d" % (ot, ch)
                        else:
                            dst, r0, tag = ks, (ot - 16) * 128, "ks:%d:%d" % (ot - 16, ch)
                        p.dma("y", dst[r0:r0 + 128, c0:c0 + C], qk[:], reads=[qk],
                              writes=[tag], track_out=False)
                        sq = sqb[oi % 2]
                        p.emit("v", lambda e, sq=sq, qk=qk: e.tensor_tensor(
                            sq[:], qk[:], qk[:], ALU.mult), reads=[qk], writes=[sq])
                        p.emit("t", lambda e, oi=oi, sq=sq: e.matmul(
                            pb[oi][:1, :C], self.ones_col_bf[:], sq[:], start=True, stop=True),
                            reads=[self.ones_col_bf, sq], writes=[pb[oi]])
                        p.emit("v", lambda e, oi=oi: e.reduce_max(
                            tmp11[:], pb[oi][:1, :C], AX.X), reads=[pb[oi]], writes=[tmp11])
                        if ot < 16:
                            p.emit("v", lambda e, ot=ot: e.tensor_tensor(
                                nqmax[:, ot:ot + 1], nqmax[:, ot:ot + 1], tmp11[:], ALU.max),
                                reads=[nqmax, tmp11], writes=[nqmax])
                        else:
                            kv = ot - 16
                            for hh in (2 * kv, 2 * kv + 1):
                                p.emit("v", lambda e, hh=hh: e.tensor_tensor(
                                    nkmax[:, hh:hh + 1], nkmax[:, hh:hh + 1], tmp11[:], ALU.max),
                                    reads=[nkmax, tmp11], writes=[nkmax])
                    else:
                        kv = ot - 24
                        vt = vtmp[oi % 2]
                        p.emit("v", lambda e, vt=vt, oi=oi: e.tensor_copy(vt[:], pb[oi][:, :C]),
                               reads=[pb[oi]], writes=[vt])
                        for j in range(4):
                            cb = (ch * 4 + j) * 128
                            p.dma("y", vT[kv][:, cb:cb + 128],
                                  vt[:, j * 128:(j + 1) * 128],
                                  reads=[vt], writes=["vT%d:%d" % (kv, cb)],
                                  track_out=False, transpose=True)
        p.emit("v", lambda e: e.tensor_tensor(brow[:], nqmax[:], nkmax[:], ALU.mult),
               reads=[nqmax, nkmax], writes=[brow])
        p.emit("s", lambda e: e.activation(brow[:], brow[:], AF.Ln),
               reads=[brow], writes=[brow])
        p.emit("s", lambda e: e.activation(brow[:], brow[:], AF.Exp, scale=0.5),
               reads=[brow], writes=[brow])
        p.emit("v", lambda e: e.tensor_scalar(brow[:], brow[:], -SC, None, ALU.mult),
               reads=[brow], writes=[brow])
        for h in range(16):
            bank = pb[h % 2]
            p.emit("t", lambda e, h=h, bank=bank: e.matmul(
                bank[:, :1], self.ones_row_f[:], brow[:, h:h + 1], start=True, stop=True),
                reads=[self.ones_row_f, brow], writes=[bank])
            p.emit("v", lambda e, h=h, bank=bank: e.tensor_copy(
                bias_sb[:, h:h + 1], bank[:, :1]), reads=[bank], writes=[bias_sb])
        p.barrier()

        p.dma("y", kbuf[0][:], ks[0:128, :])
        for h in range(HEADS):
            kv = h // 2
            kb_ = kbuf[h % 2]
            if h + 1 < HEADS:
                p.dma("y", kbuf[(h + 1) % 2][:],
                      ks[((h + 1) // 2) * 128:((h + 1) // 2 + 1) * 128, :])
            qh = qbuf[h % 2]
            p.dma("y", qh[:], qs[h * 128:(h + 1) * 128, :], writes=[qh],
                  track_out=False)
            for qc in range(NCH):
                qb = qh[:, qc * C:(qc + 1) * C]
                pend = []
                for kt in range(32):
                    bank = pb[kt % 2]
                    p.emit("t", lambda e, bank=bank, kt=kt, qb=qb, kb_=kb_: e.matmul(
                        bank[:, :C], kb_[:, kt * 128:(kt + 1) * 128], qb,
                        start=True, stop=True),
                        reads=[kb_, qh], writes=[bank])
                    p.emit("s", lambda e, bank=bank, kt=kt, h=h: e.activation(
                        Pt[:, kt * C:(kt + 1) * C], bank[:, :C], AF.Exp,
                        bias=bias_sb[:, h:h + 1], scale=SC),
                        reads=[bank, bias_sb], writes=["Pt:%d" % kt])
                    pend.append(kt)
                    if len(pend) > 1:
                        self._attn_zc(pend.pop(0), kv)
                while pend:
                    self._attn_zc(pend.pop(0), kv)
                p.emit("s", lambda e: e.activation(zr_row[:], pb[2][:1, :C], AF.Ln),
                       reads=[pb[2]], writes=[zr_row])
                p.emit("s", lambda e: e.activation(zr_row[:], zr_row[:], AF.Exp, scale=-1.0),
                       reads=[zr_row], writes=[zr_row])
                p.emit("t", lambda e: e.matmul(pb[4][:, :C], self.ones_row_f[:],
                                               zr_row[:], start=True, stop=True),
                       reads=[self.ones_row_f, zr_row], writes=[pb[4]])
                p.emit("v", lambda e: e.tensor_copy(zr_bc[:], pb[4][:, :C]),
                       reads=[pb[4]], writes=[zr_bc])
                ct = ctile[qc % 2]
                p.emit("v", lambda e, ct=ct: e.tensor_tensor(ct[:], pb[3][:, :C],
                                                             zr_bc[:], ALU.mult),
                       reads=[pb[3], zr_bc], writes=[ct])
                p.dma("y", cxs[h * 128:(h + 1) * 128, qc * C:(qc + 1) * C], ct[:],
                      reads=[ct], writes=["cxs:%d:%d" % (h, qc)], track_out=False)
        p.barrier()

        for ch in range(NCH):
            c0 = ch * C
            for ib in range(2):
                p.dma("y", Pt[:, ib * 8 * C:(ib + 1) * 8 * C].rearrange(
                          "p (i t) -> p i t", t=C),
                      cxs.rearrange("(i p) t -> p i t", p=128)[:, ib * 8:(ib + 1) * 8,
                                                               c0:c0 + C],
                      writes=["cbuf%d" % ib], track_out=False)
            for g in range(2):
                for kt in range(16):
                    b = slab[kt % 3]
                    p.dma("y" if kt % 2 == 0 else "g", b[:],
                          wot_s[kt, :, g * 1024:(g + 1) * 1024],
                          deps=self.wready["wot_s"])
                    for oi in range(8):
                        p.acc("t", lambda e, b=b, oi=oi, kt=kt: e.matmul(
                            pb[oi][:, :C], b[:, oi * 128:(oi + 1) * 128], cbuf[kt],
                            start=(kt == 0), stop=(kt == 15)),
                            reads=[b, "cbuf%d" % (kt // 8)], bank=pb[oi],
                            first=(kt == 0), last=(kt == 15))
                for oi in range(8):
                    i = g * 8 + oi
                    xv, ob = xres[oi % 2], otile[oi % 2]
                    p.dma("y", xv, xr_in[i * 128:(i + 1) * 128, c0:c0 + C],
                          writes=["a3xr%d" % (oi % 2)], track_out=False)
                    p.emit("v", lambda e, oi=oi, xv=xv, ob=ob: e.tensor_tensor(
                        ob, pb[oi][:, :C], xv, ALU.add),
                        reads=[pb[oi], "a3xr%d" % (oi % 2)], writes=["a3ot%d" % (oi % 2)])
                    x_out_writer2(i, ch, ob, "a3ot%d" % (oi % 2))
        return

    def _attn_zc(self, kt, kv):
        p, pb = self.p, self.pb
        Pt, vT = self._Pt, self._vT
        p.acc("t", lambda e, kt=kt: e.matmul(
            pb[2][:1, :C], self.ones_col_bf[:], Pt[:, kt * C:(kt + 1) * C],
            start=(kt == 0), stop=(kt == 31)),
            reads=[self.ones_col_bf, "Pt:%d" % kt], bank=pb[2],
            first=(kt == 0), last=(kt == 31))
        p.acc("t", lambda e, kt=kt, kv=kv: e.matmul(
            pb[3][:, :C], vT[kv][:, kt * 128:(kt + 1) * 128],
            Pt[:, kt * C:(kt + 1) * C], start=(kt == 0), stop=(kt == 31)),
            reads=[vT[kv], "Pt:%d" % kt], bank=pb[3],
            first=(kt == 0), last=(kt == 31))

    def _dequant(self, packed_ap, scalesT_ap, n_out_tiles, M_in, scr):
        p, pb = self.p, self.pb
        NW, NB, NG = M_in // 8, M_in // 2, M_in // 128
        if not hasattr(self, "_dq"):
            MX = 5632
            self._dq = dict(
                pk=self.sb("dq_pk", [128, MX // 8], I32),
                n16=self.sb("dq_n16", [128, MX // 2], U16),
                lo=self.sb("dq_lo", [128, MX // 2], U16),
                m=self.sb("dq_m", [128, MX // 2], U16),
                sb_=self.sb("dq_sb", [128, MX // 2], U16),
                bA=self.sb("dq_bA", [128, MX // 2], U16),
                bB=self.sb("dq_bB", [128, MX // 2], U16),
                mge=self.sb("dq_mge", [128, MX // 2], U16),
                bits=self.sb("dq_bits", [128, MX], U16),
                scl=self.sb("dq_scl", [128, MX // 128], mybir.dt.float16),
                sclf=self.sb("dq_sclf", [128, MX // 128], F32),
            )
        Q = self._dq
        pk, n16, lo = Q["pk"], Q["n16"], Q["lo"]
        m_, sb_, bA, bB, mge = Q["m"], Q["sb_"], Q["bA"], Q["bB"], Q["mge"]
        bits, scl, sclf = Q["bits"], Q["scl"], Q["sclf"]
        hi = n16
        ident = self.ident_bf()
        for o in range(n_out_tiles):
            r0 = o * 128
            p.dma("y", pk[:, :NW], packed_ap[r0:r0 + 128, :], writes=[pk], track_out=False)
            p.dma("y", scl[:, :NG], scalesT_ap[r0:r0 + 128, :], writes=[scl], track_out=False)
            p.emit("v", lambda e: e.tensor_copy(n16[:, :NB], pk[:, :NW].bitcast(U8)),
                   reads=[pk], writes=[n16])
            p.emit("v", lambda e: e.tensor_scalar(lo[:, :NB], n16[:, :NB], 15, None, ALU.bitwise_and),
                   reads=[n16], writes=[lo])
            p.emit("v", lambda e: e.tensor_scalar(n16[:, :NB], n16[:, :NB], 4, None, ALU.logical_shift_right),
                   reads=[n16], writes=[n16])
            for srcb, ph in ((lo, 0), (hi, 1)):
                p.emit("v", lambda e, s=srcb: e.tensor_scalar(m_[:, :NB], s[:, :NB], 7, None, ALU.bitwise_and),
                       reads=[srcb], writes=[m_])
                p.emit("v", lambda e, s=srcb: e.tensor_scalar(sb_[:, :NB], s[:, :NB], 8, 12,
                                                              ALU.bitwise_and, ALU.logical_shift_left),
                       reads=[srcb], writes=[sb_])
                p.emit("v", lambda e: e.tensor_single_scalar(bA[:, :NB], m_[:, :NB], 252, ALU.add),
                       reads=[m_], writes=[bA])
                p.emit("v", lambda e: e.tensor_single_scalar(bA[:, :NB], bA[:, :NB], 6, ALU.logical_shift_left),
                       reads=[bA], writes=[bA])
                p.emit("v", lambda e: e.tensor_scalar(bB[:, :NB], m_[:, :NB], 0x3F00, None, ALU.mult),
                       reads=[m_], writes=[bB])
                p.emit("v", lambda e: e.tensor_scalar(mge[:, :NB], m_[:, :NB], 2, None, ALU.is_ge),
                       reads=[m_], writes=[mge])
                p.emit("v", lambda e: e.copy_predicated(bB[:, :NB], mge[:, :NB], bA[:, :NB]),
                       reads=[mge, bA], writes=[bB])
                outv = bits[:, :M_in].rearrange("p (c two) -> p c two", two=2)[:, :, ph]
                p.emit("v", lambda e, o_=outv: e.tensor_tensor(o_, bB[:, :NB], sb_[:, :NB], ALU.bitwise_or),
                       reads=[bB, sb_], writes=[bits])
            p.emit("v", lambda e: e.tensor_copy(sclf[:, :NG], scl[:, :NG]),
                   reads=[scl], writes=[sclf])
            bview = bits[:, :M_in].bitcast(BF16).rearrange("p (g i) -> p g i", i=128)
            sview = sclf[:, :NG].unsqueeze(2).to_broadcast([128, NG, 128])
            p.emit("v", lambda e: e.tensor_tensor(bview, bview, sview, ALU.mult),
                   reads=[sclf], writes=[bits])
            p.dma("y", scr[o * 128:(o + 1) * 128, :], bits[:, :M_in].bitcast(BF16),
                  reads=[bits], writes=["dqs" + scr.tensor.name + str(o)],
                  track_out=False)

    def phase_l0_ffn(self, xr_in, x_out_writer):
        p, pb = self.p, self.pb
        I2, NJ, NO = 5632, 44, 88
        gup = self.din("L0_ffn_gate_up_packed", [2 * I2, 256], I32)
        gus = self.din("L0_ffn_gate_up_scalesT", [2 * I2, 16], mybir.dt.float16)
        dnp = self.din("L0_ffn_down_packed", [D, 704], I32)
        dns = self.din("L0_ffn_down_scalesT", [D, 44], mybir.dt.float16)
        wgu_scr = self.dscratch("wdq_gu", [NO * 128, D], BF16)
        wdn_scr = self.dscratch("wdq_dn", [16 * 128, I2], BF16)
        self._dequant(gup, gus, NO, D, wgu_scr)
        self._dequant(dnp, dns, 16, I2, wdn_scr)
        p.barrier()

        ln = [self.sb("f2_ln%d" % i, [128, C], BF16) for i in range(NDT)]
        h2 = [self.sb("f2_h%d" % j, [128, C], BF16) for j in range(NJ)]
        ysb = [self.sb("f2_y%d" % i, [128, C], BF16) for i in range(16)]
        wkg = [self.sb("f2_wkg%d" % i, [128, 512], BF16) for i in range(4)]
        wku = [self.sb("f2_wku%d" % i, [128, 512], BF16) for i in range(4)]
        dk = [self.sb("f2_dk%d" % i, [128, 1024], BF16) for i in range(4)]
        stmp = [self.sb("f2_st%d" % i, [128, C], BF16) for i in range(2)]
        sqy = [self.sb("f2_sqy%d" % i, [128, C], BF16) for i in range(2)]
        rr = self.sb("f2_rr", [1, C], F32)
        rbc = self.sb("f2_rbc", [128, C], F32)
        xres = [self.sb("f2_xr%d" % i, [128, C], F32) for i in range(2)]
        otile = [self.sb("f2_ot%d" % i, [128, C], F32) for i in range(2)]

        for ch in range(NCH):
            c0 = ch * C
            self.ln_chunk(xr_in, ch, ln)
            for g in range(11):
                for kt in range(16):
                    bg, bu = wkg[kt % 4], wku[kt % 4]
                    p.dma("y", bg[:],
                          wgu_scr[4 * g * 128:4 * g * 128 + 512,
                                  kt * 128:(kt + 1) * 128],
                          writes=[bg], track_out=False, transpose=True)
                    p.dma("y", bu[:],
                          wgu_scr[(44 + 4 * g) * 128:(44 + 4 * g) * 128 + 512,
                                  kt * 128:(kt + 1) * 128],
                          writes=[bu], track_out=False, transpose=True)
                    for oi in range(8):
                        b = bg if oi < 4 else bu
                        col = (oi % 4) * 128
                        p.acc("t", lambda e, b=b, col=col, oi=oi, kt=kt:
                              e.matmul(pb[oi][:, :C], b[:, col:col + 128], ln[kt][:],
                                       start=(kt == 0), stop=(kt == 15)),
                              reads=[b, ln[kt]], bank=pb[oi],
                              first=(kt == 0), last=(kt == 15))
                for j in range(4):
                    st = stmp[j % 2]
                    p.emit("s", lambda e, j=j, st=st: e.activation(st[:], pb[j][:, :C], AF.Silu),
                           reads=[pb[j]], writes=[st])
                    jj = 4 * g + j
                    p.emit("v", lambda e, j=j, st=st, jj=jj:
                           e.tensor_tensor(h2[jj][:], st[:], pb[4 + j][:, :C], ALU.mult),
                           reads=[st, pb[4 + j]], writes=[h2[jj]])
            for g in range(2):
                for jt in range(NJ):
                    b = dk[jt % 4]
                    p.dma("y", b[:],
                          wdn_scr[g * 1024:(g + 1) * 1024, jt * 128:(jt + 1) * 128],
                          writes=[b], track_out=False, transpose=True)
                    for oi in range(8):
                        p.acc("t", lambda e, b=b, oi=oi, jt=jt:
                              e.matmul(pb[oi][:, :C], b[:, oi * 128:(oi + 1) * 128], h2[jt][:],
                                       start=(jt == 0), stop=(jt == NJ - 1)),
                              reads=[b, h2[jt]], bank=pb[oi],
                              first=(jt == 0), last=(jt == NJ - 1))
                for oi in range(8):
                    i = g * 8 + oi
                    p.emit("v", lambda e, oi=oi, i=i: e.tensor_copy(ysb[i][:], pb[oi][:, :C]),
                           reads=[pb[oi]], writes=[ysb[i]])
            for i in range(16):
                sq = sqy[i % 2]
                p.emit("s", lambda e, i=i, sq=sq: e.activation(sq[:], ysb[i][:], AF.Square),
                       reads=[ysb[i]], writes=[sq])
                p.acc("t", lambda e, sq=sq, i=i: e.matmul(pb[0][:1, :C], self.ones_col_bf[:],
                                                          sq[:], start=(i == 0), stop=(i == 15)),
                      reads=[self.ones_col_bf, sq], bank=pb[0],
                      first=(i == 0), last=(i == 15))
            epsb = self.fconst(1e-6)
            p.emit("s", lambda e: e.activation(rr[:], pb[0][:1, :C], AF.Ln,
                                               bias=epsb[:1, :], scale=1.0 / 2048.0),
                   reads=[pb[0], epsb], writes=[rr])
            p.emit("s", lambda e: e.activation(rr[:], rr[:], AF.Exp, scale=-0.5),
                   reads=[rr], writes=[rr])
            p.emit("t", lambda e: e.matmul(pb[1][:, :C], self.ones_row_f[:], rr[:],
                                           start=True, stop=True),
                   reads=[self.ones_row_f, rr], writes=[pb[1]])
            p.emit("v", lambda e: e.tensor_copy(rbc[:], pb[1][:, :C]),
                   reads=[pb[1]], writes=[rbc])
            for i in range(16):
                xv, ob = xres[i % 2], otile[i % 2]
                p.dma("y", xv[:], xr_in[i * 128:(i + 1) * 128, c0:c0 + C])
                p.emit("v", lambda e, i=i, ob=ob: e.tensor_tensor(ob[:], ysb[i][:], rbc[:], ALU.mult),
                       reads=[ysb[i], rbc], writes=[ob])
                p.emit("v", lambda e, ob=ob, xv=xv: e.tensor_tensor(ob[:], ob[:], xv[:], ALU.add),
                       reads=[ob, xv], writes=[ob])
                x_out_writer(i, ch, ob)

    def phase_l1_attn(self, xr_in, x_out_writer):
        p, pb = self.p, self.pb
        NH, HD, DST, QL, NCK = 64, 64, 128, 256, 16
        DI = 4096
        conv_w = self.din("L1_attn_conv_w", [4352, 4])
        conv_b = self.din("L1_attn_conv_b", [4352])
        dt_bias = self.din("L1_attn_dt_bias", [64])
        A_log = self.din("L1_attn_A_log", [64])
        D_param = self.din("L1_attn_D_param", [64])
        norm_w = self.din("L1_attn_norm_w", [4096])
        self._dtbuf = self.sb("m_dt", [64, T], F32)
        win_s = self.get_ws("L1_attn_W_inT", [D, 8512], D, 8512, "win_s")
        wout_s = self.get_ws("L1_attn_W_outT", [DI, D], DI, D, "wout_s")
        zs = self.dscratch("zs", [DI, T], BF16)
        xbcp = self.dscratch("xbcp", [4352, T], F32)
        xbcc = self.dscratch("xbcc", [4352, T], BF16)
        acum_scr = self.dscratch("acum_scr", [64, T], F32)
        csscr = self.dscratch("csscr", [NCK, 64], F32)
        yns = self.dscratch("yns", [DI, T], BF16)
        identb = self.ident_bf()
        identf = self.sb("ident_f", [128, 128], F32)
        p.dma("y", identf[:], self.const("identf_c", np.eye(128, dtype=np.float32))[:])

        ln = [self.sb("m_ln%d" % i, [128, C], BF16) for i in range(NDT)]
        slab = [self.sb("m_wk%d" % i, [128, 1024], BF16) for i in range(4)]
        ztile = [self.sb("m_zt%d" % i, [128, C], BF16) for i in range(2)]
        xbt = [self.sb("m_xbt%d" % i, [128, C], F32) for i in range(2)]

        groups = [list(range(8 * g, 8 * g + 8)) for g in range(8)] + [[64, 65, 66]]
        for ch in range(NCH):
            c0 = ch * C
            self.ln_chunk(xr_in, ch, ln)
            for ots in groups:
                g0 = ots[0] * 128
                gw = (ots[-1] * 128 + (64 if ots[-1] == 66 else 128)) - g0
                for kt in range(16):
                    b = slab[kt % 4]
                    p.dma("y" if kt % 2 == 0 else "g", b[:, :gw],
                          win_s[kt, :, g0:g0 + gw], writes=[b], track_out=False,
                          deps=self.wready["win_s"])
                    for oi, ot in enumerate(ots):
                        mw = 64 if ot == 66 else 128
                        p.acc("t", lambda e, b=b, oi=oi, kt=kt, mw=mw:
                              e.matmul(pb[oi][:mw, :C], b[:, oi * 128:oi * 128 + mw],
                                       ln[kt][:], start=(kt == 0), stop=(kt == 15)),
                              reads=[b, ln[kt]], bank=pb[oi],
                              first=(kt == 0), last=(kt == 15))
                for oi, ot in enumerate(ots):
                    if ot < 32:
                        zt = ztile[oi % 2]
                        p.emit("v", lambda e, oi=oi, zt=zt: e.tensor_copy(zt[:], pb[oi][:, :C]),
                               reads=[pb[oi]], writes=[zt])
                        p.dma("y", zs[ot * 128:(ot + 1) * 128, c0:c0 + C], zt[:],
                              reads=[zt], writes=["zs:%d:%d" % (ot, ch)], track_out=False)
                    elif ot < 66:
                        xt = xbt[oi % 2]
                        p.emit("v", lambda e, oi=oi, xt=xt: e.tensor_copy(xt[:], pb[oi][:, :C]),
                               reads=[pb[oi]], writes=[xt])
                        r0 = (ot - 32) * 128
                        p.dma("y", xbcp[r0:r0 + 128, c0:c0 + C], xt[:],
                              reads=[xt], writes=["xbcp:%d:%d" % (ot, ch)], track_out=False)
                    else:
                        p.emit("v", lambda e, oi=oi, c0=c0: e.tensor_copy(
                            self._dtbuf[:, c0:c0 + C], pb[oi][:64, :C]),
                            reads=[pb[oi]], writes=[self._dtbuf])
        p.barrier()

        cw_sb = self.sb("m_cw", [128, 34 * 4], F32)
        cb_sb = self.sb("m_cb", [128, 34], F32)
        p.dma("y", cw_sb[:].rearrange("p (a k) -> p a k", k=4),
              conv_w.rearrange("(a p) k -> p a k", p=128), writes=[cw_sb], track_out=False)
        p.dma("y", cb_sb[:], conv_b.rearrange("(a p) -> p a", p=128),
              writes=[cb_sb], track_out=False, allow_slow_non_contiguous=True)
        SEG = 1024
        xin = [self.sb("m_xin%d" % i, [128, SEG + 3], F32) for i in range(2)]
        cacc = [self.sb("m_cacc%d" % i, [128, SEG], F32) for i in range(2)]
        cvout = [self.sb("m_cvout%d" % i, [128, SEG], BF16) for i in range(2)]
        for a in range(34):
            r0 = a * 128
            for sg in range(T // SEG):
                xi = xin[sg % 2]
                ca, cv = cacc[sg % 2], cvout[sg % 2]
                s0 = sg * SEG
                if sg == 0:
                    p.emit("g", lambda e, xi=xi: e.memset(xi[:, :3], 0.0), writes=[xi])
                    p.dma("y", xi[:, 3:], xbcp[r0:r0 + 128, 0:SEG], writes=[xi],
                          track_out=False)
                else:
                    p.dma("y", xi[:], xbcp[r0:r0 + 128, s0 - 3:s0 + SEG], writes=[xi],
                          track_out=False)
                p.emit("v", lambda e, xi=xi, a=a, ca=ca: e.tensor_scalar(
                    ca[:], xi[:, 0:SEG], cw_sb[:, 4 * a:4 * a + 1], cb_sb[:, a:a + 1],
                    ALU.mult, ALU.add), reads=[xi, cw_sb, cb_sb], writes=[ca])
                for k in range(1, 4):
                    p.emit("v", lambda e, xi=xi, a=a, k=k, ca=ca: e.scalar_tensor_tensor(
                        ca[:], xi[:, k:k + SEG], cw_sb[:, 4 * a + k:4 * a + k + 1],
                        ca[:], ALU.mult, ALU.add),
                        reads=[xi, cw_sb, ca], writes=[ca])
                p.emit("s", lambda e, ca=ca, cv=cv: e.activation(cv[:], ca[:], AF.Silu),
                       reads=[ca], writes=[cv])
                p.dma("y", xbcc[r0:r0 + 128, s0:s0 + SEG], cv[:],
                      reads=[cv], writes=["xbcc:%d:%d" % (a, sg)], track_out=False)
        dtb_col = self.sb("m_dtb", [64, 1], F32)
        acol = self.sb("m_acol", [64, 1], F32)
        dcol = self.sb("m_dcol", [64, 1], F32)
        p.dma("y", dtb_col[:], dt_bias.unsqueeze(1))
        p.dma("y", acol[:], A_log.unsqueeze(1))
        p.dma("y", dcol[:], D_param.unsqueeze(1))
        p.emit("s", lambda e: e.activation(acol[:], acol[:], AF.Exp),
               reads=[acol], writes=[acol])
        p.emit("v", lambda e: e.tensor_scalar(acol[:], acol[:], -1.0, None, ALU.mult),
               reads=[acol], writes=[acol])
        dts = self._dtbuf
        stmp = self.sb("m_stmp", [64, 2048], F32)
        one64 = self.fconst(1.0)
        for sg in range(2):
            s0 = sg * 2048
            sl = slice(s0, s0 + 2048)
            p.emit("v", lambda e, sl=sl: e.tensor_scalar(
                dts[:, sl], dts[:, sl], dtb_col[:], None, ALU.add),
                reads=[dts, dtb_col], writes=[dts])
            p.emit("s", lambda e, sl=sl: e.activation(stmp[:], dts[:, sl], AF.Abs),
                   reads=[dts], writes=[stmp])
            p.emit("s", lambda e: e.activation(stmp[:], stmp[:], AF.Exp, scale=-1.0),
                   reads=[stmp], writes=[stmp])
            p.emit("s", lambda e: e.activation(stmp[:], stmp[:], AF.Ln, bias=one64[:64, :]),
                   reads=[stmp, one64], writes=[stmp])
            p.emit("v", lambda e, sl=sl: e.tensor_scalar(
                dts[:, sl], dts[:, sl], 0.0, None, ALU.max),
                reads=[dts], writes=[dts])
            p.emit("v", lambda e, sl=sl: e.tensor_tensor(
                dts[:, sl], dts[:, sl], stmp[:], ALU.add),
                reads=[dts, stmp], writes=[dts])
        m1 = np.zeros((64, 128), np.float32)
        for k in range(64):
            m1[k, (k % 2) * 64:(k % 2) * 64 + 64] = 1.0
        pat = np.zeros((64, 32), np.float32)
        for k in range(64):
            pat[k, k // 2] = 1.0
        m1_sb = self.sb("m_m1", [64, 128], F32)
        pat_sb = self.sb("m_pat", [64, 32], F32)
        p.dma("y", m1_sb[:], self.const("m1_c", m1)[:])
        p.dma("y", pat_sb[:], self.const("pat_c", pat)[:])
        m2 = self.sb("m_m2", [64, 32], F32)
        p.emit("v", lambda e: e.tensor_tensor(
            m2[:], pat_sb[:], dcol[:].to_broadcast([64, 32]), ALU.mult),
            reads=[pat_sb, dcol], writes=[m2])
        p.emit("t", lambda e: e.matmul(pb[0][:, :32], m1_sb[:], m2[:], start=True, stop=True),
               reads=[m1_sb, m2], writes=[pb[0]])
        dbc = self.sb("m_dbc", [128, 32], F32)
        p.emit("v", lambda e: e.tensor_copy(dbc[:], pb[0][:, :32]),
               reads=[pb[0]], writes=[dbc])
        nw_sb = self.sb("m_nw", [128, 32], F32)
        p.dma("y", nw_sb[:], norm_w.rearrange("(a p) -> p a", p=128),
              writes=[nw_sb], track_out=False, allow_slow_non_contiguous=True)
        p.barrier()

        mask = np.zeros((128, 512), np.float32)
        for st in range(2):
            for pp in range(128):
                s_idx = st * 128 + pp
                mask[pp, st * 256 + s_idx:st * 256 + 256] = 1.0
        mask_sb = self.sb("m_mask", [128, 512], BF16)
        p.dma("y", mask_sb[:], self.const("mask_c", mask.astype(bfdt))[:])
        acum = self.sb("m_acum", [64, QL], F32)
        dAc = self.sb("m_dAc", [64, QL], F32)
        onesQ = self.sb("m_onesQ", [64, QL], F32)
        p.emit("g", lambda e: e.memset(onesQ[:], 1.0), writes=[onesQ])
        dec = self.sb("m_dec", [64, QL], F32)
        decf = self.sb("m_decf", [64, QL], F32)
        dtcb = self.sb("m_dtcb", [64, QL], BF16)
        acT = self.sb("m_acT", [128, 128], F32)
        acTn = self.sb("m_acTn", [128, 128], F32)
        dcT = self.sb("m_dcT", [128, 128], F32)
        dtT = self.sb("m_dtT", [128, 128], BF16)
        Bsb = self.sb("m_B", [128, QL], BF16)
        Csb = self.sb("m_C", [128, QL], BF16)
        BT = self.sb("m_BT", [128, 2 * 128], BF16)
        msc = self.sb("m_msc", [128, 2 * QL], BF16)
        Xb = self.sb("m_Xb", [128, 32 * QL], BF16)
        Xsb = [Xb[:, j * QL:(j + 1) * QL] for j in range(32)]
        XT = self.sb("m_XT", [128, 8192], BF16)
        yc = [self.sb("m_y%d" % j, [128, QL], BF16) for j in range(32)]
        P = self.sb("m_P", [128, 4096], F32)
        Pbf = self.sb("m_Pbf", [128, 4096], BF16)
        eA = [self.sb("m_eA%d" % i, [128, QL], BF16) for i in range(2)]
        Csc = [self.sb("m_Csc%d" % i, [128, QL], BF16) for i in range(2)]
        mtmp = [self.sb("m_mtmp%d" % i, [128, 2 * QL], F32) for i in range(2)]
        me = [self.sb("m_me%d" % i, [128, 2 * QL], BF16) for i in range(2)]
        Mt = [[self.sb("m_Mt%d_%d" % (i, st), [128, QL], BF16) for st in range(2)]
              for i in range(2)]
        sBT = [[self.sb("m_sBT%d_%d" % (i, st), [128, 128], BF16) for st in range(2)]
               for i in range(2)]
        arow = [self.sb("m_ar%d" % i, [1, QL], F32) for i in range(2)]
        csrow = self.sb("m_csr", [1, 64], F32)
        excsT = self.sb("m_excsT", [128, 64], F32)
        zb = Xb
        sqg = [self.sb("m_sqg%d" % i, [128, QL], BF16) for i in range(2)]
        gtmp = [self.sb("m_gt%d" % i, [128, QL], BF16) for i in range(2)]
        rr = self.sb("m_rr", [1, QL], F32)
        rbc = self.sb("m_rbc", [128, QL], F32)
        p.emit("g", lambda e: e.memset(P[:], 0.0), writes=[P])

        eps5 = self.fconst(1e-5)
        for c in range(NCK):
            c0 = c * QL
            p.emit("v", lambda e, c0=c0: e.tensor_scalar(
                dAc[:], dts[:, c0:c0 + QL], acol[:], None, ALU.mult),
                reads=[dts, acol], writes=[dAc])
            p.emit("v", lambda e: e.tensor_tensor_scan(
                acum[:], onesQ[:], dAc[:], 0.0, ALU.mult, ALU.add),
                reads=[onesQ, dAc], writes=[acum])
            p.dma("y", acum_scr[:, c0:c0 + QL], acum[:], reads=[acum],
                  writes=["acs:%d" % c], track_out=False)
            p.dma("y", csscr[c].unsqueeze(1), acum[:, QL - 1:QL], reads=[acum],
                  writes=["css:%d" % c], track_out=False)
            p.emit("v", lambda e: e.tensor_scalar(
                dec[:], acum[:], acum[:, QL - 1:QL], None, ALU.subtract),
                reads=[acum], writes=[dec])
            p.emit("s", lambda e: e.activation(decf[:], dec[:], AF.Exp, scale=-1.0),
                   reads=[dec], writes=[decf])
            p.emit("v", lambda e, c0=c0: e.tensor_copy(dtcb[:], dts[:, c0:c0 + QL]),
                   reads=[dts], writes=[dtcb])
            for sub in range(2):
                p.emit("t", lambda e, sub=sub: e.transpose(
                    pb[7][:, :64], acum[:, sub * 128:(sub + 1) * 128], identf[:64, :64]),
                    reads=[acum, identf], writes=[pb[7]])
                p.emit("v", lambda e, sub=sub: e.tensor_copy(
                    acT[:, sub * 64:(sub + 1) * 64], pb[7][:, :64]),
                    reads=[pb[7]], writes=[acT])
                p.emit("t", lambda e, sub=sub: e.transpose(
                    pb[6][:, :64], decf[:, sub * 128:(sub + 1) * 128],
                    identf[:64, :64]), reads=[decf, identf], writes=[pb[6]])
                p.emit("v", lambda e, sub=sub: e.tensor_copy(
                    dcT[:, sub * 64:(sub + 1) * 64], pb[6][:, :64]),
                    reads=[pb[6]], writes=[dcT])
                p.emit("t", lambda e, sub=sub: e.transpose(
                    pb[6][:, :].bitcast(BF16)[:, :64], dtcb[:, sub * 128:(sub + 1) * 128],
                    identb[:64, :64]), reads=[dtcb, identb], writes=[pb[6]])
                p.emit("v", lambda e, sub=sub: e.tensor_copy(
                    dtT[:, sub * 64:(sub + 1) * 64], pb[6][:, :].bitcast(BF16)[:, :64]),
                    reads=[pb[6]], writes=[dtT])
            p.emit("v", lambda e: e.tensor_scalar(acTn[:], acT[:], -1.0, None, ALU.mult),
                   reads=[acT], writes=[acTn])
            p.dma("y", Bsb[:], xbcc[4096:4224, c0:c0 + QL], writes=[Bsb], track_out=False)
            p.dma("y", Csb[:], xbcc[4224:4352, c0:c0 + QL], writes=[Csb], track_out=False)
            for st in range(2):
                p.emit("t", lambda e, st=st: e.transpose(
                    pb[5][:, :].bitcast(BF16)[:, :128], Bsb[:, st * 128:(st + 1) * 128],
                    identb[:]), reads=[Bsb, identb], writes=[pb[5]])
                p.emit("v", lambda e, st=st: e.tensor_copy(
                    BT[:, st * 128:(st + 1) * 128], pb[5][:, :].bitcast(BF16)[:, :128]),
                    reads=[pb[5]], writes=[BT])
                p.emit("t", lambda e, st=st: e.matmul(
                    pb[st][:, :QL], Bsb[:, st * 128:(st + 1) * 128], Csb[:],
                    start=True, stop=True), reads=[Bsb, Csb], writes=[pb[st]])
                p.emit("v", lambda e, st=st: e.tensor_tensor(
                    msc[:, st * QL:(st + 1) * QL], pb[st][:, :QL],
                    mask_sb[:, st * QL:(st + 1) * QL], ALU.mult),
                    reads=[pb[st], mask_sb], writes=[msc])
            for jb in range(4):
                p.dma("g", Xb[:, jb * 8 * QL:(jb + 1) * 8 * QL].rearrange(
                          "p (j t) -> p j t", t=QL),
                      xbcc.rearrange("(j p) t -> p j t", p=128)[:, jb * 8:(jb + 1) * 8,
                                                               c0:c0 + QL],
                      writes=["Xb"], track_out=False)
            for sub in range(2):
                p.dma("y", XT[:, sub * 4096:(sub + 1) * 4096],
                      xbcc[0:4096, c0 + sub * 128:c0 + (sub + 1) * 128],
                      writes=["XTs%d" % sub, "ynb%d" % (2 * sub), "ynb%d" % (2 * sub + 1)],
                      track_out=False, transpose=True)
            for sub in range(2):
                xv = XT[:, sub * 4096:(sub + 1) * 4096].rearrange("p (h i) -> p h i", i=64)
                dv = dtT[:, sub * 64:(sub + 1) * 64].unsqueeze(2).to_broadcast([128, 64, 64])
                p.emit("v", lambda e, xv=xv, dv=dv: e.tensor_tensor(xv, xv, dv, ALU.mult),
                       reads=[dtT], writes=["XTs%d" % sub])
            p.emit("v", lambda e: e.tensor_copy(Pbf[:], P[:]), reads=[P], writes=[Pbf])
            p.dma("g", csrow[:], csscr[c].unsqueeze(0), writes=[csrow],
                  track_out=False, reads=["css:%d" % c])
            p.emit("s", lambda e: e.activation(csrow[:], csrow[:], AF.Exp),
                   reads=[csrow], writes=[csrow])
            p.emit("t", lambda e: e.matmul(pb[2][:, :64], self.ones_row_f[:], csrow[:],
                                           start=True, stop=True),
                   reads=[self.ones_row_f, csrow], writes=[pb[2]])
            p.emit("v", lambda e: e.tensor_copy(excsT[:], pb[2][:, :64]),
                   reads=[pb[2]], writes=[excsT])
            pv = P[:].rearrange("p (h i) -> p h i", i=64)
            ev = excsT[:].unsqueeze(2).to_broadcast([128, 64, 64])
            p.emit("v", lambda e, pv=pv, ev=ev: e.tensor_tensor(pv, pv, ev, ALU.mult),
                   reads=[excsT, P], writes=[P])
            def prep(h):
                par = h % 2
                bA = pb[0 + par]
                ar = arow[par]
                eAp, Cscp, mtp, mep = eA[par], Csc[par], mtmp[par], me[par]
                Mtp, sBTp = Mt[par], sBT[par]
                p.dma("g", ar[:], acum_scr[h:h + 1, c0:c0 + QL], writes=[ar],
                      track_out=False, reads=["acs:%d" % c])
                p.emit("t", lambda e, ar=ar, bA=bA: e.matmul(
                    bA[:, :QL], self.ones_row_f[:], ar[:], start=True, stop=True),
                    reads=[self.ones_row_f, ar], writes=[bA])
                p.emit("s", lambda e, bA=bA, eAp=eAp: e.activation(eAp[:], bA[:, :QL], AF.Exp),
                       reads=[bA], writes=[eAp])
                p.emit("v", lambda e, eAp=eAp, Cscp=Cscp: e.tensor_tensor(
                    Cscp[:], Csb[:], eAp[:], ALU.mult),
                    reads=[Csb, eAp], writes=[Cscp])
                for st in range(2):
                    p.emit("v", lambda e, st=st, h=h, bA=bA, mtp=mtp: e.tensor_scalar(
                        mtp[:, st * QL:(st + 1) * QL], bA[:, :QL],
                        acTn[:, st * 64 + h:st * 64 + h + 1],
                        0.0, ALU.add, ALU.min),
                        reads=[bA, acTn], writes=[mtp])
                p.emit("s", lambda e, mtp=mtp, mep=mep: e.activation(mep[:], mtp[:], AF.Exp),
                       reads=[mtp], writes=[mep])
                for st in range(2):
                    p.emit("v", lambda e, st=st, mep=mep, Mtp=Mtp: e.tensor_tensor(
                        Mtp[st][:], mep[:, st * QL:(st + 1) * QL],
                        msc[:, st * QL:(st + 1) * QL], ALU.mult),
                        reads=[mep, msc], writes=[Mtp[st]])
                for st in range(2):
                    p.emit("v", lambda e, st=st, h=h, sBTp=sBTp: e.tensor_scalar(
                        sBTp[st][:], BT[:, st * 128:(st + 1) * 128],
                        dcT[:, st * 64 + h:st * 64 + h + 1], None, ALU.mult),
                        reads=[BT, dcT], writes=[sBTp[st]])

            def consume(h):
                par = h % 2
                bY, bS = pb[2 + par], pb[4 + par]
                Cscp, Mtp, sBTp = Csc[par], Mt[par], sBT[par]
                hb = (h % 2) * 64
                p.emit("t", lambda e, h=h, hb=hb, bY=bY, Cscp=Cscp: e.matmul(
                    bY[hb:hb + 64, :QL], Pbf[:, h * 64:(h + 1) * 64], Cscp[:],
                    start=True, stop=False), reads=[Pbf, Cscp], writes=[bY])
                for st in range(2):
                    p.acc("t", lambda e, st=st, h=h, hb=hb, bY=bY, Mtp=Mtp: e.matmul(
                        bY[hb:hb + 64, :QL], XT[:, st * 4096 + h * 64:st * 4096 + h * 64 + 64],
                        Mtp[st][:], start=False, stop=(st == 1)),
                        reads=["XTs%d" % st, Mtp[st]], bank=bY,
                        first=False, last=(st == 1))
                for st in range(2):
                    p.acc("t", lambda e, st=st, h=h, bS=bS, sBTp=sBTp: e.matmul(
                        bS[:, :64], sBTp[st][:], XT[:, st * 4096 + h * 64:st * 4096 + h * 64 + 64],
                        start=(st == 0), stop=(st == 1)),
                        reads=[sBTp[st], "XTs%d" % st], bank=bS,
                        first=(st == 0), last=(st == 1))
                xj, half = h // 2, (h % 2) * 64
                p.emit("v", lambda e, h=h, xj=xj, half=half, bY=bY: e.scalar_tensor_tensor(
                    yc[xj][half:half + 64, :], Xsb[xj][half:half + 64, :],
                    dbc[half:half + 64, xj:xj + 1], bY[half:half + 64, :QL],
                    ALU.mult, ALU.add),
                    reads=["Xb", dbc, bY], writes=["yck%d" % xj])
                p.emit("v", lambda e, h=h, bS=bS: e.tensor_tensor(
                    P[:, h * 64:(h + 1) * 64], P[:, h * 64:(h + 1) * 64],
                    bS[:, :64], ALU.add), reads=[P, bS], writes=[P])

            prep(0)
            for h in range(NH):
                if h + 1 < NH:
                    prep(h + 1)
                consume(h)
            for jb in range(4):
                p.dma("y", zb[:, jb * 8 * QL:(jb + 1) * 8 * QL].rearrange(
                          "p (j t) -> p j t", t=QL),
                      zs.rearrange("(j p) t -> p j t", p=128)[:, jb * 8:(jb + 1) * 8,
                                                             c0:c0 + QL],
                      writes=["Xb"], track_out=False)
            for j in range(32):
                gt = gtmp[j % 2]
                p.emit("s", lambda e, j=j, gt=gt: e.activation(
                    gt[:], zb[:, j * QL:(j + 1) * QL], AF.Silu),
                    reads=["Xb"], writes=[gt])
                p.emit("v", lambda e, j=j, gt=gt: e.tensor_tensor(
                    yc[j][:], yc[j][:], gt[:], ALU.mult),
                    reads=[gt], writes=["yck%d" % j])
                sq = sqg[j % 2]
                p.emit("s", lambda e, j=j, sq=sq: e.activation(sq[:], yc[j][:], AF.Square),
                       reads=["yck%d" % j], writes=[sq])
                p.acc("t", lambda e, sq=sq, j=j: e.matmul(
                    pb[7][:1, :QL], self.ones_col_bf[:], sq[:],
                    start=(j == 0), stop=(j == 31)),
                    reads=[self.ones_col_bf, sq], bank=pb[7],
                    first=(j == 0), last=(j == 31))
            p.emit("s", lambda e: e.activation(rr[:], pb[7][:1, :QL], AF.Ln,
                                               bias=eps5[:1, :], scale=1.0 / 4096.0),
                   reads=[pb[7], eps5], writes=[rr])
            p.emit("s", lambda e: e.activation(rr[:], rr[:], AF.Exp, scale=-0.5),
                   reads=[rr], writes=[rr])
            p.emit("t", lambda e: e.matmul(pb[7][:, :QL], self.ones_row_f[:], rr[:],
                                           start=True, stop=True),
                   reads=[self.ones_row_f, rr], writes=[pb[7]])
            p.emit("v", lambda e: e.tensor_copy(rbc[:], pb[7][:, :QL]),
                   reads=[pb[7]], writes=[rbc])
            for j in range(32):
                p.emit("v", lambda e, j=j: e.scalar_tensor_tensor(
                    yc[j][:], yc[j][:], nw_sb[:, j:j + 1], rbc[:], ALU.mult, ALU.mult),
                    reads=[nw_sb, rbc], writes=["yck%d" % j])
                p.dma("y", yns[j * 128:(j + 1) * 128, c0:c0 + QL], yc[j][:],
                      reads=["yck%d" % j], writes=["yns:%d:%d" % (j, c)],
                      track_out=False)
        p.barrier()

        ynb = [XT[:, j * QL:(j + 1) * QL] for j in range(32)]
        xres = xbt
        otile = [xin[0], xin[1]]
        for cc in range(T // QL):
            c0 = cc * QL
            for jb in range(4):
                p.dma("y", XT[:, jb * 8 * QL:(jb + 1) * 8 * QL].rearrange(
                          "p (j t) -> p j t", t=QL),
                      yns.rearrange("(j p) t -> p j t", p=128)[:, jb * 8:(jb + 1) * 8,
                                                               c0:c0 + QL],
                      writes=["ynb%d" % jb], track_out=False)
            for g in range(2):
                for kt in range(32):
                    b = slab[kt % 4]
                    p.dma("y" if kt % 2 == 0 else "g", b[:],
                          wout_s[kt, :, g * 1024:(g + 1) * 1024],
                          writes=[b], track_out=False, deps=self.wready["wout_s"])
                    for oi in range(8):
                        p.acc("t", lambda e, b=b, oi=oi, kt=kt: e.matmul(
                            pb[oi][:, :QL], b[:, oi * 128:(oi + 1) * 128], ynb[kt],
                            start=(kt == 0), stop=(kt == 31)),
                            reads=[b, "ynb%d" % (kt // 8)], bank=pb[oi],
                            first=(kt == 0), last=(kt == 31))
                for oi in range(8):
                    i = g * 8 + oi
                    xv, ob = xres[oi % 2][:, :QL], otile[oi % 2][:, :QL]
                    xk, ok = xres[oi % 2], otile[oi % 2]
                    p.dma("y", xv, xr_in[i * 128:(i + 1) * 128, c0:c0 + QL],
                          writes=[xk], track_out=False)
                    p.emit("v", lambda e, oi=oi, xv=xv, ob=ob: e.tensor_tensor(
                        ob, pb[oi][:, :QL], xv, ALU.add),
                        reads=[pb[oi], xk], writes=[ok])
                    x_out_writer(i, cc, ob)

    def finish(self):
        self.p.barrier()
        self.p.build()



def _qkv_perm():
    i = np.arange(2048)
    permq = (i % 128) * 16 + i // 128
    j = np.arange(1024)
    permkv = (j % 128) * 8 + j // 128
    return permq, permkv


def prep_l0attn_inputs(W_Q, W_K, W_V, W_O):
    permq, permkv = _qkv_perm()
    W_Q, W_K = np.asarray(W_Q), np.asarray(W_K)
    W_V, W_O = np.asarray(W_V), np.asarray(W_O)
    wq = W_Q[permq].T
    wk = W_K[permkv].T
    wv = W_V[permkv].T
    return {
        "L0_attn_WqkvT": np.ascontiguousarray(np.concatenate([wq, wk, wv], axis=1)),
        "L0_attn_WOT": np.ascontiguousarray(W_O.T),
    }


def build_l0attn_program():
    kb = KB(blocks=(0,))
    xr_in = kb.din("x_in", [D, T])
    x_out = kb.dout("x_out", [D, T])

    def writer(i, ch, ob):
        key = getattr(kb, "_writer_key", None)
        kb.p.dma("y", x_out[i * 128:(i + 1) * 128, ch * C:(ch + 1) * C], ob[:],
                 reads=[ob if key is None else key],
                 writes=["x_out:%d:%d" % (i, ch)], track_out=False)
        kb._writer_key = None
    kb.prelude_shared()
    kb.phase_begin('a')
    kb.phase_l0_attn(xr_in, writer)
    kb.finish()
    return kb


def prep_l0ffn_inputs(gup, gus, dnp, dns):
    return {
        "L0_ffn_gate_up_packed": np.ascontiguousarray(np.asarray(gup)),
        "L0_ffn_gate_up_scalesT": np.ascontiguousarray(np.asarray(gus).T),
        "L0_ffn_down_packed": np.ascontiguousarray(np.asarray(dnp)),
        "L0_ffn_down_scalesT": np.ascontiguousarray(np.asarray(dns).T),
    }


def build_l0ffn_program():
    kb = KB(blocks=(1,))
    xr_in = kb.din("x_in", [D, T])
    x_out = kb.dout("x_out", [D, T])

    def writer(i, ch, ob):
        kb.p.dma("y", x_out[i * 128:(i + 1) * 128, ch * C:(ch + 1) * C], ob[:],
                 reads=[ob], writes=["x_out:%d:%d" % (i, ch)], track_out=False)
    kb.prelude_shared()
    kb.phase_begin('b')
    kb.phase_l0_ffn(xr_in, writer)
    kb.finish()
    return kb


def prep_l1attn_inputs(W_in, conv_w, conv_b, dt_bias, A_log, D_param, norm_w, W_out):
    return {
        "L1_attn_W_inT": np.ascontiguousarray(np.asarray(W_in).T),
        "L1_attn_W_outT": np.ascontiguousarray(np.asarray(W_out).T),
        "L1_attn_conv_w": np.ascontiguousarray(np.asarray(conv_w)),
        "L1_attn_conv_b": np.ascontiguousarray(np.asarray(conv_b)),
        "L1_attn_dt_bias": np.ascontiguousarray(np.asarray(dt_bias)),
        "L1_attn_A_log": np.ascontiguousarray(np.asarray(A_log)),
        "L1_attn_D_param": np.ascontiguousarray(np.asarray(D_param)),
        "L1_attn_norm_w": np.ascontiguousarray(np.asarray(norm_w)),
    }


def build_l1attn_program():
    kb = KB(blocks=(2,))
    xr_in = kb.din("x_in", [D, T])
    x_out = kb.dout("x_out", [D, T])

    def writer(i, cc, ob):
        kb.p.dma("y", x_out[i * 128:(i + 1) * 128, cc * 256:(cc + 1) * 256], ob[:],
                 reads=[ob], writes=["x_out:%d:%d" % (i, cc)], track_out=False)
    kb.prelude_shared()
    kb.phase_begin('c')
    kb.phase_l1_attn(xr_in, writer)
    kb.finish()
    return kb


def build_l1ffn_program():
    kb = KB(blocks=(3,))
    xr_in = kb.din("x_in", [D, T])
    x_out = kb.dout("x_out", [D, T])

    def writer(i, ch, ob):
        kb.p.dma("y", x_out[i * 128:(i + 1) * 128, ch * C:(ch + 1) * C], ob[:],
                 reads=[ob], writes=["x_out:%d:%d" % (i, ch)], track_out=False)
    kb.prelude_shared()
    kb.phase_begin('d')
    kb.phase_l1_ffn(xr_in, writer)
    kb.finish()
    return kb



def prep_l1ffn_inputs(L1_ffn_w_gate_up, L1_ffn_w_down):
    return {
        "L1_ffn_w_gate_upT": np.ascontiguousarray(np.asarray(L1_ffn_w_gate_up).T),
        "L1_ffn_w_downT": np.ascontiguousarray(np.asarray(L1_ffn_w_down).T),
    }


def build_fused_program():
    kb = KB()
    x_in = kb.din("x_in", [D, T])
    x_out = kb.dout("x_out", [D, T])
    xr1 = kb.dscratch("xr1", [D, T], F32)
    xr2 = kb.dscratch("xr2", [D, T], F32)
    xr3 = kb.dscratch("xr3", [D, T], F32)
    kb.prelude_shared()
    kb.get_ws("L0_attn_WqkvT", [D, 4096], D, 4096, "wqkv_s")
    kb.get_ws("L0_attn_WOT", [D, D], D, D, "wot_s")
    kb.get_ws("L1_attn_W_inT", [D, 8512], D, 8512, "win_s")
    kb.get_ws("L1_attn_W_outT", [4096, D], 4096, D, "wout_s")
    kb.get_ws("L1_ffn_w_gate_upT", [D, 16384], D, 16384, "wgu_s")
    kb.get_ws("L1_ffn_w_downT", [8192, D], 8192, D, "wdn_s")

    def mkw(dst, csz):
        def w(i, ch, ob):
            key = getattr(kb, "_writer_key", None)
            kb.p.dma("y", dst[i * 128:(i + 1) * 128, ch * csz:(ch + 1) * csz],
                     ob[:], reads=[ob if key is None else key],
                     writes=["%s:%d:%d" % (dst.tensor.name, i, ch)],
                     track_out=False)
            kb._writer_key = None
        return w
    kb.phase_begin("a")
    kb.phase_l0_attn(x_in, mkw(xr1, C))
    kb.phase_begin("b")
    kb.phase_l0_ffn(xr1, mkw(xr2, C))
    kb.phase_begin("c")
    kb.phase_l1_attn(xr2, mkw(xr3, 256))
    kb.phase_begin("d")
    kb.phase_l1_ffn(xr3, mkw(x_out, C))
    kb.finish()
    return kb


_PROGRAMS = {}


def _get_fused():
    if "f" not in _PROGRAMS:
        _PROGRAMS["f"] = build_fused_program()
    return _PROGRAMS["f"]


def kernel(**inputs):
    x = np.asarray(inputs["L0_attn_x"])
    ncores = x.shape[0]
    wm = {}
    wm.update(prep_l0attn_inputs(inputs["L0_attn_W_Q"], inputs["L0_attn_W_K"],
                                 inputs["L0_attn_W_V"], inputs["L0_attn_W_O"]))
    wm.update(prep_l0ffn_inputs(inputs["L0_ffn_gate_up_packed"],
                                inputs["L0_ffn_gate_up_scales"],
                                inputs["L0_ffn_down_packed"],
                                inputs["L0_ffn_down_scales"]))
    wm.update(prep_l1attn_inputs(inputs["L1_attn_W_in"], inputs["L1_attn_conv_w"],
                                 inputs["L1_attn_conv_b"], inputs["L1_attn_dt_bias"],
                                 inputs["L1_attn_A_log"], inputs["L1_attn_D_param"],
                                 inputs["L1_attn_norm_w"], inputs["L1_attn_W_out"]))
    wm.update(prep_l1ffn_inputs(inputs["L1_ffn_w_gate_up"], inputs["L1_ffn_w_down"]))
    kb = _get_fused()
    in_maps = [dict(wm, x_in=np.ascontiguousarray(x[c].T)) for c in range(ncores)]
    res = run_bass_kernel_spmd(kb.nc, in_maps, core_ids=list(range(ncores)))
    out = np.stack([np.asarray(res.results[c]["x_out"]).T for c in range(ncores)], axis=0)
    return np.ascontiguousarray(out.astype(np.float32))


# version 10
# speedup vs baseline: 1.1538x; 1.0118x over previous
import numpy as np
import ml_dtypes
import concourse.bass as bass
import concourse.mybir as mybir
from concourse.bass_utils import run_bass_kernel_spmd

F32 = mybir.dt.float32
BF16 = mybir.dt.bfloat16
I32 = mybir.dt.int32
U8 = mybir.dt.uint8
U16 = mybir.dt.uint16
AF = mybir.ActivationFunctionType
ALU = mybir.AluOpType
AX = mybir.AxisListType
bfdt = ml_dtypes.bfloat16

T = 4096
D = 2048
C = 512
NCH = T // C
NDT = D // 128

QUEUES = ("t", "v", "s", "g", "y")


def _key(r):
    if isinstance(r, str):
        return r
    t = getattr(r, "tensor", None)
    if t is not None:
        return t.name
    return r.name


class Prog:

    def __init__(self, nc, same_engine_sync=True, dma_pool=8):
        self.nc = nc
        self.same_engine_sync = same_engine_sync
        self.instr = {q: [] for q in QUEUES}
        self.sems = {}
        self.cnt = {}
        self.hw = {q: {} for q in QUEUES}
        self.last_w = {}
        self.readers = {}
        self.dma_pool = dma_pool
        self._dma_rr = {q: 0 for q in QUEUES}
        self.eng_sem = {q: q for q in QUEUES}
        self._dma_tag = ""
        self.sem_queue = {}
        for q in QUEUES:
            self._get_sem(q, queue=q)

    def _get_sem(self, name, queue=None):
        if name not in self.sems:
            self.sems[name] = self.nc.alloc_semaphore(name="sem_" + name)
            self.cnt[name] = 0
            if queue is not None:
                self.sem_queue[name] = queue
        return self.sems[name]

    def _deps_for(self, reads, writes, extra):
        deps = list(extra)
        for r in reads:
            t = self.last_w.get(_key(r))
            if t is not None:
                deps.append(t)
        for w in writes:
            k = _key(w)
            t = self.last_w.get(k)
            if t is not None:
                deps.append(t)
            deps.extend(self.readers.get(k, ()))
        return deps

    def _register(self, token, reads, writes):
        if token is None:
            return
        for r in reads:
            self.readers.setdefault(_key(r), []).append(token)
        for w in writes:
            k = _key(w)
            self.last_w[k] = token
            self.readers[k] = []

    def _filter(self, q, alldeps):
        waits = []
        for d in alldeps:
            if d is None:
                continue
            s, v = d
            if q == "t" and self.sem_queue.get(s) == "t":
                continue
            if not self.same_engine_sync and s == q:
                continue
            if self.hw[q].get(s, 0) < v:
                self.hw[q][s] = v
                waits.append((s, v))
        return waits

    def epoch(self, tag):
        for q in QUEUES:
            self.eng_sem[q] = q + tag
            self._get_sem(q + tag, queue=q)
        self._dma_tag = tag

    def emit(self, q, fn, reads=(), writes=(), deps=(), sem=None, inc=1):
        waits = self._filter(q, self._deps_for(reads, writes, deps))
        token = None
        if inc:
            sname = sem if sem is not None else self.eng_sem[q]
            self._get_sem(sname)
            self.cnt[sname] += inc
            token = (sname, self.cnt[sname])
            if (not self.same_engine_sync) and sname == q \
                    and self.hw[q].get(sname, 0) < self.cnt[sname]:
                self.hw[q][sname] = self.cnt[sname]
            self.instr[q].append((waits, fn, sname, inc))
        else:
            self.instr[q].append((waits, fn, None, 0))
        self._register(token, reads, writes)
        return token

    def acc(self, q, fn, reads=(), bank=None, first=False, last=False, deps=()):
        if first:
            return self.emit(q, fn, reads=reads, writes=[bank], deps=deps)
        token = self.emit(q, fn, reads=reads, deps=deps)
        if last:
            k = _key(bank)
            self.last_w[k] = token
            self.readers[k] = []
        return token

    def dma(self, q, out, in_, reads=(), writes=(), deps=(), sem=None,
            track_out=True, track_in=True, **kw):
        if sem is None:
            sem = "d%s%d%s" % (q, self._dma_rr[q] % self.dma_pool, self._dma_tag)
            self._dma_rr[q] += 1
        self._get_sem(sem)
        deps = list(deps)
        if self.cnt[sem] > 0:
            deps.append((sem, self.cnt[sem]))
        reads = list(reads) + ([in_] if track_in else [])
        writes = list(writes) + ([out] if track_out else [])
        fn = lambda eng: eng.dma_start(out=out, in_=in_, **kw)
        return self.emit(q, fn, reads=reads, writes=writes, deps=deps, sem=sem, inc=16)

    def wait(self, q, deps):
        self.emit(q, None, deps=deps, inc=0)

    def tokens_now(self):
        return [(s, c) for s, c in self.cnt.items() if c > 0]

    def barrier(self, qs=QUEUES, exclude_prefix=None):
        toks = [(s, c) for s, c in self.cnt.items()
                if c > 0 and not (exclude_prefix and s.startswith(exclude_prefix))]
        for q in qs:
            self.wait(q, toks)

    def build(self):
        nc = self.nc
        with nc.Block() as blk:
            def mk(q):
                def body(eng):
                    for waits, fn, sname, inc in self.instr[q]:
                        for s, v in waits:
                            eng.wait_ge(self.sems[s], v)
                        if fn is not None:
                            inst = fn(eng)
                            if sname is not None:
                                inst.then_inc(self.sems[sname], inc)
                return body
            blk.tensor(mk("t"))
            blk.vector(mk("v"))
            blk.scalar(mk("s"))
            blk.gpsimd(mk("g"))
            blk.sync(mk("y"))



def _rope_tables():
    half = 64
    inv_freq = 1.0 / 10000.0 ** (np.arange(half, dtype=np.float64) * 2.0 / 128.0)
    pos = np.arange(T, dtype=np.float64)
    ang = pos[None, :] * inv_freq[:, None]
    ang = np.concatenate([ang, ang], axis=0)
    return np.cos(ang).astype(np.float32), np.sin(ang).astype(np.float32)


class KB:

    def __init__(self, blocks=(0, 1, 2, 3)):
        self.nc = bass.Bass("TRN2", target_bir_lowering=False)
        self.p = Prog(self.nc, same_engine_sync=True, dma_pool=4)
        self.blocks = blocks
        self.inputs = {}
        self._consts = {}
        self._cur = None
        self._arena0 = None
        self.pb = [self.nc.alloc_psum_tensor("pb%d" % i, [128, 512], F32)
                   for i in range(8)]
        self.ones_col_bf = self.sb("ones_col_bf", [128, 1], BF16)
        self.ones_col_f = self.sb("ones_col_f", [128, 1], F32)
        self.ones_row_f = self.sb("ones_row_f", [1, 128], F32)
        self.p.emit("g", lambda e: e.memset(self.ones_col_bf[:], 1.0),
                    writes=[self.ones_col_bf])
        self.p.emit("g", lambda e: e.memset(self.ones_col_f[:], 1.0),
                    writes=[self.ones_col_f])
        self.p.emit("g", lambda e: e.memset(self.ones_row_f[:], 1.0),
                    writes=[self.ones_row_f])
        self._ident_bf = None

    def prelude_shared(self):
        self.fconst(1e-5)
        self.fconst(1e-6)
        self.fconst(1.0)
        self.ident_bf()
        self._lnp = dict(
            xfb=[self.sb("ln_xfb%d" % i, [128, 2 * C], F32) for i in range(2)],
            sq=[self.sb("ln_sq%d" % i, [128, C], BF16) for i in range(2)],
            tmp=[self.sb("ln_tmp%d" % i, [128, C], F32) for i in range(2)],
            m=self.sb("m_row", [1, C], F32), v=self.sb("v_row", [1, C], F32),
            a=self.sb("a_row", [1, C], F32), b=self.sb("b_row", [1, C], F32),
        )
        self._cvbuf = [self.sb("cvbuf%d" % i, [128, 2048], BF16) for i in range(2)]
        self._arena0 = self.nc.SBUF_PARTITION_SIZE_BYTES - self.nc.sbuf_bytes_remaining

    def phase_begin(self, tag):
        self.p.barrier(exclude_prefix="cw")
        self.p.epoch(tag)
        self._cur = self._arena0

    def sb(self, name, shape, dtype):
        if self._cur is None:
            return self.nc.alloc_sbuf_tensor(name, shape, dtype)
        nbytes = int(np.prod(shape[1:])) * mybir.dt.size(dtype)
        off = self._cur
        self._cur = (off + nbytes + 31) & ~31
        assert self._cur <= self.nc.SBUF_PARTITION_SIZE_BYTES, \
            "arena overflow at %s (%d)" % (name, self._cur)
        return self.nc.alloc_sbuf_tensor_at(name, shape, dtype, offset=off)

    def din(self, name, shape, dtype=F32):
        if name in self.inputs:
            return self.inputs[name][2]
        ap = self.nc.dram_tensor(name, list(shape), dtype, kind="ExternalInput").ap()
        self.inputs[name] = (tuple(shape), dtype, ap)
        return ap

    def get_ws(self, iname, shape, R, M, sname):
        if not hasattr(self, "_ws"):
            self._ws = {}
        if sname not in self._ws:
            ap = self.din(iname, shape)
            self._ws[sname] = self.convert_weight(ap, R, M, sname)
        return self._ws[sname]

    def dout(self, name, shape, dtype=F32):
        return self.nc.dram_tensor(name, list(shape), dtype, kind="ExternalOutput").ap()

    def dscratch(self, name, shape, dtype):
        return self.nc.dram_tensor(name, list(shape), dtype, kind="Internal").ap()

    def const(self, name, arr):
        if name not in self._consts:
            self._consts[name] = self.nc.inline_tensor(np.ascontiguousarray(arr), name).ap()
        return self._consts[name]

    def fconst(self, val):
        key = "fc_%r" % (val,)
        t = self._consts.get(key)
        if t is None:
            t = self.sb("fc%d" % len(self._consts), [128, 1], F32)
            self.p.emit("g", lambda e: e.memset(t[:], float(val)), writes=[t])
            self._consts[key] = t
        return t

    def ident_bf(self):
        if self._ident_bf is None:
            self._ident_bf = self.sb("ident_bf", [128, 128], BF16)
            src = self.const("ident_c", np.eye(128, dtype=bfdt))
            self.p.dma("y", self._ident_bf[:], src[:])
        return self._ident_bf

    def convert_weight(self, src_ap, R, M, name, tmp_cols=2048):
        scr = self.dscratch(name, [R // 128, 128, M], BF16)
        ncols = min(tmp_cols, M)
        if not hasattr(self, "_cvbuf"):
            self._cvbuf = [self.sb("cvbuf%d" % i, [128, 2048], BF16) for i in range(2)]
        buf = self._cvbuf
        if not hasattr(self, "wready"):
            self.wready = {}
        toks = []
        n = 0
        for kt in range(R // 128):
            for ci, c0 in enumerate(range(0, M, ncols)):
                b = buf[(kt * ((M + ncols - 1) // ncols) + ci) % 2]
                nw = min(ncols, M - c0)
                self.p.dma("g", b[:, :nw], src_ap[kt * 128:(kt + 1) * 128, c0:c0 + nw],
                           writes=[b], track_out=False, sem="cwg%d" % (n % 4))
                toks.append(self.p.dma("y", scr[kt, :, c0:c0 + nw], b[:, :nw], reads=[b],
                                       writes=["%s:%d:%d" % (name, kt, ci)],
                                       track_out=False, sem="cwy%d" % (n % 4)))
                n += 1
        self.wready[name] = toks
        return scr

    def ln_chunk(self, xr, ch, ln, eps=1e-5, nd=NDT,
                 pb_sx=4, pb_sq=5, pb_a=6, pb_b=7):
        p, pb = self.p, self.pb
        c0 = ch * C
        L = self._lnp
        m_row, v_row, a_row, b_row = L["m"], L["v"], L["a"], L["b"]

        def load2(i2):
            b = L["xfb"][i2 % 2]
            p.dma("y", b[:].rearrange("p (j t) -> p j t", t=C),
                  xr[i2 * 256:(i2 + 1) * 256, c0:c0 + C].rearrange(
                      "(j p) t -> p j t", p=128),
                  writes=[b], track_out=False)
            return b
        for i2 in range(nd // 2):
            b = load2(i2)
            for j in range(2):
                i = 2 * i2 + j
                sl = b[:, j * C:(j + 1) * C]
                p.acc("t", lambda e, sl=sl, i=i: e.matmul(
                    pb[pb_sx][:1, :C], self.ones_col_f[:], sl,
                    start=(i == 0), stop=(i == nd - 1)),
                    reads=[b, self.ones_col_f], bank=pb[pb_sx],
                    first=(i == 0), last=(i == nd - 1))
                sq = L["sq"][i % 2]
                p.emit("s", lambda e, sl=sl, sq=sq: e.activation(sq[:], sl, AF.Square),
                       reads=[b], writes=[sq])
                p.acc("t", lambda e, sq=sq, i=i: e.matmul(
                    pb[pb_sq][:1, :C], self.ones_col_bf[:], sq[:],
                    start=(i == 0), stop=(i == nd - 1)),
                    reads=[self.ones_col_bf, sq], bank=pb[pb_sq],
                    first=(i == 0), last=(i == nd - 1))
        dinv = 1.0 / (nd * 128)
        p.emit("v", lambda e: e.tensor_scalar(m_row[:], pb[pb_sx][:1, :C], dinv, None, ALU.mult),
               reads=[pb[pb_sx]], writes=[m_row])
        p.emit("v", lambda e: e.tensor_scalar(v_row[:], pb[pb_sq][:1, :C], dinv, None, ALU.mult),
               reads=[pb[pb_sq]], writes=[v_row])
        p.emit("v", lambda e: e.tensor_tensor(a_row[:], m_row[:], m_row[:], ALU.mult),
               reads=[m_row], writes=[a_row])
        p.emit("v", lambda e: e.tensor_tensor(v_row[:], v_row[:], a_row[:], ALU.subtract),
               reads=[v_row, a_row], writes=[v_row])
        epsb = self.fconst(eps)
        p.emit("s", lambda e: e.activation(a_row[:], v_row[:], AF.Ln, bias=epsb[:1, :]),
               reads=[v_row, epsb], writes=[a_row])
        p.emit("s", lambda e: e.activation(a_row[:], a_row[:], AF.Exp, scale=-0.5),
               reads=[a_row], writes=[a_row])
        p.emit("v", lambda e: e.scalar_tensor_tensor(b_row[:], m_row[:], -1.0, a_row[:],
                                                     ALU.mult, ALU.mult),
               reads=[m_row, a_row], writes=[b_row])
        p.emit("t", lambda e: e.matmul(pb[pb_a][:, :C], self.ones_row_f[:], a_row[:],
                                       start=True, stop=True),
               reads=[self.ones_row_f, a_row], writes=[pb[pb_a]])
        p.emit("t", lambda e: e.matmul(pb[pb_b][:, :C], self.ones_row_f[:], b_row[:],
                                       start=True, stop=True),
               reads=[self.ones_row_f, b_row], writes=[pb[pb_b]])
        for i2 in range(nd // 2):
            b = load2(i2)
            for j in range(2):
                i = 2 * i2 + j
                sl = b[:, j * C:(j + 1) * C]
                tmp = L["tmp"][i % 2]
                p.emit("v", lambda e, sl=sl, tmp=tmp: e.tensor_tensor(
                    tmp[:], sl, pb[pb_a][:, :C], ALU.mult),
                    reads=[b, pb[pb_a]], writes=[tmp])
                p.emit("v", lambda e, tmp=tmp, i=i: e.tensor_tensor(
                    ln[i][:], tmp[:], pb[pb_b][:, :C], ALU.add),
                    reads=[tmp, pb[pb_b]], writes=[ln[i]])

    def phase_l1_ffn(self, xr_in, x_out_writer):
        p, pb, nc = self.p, self.pb, self.nc
        INTER = 8192
        NJ = INTER // 128
        NO = 2 * INTER // 128
        wgu_s = self.get_ws("L1_ffn_w_gate_upT", [D, 2 * INTER], D, 2 * INTER, "wgu_s")
        wdn_s = self.get_ws("L1_ffn_w_downT", [INTER, D], INTER, D, "wdn_s")

        ln = [self.sb("f4_ln%d" % i, [128, C], BF16) for i in range(NDT)]
        xres = [self.sb("f4_xr%d" % i, [128, C], F32) for i in range(2)]
        h2 = [self.sb("f4_h%d" % j, [128, C], BF16) for j in range(NJ)]
        wkg = [self.sb("f4_wkg%d" % i, [128, 1024], BF16) for i in range(4)]
        dk = [self.sb("f4_dk%d" % i, [128, 1024], BF16) for i in range(4)]
        stmp = [self.sb("f4_st%d" % i, [128, C], BF16) for i in range(2)]
        otile = [self.sb("f4_ot%d" % i, [128, C], F32) for i in range(2)]

        for ch in range(NCH):
            self.ln_chunk(xr_in, ch, ln)
            for g in range(16):
                for kt in range(16):
                    bg = wkg[kt % 4]
                    gb = 4 * g * 128
                    srcv = wgu_s[kt].rearrange("p (blk m) -> p blk m", m=8192)[:, :, gb:gb + 512]
                    p.dma("y" if kt % 2 == 0 else "g",
                          bg[:].rearrange("p (blk m) -> p blk m", m=512), srcv,
                          writes=[bg], track_out=False, deps=self.wready["wgu_s"])
                    for oi in range(8):
                        col = (oi // 4) * 512 + (oi % 4) * 128
                        p.acc("t", lambda e, b=bg, col=col, oi=oi, kt=kt:
                              e.matmul(pb[oi][:, :C], b[:, col:col + 128], ln[kt][:],
                                       start=(kt == 0), stop=(kt == 15)),
                              reads=[bg, ln[kt]], bank=pb[oi],
                              first=(kt == 0), last=(kt == 15))
                for j in range(4):
                    st = stmp[j % 2]
                    p.emit("s", lambda e, j=j, st=st: e.activation(st[:], pb[j][:, :C], AF.Silu),
                           reads=[pb[j]], writes=[st])
                    jj = 4 * g + j
                    p.emit("v", lambda e, j=j, st=st, jj=jj:
                           e.tensor_tensor(h2[jj][:], st[:], pb[4 + j][:, :C], ALU.mult),
                           reads=[st, pb[4 + j]], writes=[h2[jj]])
            for g in range(2):
                for jt in range(NJ):
                    b = dk[jt % 4]
                    p.dma("y" if jt % 2 == 0 else "g", b[:],
                          wdn_s[jt, :, g * 1024:(g + 1) * 1024], writes=[b],
                          deps=self.wready["wdn_s"])
                    for oi in range(8):
                        p.acc("t", lambda e, b=b, oi=oi, jt=jt:
                              e.matmul(pb[oi][:, :C], b[:, oi * 128:(oi + 1) * 128], h2[jt][:],
                                       start=(jt == 0), stop=(jt == NJ - 1)),
                              reads=[b, h2[jt]], bank=pb[oi],
                              first=(jt == 0), last=(jt == NJ - 1))
                for oi in range(8):
                    i = g * 8 + oi
                    ob = otile[oi % 2]
                    xv = xres[oi % 2]
                    p.dma("y", xv[:], xr_in[i * 128:(i + 1) * 128, ch * C:(ch + 1) * C])
                    p.emit("v", lambda e, oi=oi, ob=ob, xv=xv:
                           e.tensor_tensor(ob[:], pb[oi][:, :C], xv[:], ALU.add),
                           reads=[pb[oi], xv], writes=[ob])
                    x_out_writer(i, ch, ob)

    def phase_l0_attn(self, xr_in, x_out_writer):
        def x_out_writer2(i, ch, ob, key):
            self._writer_key = key
            x_out_writer(i, ch, ob)
        p, pb, nc = self.p, self.pb, self.nc
        HEADS, KV = 16, 8
        SC = 1.0 / float(np.sqrt(2048.0))
        wqkv_s = self.get_ws("L0_attn_WqkvT", [D, 4096], D, 4096, "wqkv_s")
        wot_s = self.get_ws("L0_attn_WOT", [D, D], D, D, "wot_s")
        qs = self.dscratch("qs", [D, T], BF16)
        ks = self.dscratch("ks", [KV * 128, T], BF16)
        cxs = self.dscratch("cxs", [D, T], BF16)
        cosT, sinT = _rope_tables()
        cos_sb = self.sb("cos_sb", [128, T], BF16)
        sin_sb = self.sb("sin_sb", [128, T], BF16)
        p.dma("y", cos_sb[:], self.const("cos_c", cosT.astype(bfdt))[:])
        p.dma("y", sin_sb[:], self.const("sin_c", sinT.astype(bfdt))[:])
        ident = self.ident_bf()
        vT = [self.sb("vT%d" % kv, [128, T], BF16) for kv in range(KV)]
        Pt = self.sb("Pt", [128, 32 * C], BF16)
        self._Pt, self._vT = Pt, vT
        ln = [self.sb("a1_ln%d" % i, [128, C], BF16) for i in range(NDT)]
        slab = [self.sb("a1_wk%d" % i, [128, 1024], BF16) for i in range(3)]
        rot = self.sb("a1_rot", [128, C], BF16)
        t1 = self.sb("a1_t1", [128, C], BF16)
        vtmp = [self.sb("a1_vt%d" % i, [128, C], BF16) for i in range(2)]
        sqb = vtmp
        qkt = [self.sb("a1_qk%d" % i, [128, C], BF16) for i in range(2)]
        nqmax = self.sb("a1_nq", [1, 16], F32)
        nkmax = self.sb("a1_nk", [1, 16], F32)
        brow = self.sb("a1_brow", [1, 16], F32)
        tmp11 = self.sb("a1_t11", [1, 1], F32)
        bias_sb = self.sb("a1_bias", [128, 16], F32)
        kbuf = [self.sb("a1_kb%d" % i, [128, T], BF16) for i in range(2)]
        qbuf = [self.sb("a1_qb%d" % i, [128, T], BF16) for i in range(2)]
        zr_row = self.sb("a1_zr", [1, C], F32)
        zr_bc = self.sb("a1_zbc", [128, C], F32)
        ctile = qkt
        cbuf = [Pt[:, i * C:(i + 1) * C] for i in range(16)]
        xres = [Pt[:, 8192 + i * 1024:8192 + (i + 1) * 1024].bitcast(F32)
                for i in range(2)]
        otile = [Pt[:, 10240 + i * 1024:10240 + (i + 1) * 1024].bitcast(F32)
                 for i in range(2)]
        p.emit("g", lambda e: e.memset(nqmax[:], 0.0), writes=[nqmax])
        p.emit("g", lambda e: e.memset(nkmax[:], 0.0), writes=[nkmax])

        for ch in range(NCH):
            self.ln_chunk(xr_in, ch, ln)
            c0 = ch * C
            for g in range(4):
                ots = list(range(8 * g, 8 * g + 8))
                for kt in range(16):
                    b = slab[kt % 3]
                    p.dma("y" if kt % 2 == 0 else "g", b[:],
                          wqkv_s[kt, :, g * 1024:(g + 1) * 1024],
                          deps=self.wready["wqkv_s"])
                    for oi in range(8):
                        p.acc("t", lambda e, b=b, oi=oi, kt=kt:
                              e.matmul(pb[oi][:, :C], b[:, oi * 128:(oi + 1) * 128],
                                       ln[kt][:], start=(kt == 0), stop=(kt == 15)),
                              reads=[b, ln[kt]], bank=pb[oi],
                              first=(kt == 0), last=(kt == 15))
                for oi in range(8):
                    ot = 8 * g + oi
                    if ot < 24:
                        p.emit("v", lambda e, oi=oi: e.tensor_scalar(
                            rot[0:64, :], pb[oi][64:128, :C], -1.0, None, ALU.mult),
                            reads=[pb[oi]], writes=[rot])
                        p.emit("v", lambda e, oi=oi: e.tensor_copy(
                            rot[64:128, :], pb[oi][0:64, :C]),
                            reads=[pb[oi]], writes=[rot])
                        p.emit("v", lambda e, oi=oi, c0=c0: e.tensor_tensor(
                            t1[:], pb[oi][:, :C], cos_sb[:, c0:c0 + C], ALU.mult),
                            reads=[pb[oi], cos_sb], writes=[t1])
                        p.emit("v", lambda e, c0=c0: e.tensor_tensor(
                            rot[:], rot[:], sin_sb[:, c0:c0 + C], ALU.mult),
                            reads=[rot, sin_sb], writes=[rot])
                        qk = qkt[oi % 2]
                        p.emit("v", lambda e, qk=qk: e.tensor_tensor(
                            qk[:], t1[:], rot[:], ALU.add),
                            reads=[t1, rot], writes=[qk])
                        if ot < 16:
                            dst, r0, tag = qs, ot * 128, "qs:%d:%d" % (ot, ch)
                        else:
                            dst, r0, tag = ks, (ot - 16) * 128, "ks:%d:%d" % (ot - 16, ch)
                        p.dma("y", dst[r0:r0 + 128, c0:c0 + C], qk[:], reads=[qk],
                              writes=[tag], track_out=False)
                        sq = sqb[oi % 2]
                        p.emit("v", lambda e, sq=sq, qk=qk: e.tensor_tensor(
                            sq[:], qk[:], qk[:], ALU.mult), reads=[qk], writes=[sq])
                        p.emit("t", lambda e, oi=oi, sq=sq: e.matmul(
                            pb[oi][:1, :C], self.ones_col_bf[:], sq[:], start=True, stop=True),
                            reads=[self.ones_col_bf, sq], writes=[pb[oi]])
                        p.emit("v", lambda e, oi=oi: e.reduce_max(
                            tmp11[:], pb[oi][:1, :C], AX.X), reads=[pb[oi]], writes=[tmp11])
                        if ot < 16:
                            p.emit("v", lambda e, ot=ot: e.tensor_tensor(
                                nqmax[:, ot:ot + 1], nqmax[:, ot:ot + 1], tmp11[:], ALU.max),
                                reads=[nqmax, tmp11], writes=[nqmax])
                        else:
                            kv = ot - 16
                            for hh in (2 * kv, 2 * kv + 1):
                                p.emit("v", lambda e, hh=hh: e.tensor_tensor(
                                    nkmax[:, hh:hh + 1], nkmax[:, hh:hh + 1], tmp11[:], ALU.max),
                                    reads=[nkmax, tmp11], writes=[nkmax])
                    else:
                        kv = ot - 24
                        vt = vtmp[oi % 2]
                        p.emit("v", lambda e, vt=vt, oi=oi: e.tensor_copy(vt[:], pb[oi][:, :C]),
                               reads=[pb[oi]], writes=[vt])
                        for j in range(4):
                            cb = (ch * 4 + j) * 128
                            p.dma("y", vT[kv][:, cb:cb + 128],
                                  vt[:, j * 128:(j + 1) * 128],
                                  reads=[vt], writes=["vT%d:%d" % (kv, cb)],
                                  track_out=False, transpose=True)
        p.emit("v", lambda e: e.tensor_tensor(brow[:], nqmax[:], nkmax[:], ALU.mult),
               reads=[nqmax, nkmax], writes=[brow])
        p.emit("s", lambda e: e.activation(brow[:], brow[:], AF.Ln),
               reads=[brow], writes=[brow])
        p.emit("s", lambda e: e.activation(brow[:], brow[:], AF.Exp, scale=0.5),
               reads=[brow], writes=[brow])
        p.emit("v", lambda e: e.tensor_scalar(brow[:], brow[:], -SC, None, ALU.mult),
               reads=[brow], writes=[brow])
        for h in range(16):
            bank = pb[h % 2]
            p.emit("t", lambda e, h=h, bank=bank: e.matmul(
                bank[:, :1], self.ones_row_f[:], brow[:, h:h + 1], start=True, stop=True),
                reads=[self.ones_row_f, brow], writes=[bank])
            p.emit("v", lambda e, h=h, bank=bank: e.tensor_copy(
                bias_sb[:, h:h + 1], bank[:, :1]), reads=[bank], writes=[bias_sb])
        p.barrier()

        p.dma("y", kbuf[0][:], ks[0:128, :])
        for h in range(HEADS):
            kv = h // 2
            kb_ = kbuf[h % 2]
            if h + 1 < HEADS:
                p.dma("y", kbuf[(h + 1) % 2][:],
                      ks[((h + 1) // 2) * 128:((h + 1) // 2 + 1) * 128, :])
            qh = qbuf[h % 2]
            p.dma("y", qh[:], qs[h * 128:(h + 1) * 128, :], writes=[qh],
                  track_out=False)
            for qc in range(NCH):
                qb = qh[:, qc * C:(qc + 1) * C]
                pend = []
                for kt in range(32):
                    bank = pb[kt % 2]
                    p.emit("t", lambda e, bank=bank, kt=kt, qb=qb, kb_=kb_: e.matmul(
                        bank[:, :C], kb_[:, kt * 128:(kt + 1) * 128], qb,
                        start=True, stop=True),
                        reads=[kb_, qh], writes=[bank])
                    p.emit("s", lambda e, bank=bank, kt=kt, h=h: e.activation(
                        Pt[:, kt * C:(kt + 1) * C], bank[:, :C], AF.Exp,
                        bias=bias_sb[:, h:h + 1], scale=SC),
                        reads=[bank, bias_sb], writes=["Pt:%d" % kt])
                    pend.append(kt)
                    if len(pend) > 1:
                        self._attn_zc(pend.pop(0), kv)
                while pend:
                    self._attn_zc(pend.pop(0), kv)
                p.emit("s", lambda e: e.activation(zr_row[:], pb[2][:1, :C], AF.Ln),
                       reads=[pb[2]], writes=[zr_row])
                p.emit("s", lambda e: e.activation(zr_row[:], zr_row[:], AF.Exp, scale=-1.0),
                       reads=[zr_row], writes=[zr_row])
                p.emit("t", lambda e: e.matmul(pb[4][:, :C], self.ones_row_f[:],
                                               zr_row[:], start=True, stop=True),
                       reads=[self.ones_row_f, zr_row], writes=[pb[4]])
                p.emit("v", lambda e: e.tensor_copy(zr_bc[:], pb[4][:, :C]),
                       reads=[pb[4]], writes=[zr_bc])
                ct = ctile[qc % 2]
                p.emit("v", lambda e, ct=ct: e.tensor_tensor(ct[:], pb[3][:, :C],
                                                             zr_bc[:], ALU.mult),
                       reads=[pb[3], zr_bc], writes=[ct])
                p.dma("y", cxs[h * 128:(h + 1) * 128, qc * C:(qc + 1) * C], ct[:],
                      reads=[ct], writes=["cxs:%d:%d" % (h, qc)], track_out=False)
        p.barrier()

        for ch in range(NCH):
            c0 = ch * C
            for ib in range(2):
                p.dma("y", Pt[:, ib * 8 * C:(ib + 1) * 8 * C].rearrange(
                          "p (i t) -> p i t", t=C),
                      cxs.rearrange("(i p) t -> p i t", p=128)[:, ib * 8:(ib + 1) * 8,
                                                               c0:c0 + C],
                      writes=["cbuf%d" % ib], track_out=False)
            for g in range(2):
                for kt in range(16):
                    b = slab[kt % 3]
                    p.dma("y" if kt % 2 == 0 else "g", b[:],
                          wot_s[kt, :, g * 1024:(g + 1) * 1024],
                          deps=self.wready["wot_s"])
                    for oi in range(8):
                        p.acc("t", lambda e, b=b, oi=oi, kt=kt: e.matmul(
                            pb[oi][:, :C], b[:, oi * 128:(oi + 1) * 128], cbuf[kt],
                            start=(kt == 0), stop=(kt == 15)),
                            reads=[b, "cbuf%d" % (kt // 8)], bank=pb[oi],
                            first=(kt == 0), last=(kt == 15))
                for oi in range(8):
                    i = g * 8 + oi
                    xv, ob = xres[oi % 2], otile[oi % 2]
                    p.dma("y", xv, xr_in[i * 128:(i + 1) * 128, c0:c0 + C],
                          writes=["a3xr%d" % (oi % 2)], track_out=False)
                    p.emit("v", lambda e, oi=oi, xv=xv, ob=ob: e.tensor_tensor(
                        ob, pb[oi][:, :C], xv, ALU.add),
                        reads=[pb[oi], "a3xr%d" % (oi % 2)], writes=["a3ot%d" % (oi % 2)])
                    x_out_writer2(i, ch, ob, "a3ot%d" % (oi % 2))
        return

    def _attn_zc(self, kt, kv):
        p, pb = self.p, self.pb
        Pt, vT = self._Pt, self._vT
        p.acc("t", lambda e, kt=kt: e.matmul(
            pb[2][:1, :C], self.ones_col_bf[:], Pt[:, kt * C:(kt + 1) * C],
            start=(kt == 0), stop=(kt == 31)),
            reads=[self.ones_col_bf, "Pt:%d" % kt], bank=pb[2],
            first=(kt == 0), last=(kt == 31))
        p.acc("t", lambda e, kt=kt, kv=kv: e.matmul(
            pb[3][:, :C], vT[kv][:, kt * 128:(kt + 1) * 128],
            Pt[:, kt * C:(kt + 1) * C], start=(kt == 0), stop=(kt == 31)),
            reads=[vT[kv], "Pt:%d" % kt], bank=pb[3],
            first=(kt == 0), last=(kt == 31))

    def _dequant(self, packed_ap, scalesT_ap, n_out_tiles, M_in, scr):
        p, pb = self.p, self.pb
        NW, NB, NG = M_in // 8, M_in // 2, M_in // 128
        if not hasattr(self, "_dq"):
            MX = 5632
            self._dq = dict(
                pk=self.sb("dq_pk", [128, MX // 8], I32),
                n16=self.sb("dq_n16", [128, MX // 2], U16),
                lo=self.sb("dq_lo", [128, MX // 2], U16),
                m=self.sb("dq_m", [128, MX // 2], U16),
                sb_=self.sb("dq_sb", [128, MX // 2], U16),
                bA=self.sb("dq_bA", [128, MX // 2], U16),
                bB=self.sb("dq_bB", [128, MX // 2], U16),
                mge=self.sb("dq_mge", [128, MX // 2], U16),
                bits=self.sb("dq_bits", [128, MX], U16),
                scl=self.sb("dq_scl", [128, MX // 128], mybir.dt.float16),
                sclf=self.sb("dq_sclf", [128, MX // 128], F32),
            )
        Q = self._dq
        pk, n16, lo = Q["pk"], Q["n16"], Q["lo"]
        m_, sb_, bA, bB, mge = Q["m"], Q["sb_"], Q["bA"], Q["bB"], Q["mge"]
        bits, scl, sclf = Q["bits"], Q["scl"], Q["sclf"]
        hi = n16
        ident = self.ident_bf()
        for o in range(n_out_tiles):
            r0 = o * 128
            p.dma("y", pk[:, :NW], packed_ap[r0:r0 + 128, :], writes=[pk], track_out=False)
            p.dma("y", scl[:, :NG], scalesT_ap[r0:r0 + 128, :], writes=[scl], track_out=False)
            p.emit("v", lambda e: e.tensor_copy(n16[:, :NB], pk[:, :NW].bitcast(U8)),
                   reads=[pk], writes=[n16])
            p.emit("v", lambda e: e.tensor_scalar(lo[:, :NB], n16[:, :NB], 15, None, ALU.bitwise_and),
                   reads=[n16], writes=[lo])
            p.emit("v", lambda e: e.tensor_scalar(n16[:, :NB], n16[:, :NB], 4, None, ALU.logical_shift_right),
                   reads=[n16], writes=[n16])
            for srcb, ph in ((lo, 0), (hi, 1)):
                p.emit("v", lambda e, s=srcb: e.tensor_scalar(m_[:, :NB], s[:, :NB], 7, None, ALU.bitwise_and),
                       reads=[srcb], writes=[m_])
                p.emit("v", lambda e, s=srcb: e.tensor_scalar(sb_[:, :NB], s[:, :NB], 8, 12,
                                                              ALU.bitwise_and, ALU.logical_shift_left),
                       reads=[srcb], writes=[sb_])
                p.emit("v", lambda e: e.tensor_single_scalar(bA[:, :NB], m_[:, :NB], 252, ALU.add),
                       reads=[m_], writes=[bA])
                p.emit("v", lambda e: e.tensor_single_scalar(bA[:, :NB], bA[:, :NB], 6, ALU.logical_shift_left),
                       reads=[bA], writes=[bA])
                p.emit("v", lambda e: e.tensor_scalar(bB[:, :NB], m_[:, :NB], 0x3F00, None, ALU.mult),
                       reads=[m_], writes=[bB])
                p.emit("v", lambda e: e.tensor_scalar(mge[:, :NB], m_[:, :NB], 2, None, ALU.is_ge),
                       reads=[m_], writes=[mge])
                p.emit("v", lambda e: e.copy_predicated(bB[:, :NB], mge[:, :NB], bA[:, :NB]),
                       reads=[mge, bA], writes=[bB])
                outv = bits[:, :M_in].rearrange("p (c two) -> p c two", two=2)[:, :, ph]
                p.emit("v", lambda e, o_=outv: e.tensor_tensor(o_, bB[:, :NB], sb_[:, :NB], ALU.bitwise_or),
                       reads=[bB, sb_], writes=[bits])
            p.emit("v", lambda e: e.tensor_copy(sclf[:, :NG], scl[:, :NG]),
                   reads=[scl], writes=[sclf])
            bview = bits[:, :M_in].bitcast(BF16).rearrange("p (g i) -> p g i", i=128)
            sview = sclf[:, :NG].unsqueeze(2).to_broadcast([128, NG, 128])
            p.emit("v", lambda e: e.tensor_tensor(bview, bview, sview, ALU.mult),
                   reads=[sclf], writes=[bits])
            p.dma("y", scr[o * 128:(o + 1) * 128, :], bits[:, :M_in].bitcast(BF16),
                  reads=[bits], writes=["dqs" + scr.tensor.name + str(o)],
                  track_out=False)

    def phase_l0_ffn(self, xr_in, x_out_writer):
        p, pb = self.p, self.pb
        I2, NJ, NO = 5632, 44, 88
        gup = self.din("L0_ffn_gate_up_packed", [2 * I2, 256], I32)
        gus = self.din("L0_ffn_gate_up_scalesT", [2 * I2, 16], mybir.dt.float16)
        dnp = self.din("L0_ffn_down_packed", [D, 704], I32)
        dns = self.din("L0_ffn_down_scalesT", [D, 44], mybir.dt.float16)
        wgu_scr = self.dscratch("wdq_gu", [NO * 128, D], BF16)
        wdn_scr = self.dscratch("wdq_dn", [16 * 128, I2], BF16)
        self._dequant(gup, gus, NO, D, wgu_scr)
        self._dequant(dnp, dns, 16, I2, wdn_scr)
        p.barrier()

        ln = [self.sb("f2_ln%d" % i, [128, C], BF16) for i in range(NDT)]
        h2 = [self.sb("f2_h%d" % j, [128, C], BF16) for j in range(NJ)]
        ysb = [self.sb("f2_y%d" % i, [128, C], BF16) for i in range(16)]
        wkg = [self.sb("f2_wkg%d" % i, [128, 512], BF16) for i in range(4)]
        wku = [self.sb("f2_wku%d" % i, [128, 512], BF16) for i in range(4)]
        dk = [self.sb("f2_dk%d" % i, [128, 1024], BF16) for i in range(4)]
        stmp = [self.sb("f2_st%d" % i, [128, C], BF16) for i in range(2)]
        sqy = [self.sb("f2_sqy%d" % i, [128, C], BF16) for i in range(2)]
        rr = self.sb("f2_rr", [1, C], F32)
        rbc = self.sb("f2_rbc", [128, C], F32)
        xres = [self.sb("f2_xr%d" % i, [128, C], F32) for i in range(2)]
        otile = [self.sb("f2_ot%d" % i, [128, C], F32) for i in range(2)]

        for ch in range(NCH):
            c0 = ch * C
            self.ln_chunk(xr_in, ch, ln)
            for g in range(11):
                for kt in range(16):
                    bg, bu = wkg[kt % 4], wku[kt % 4]
                    p.dma("y", bg[:],
                          wgu_scr[4 * g * 128:4 * g * 128 + 512,
                                  kt * 128:(kt + 1) * 128],
                          writes=[bg], track_out=False, transpose=True)
                    p.dma("y", bu[:],
                          wgu_scr[(44 + 4 * g) * 128:(44 + 4 * g) * 128 + 512,
                                  kt * 128:(kt + 1) * 128],
                          writes=[bu], track_out=False, transpose=True)
                    for oi in range(8):
                        b = bg if oi < 4 else bu
                        col = (oi % 4) * 128
                        p.acc("t", lambda e, b=b, col=col, oi=oi, kt=kt:
                              e.matmul(pb[oi][:, :C], b[:, col:col + 128], ln[kt][:],
                                       start=(kt == 0), stop=(kt == 15)),
                              reads=[b, ln[kt]], bank=pb[oi],
                              first=(kt == 0), last=(kt == 15))
                for j in range(4):
                    st = stmp[j % 2]
                    p.emit("s", lambda e, j=j, st=st: e.activation(st[:], pb[j][:, :C], AF.Silu),
                           reads=[pb[j]], writes=[st])
                    jj = 4 * g + j
                    p.emit("v", lambda e, j=j, st=st, jj=jj:
                           e.tensor_tensor(h2[jj][:], st[:], pb[4 + j][:, :C], ALU.mult),
                           reads=[st, pb[4 + j]], writes=[h2[jj]])
            for g in range(2):
                for jt in range(NJ):
                    b = dk[jt % 4]
                    p.dma("y", b[:],
                          wdn_scr[g * 1024:(g + 1) * 1024, jt * 128:(jt + 1) * 128],
                          writes=[b], track_out=False, transpose=True)
                    for oi in range(8):
                        p.acc("t", lambda e, b=b, oi=oi, jt=jt:
                              e.matmul(pb[oi][:, :C], b[:, oi * 128:(oi + 1) * 128], h2[jt][:],
                                       start=(jt == 0), stop=(jt == NJ - 1)),
                              reads=[b, h2[jt]], bank=pb[oi],
                              first=(jt == 0), last=(jt == NJ - 1))
                for oi in range(8):
                    i = g * 8 + oi
                    p.emit("v", lambda e, oi=oi, i=i: e.tensor_copy(ysb[i][:], pb[oi][:, :C]),
                           reads=[pb[oi]], writes=[ysb[i]])
            for i in range(16):
                sq = sqy[i % 2]
                p.emit("s", lambda e, i=i, sq=sq: e.activation(sq[:], ysb[i][:], AF.Square),
                       reads=[ysb[i]], writes=[sq])
                p.acc("t", lambda e, sq=sq, i=i: e.matmul(pb[0][:1, :C], self.ones_col_bf[:],
                                                          sq[:], start=(i == 0), stop=(i == 15)),
                      reads=[self.ones_col_bf, sq], bank=pb[0],
                      first=(i == 0), last=(i == 15))
            epsb = self.fconst(1e-6)
            p.emit("s", lambda e: e.activation(rr[:], pb[0][:1, :C], AF.Ln,
                                               bias=epsb[:1, :], scale=1.0 / 2048.0),
                   reads=[pb[0], epsb], writes=[rr])
            p.emit("s", lambda e: e.activation(rr[:], rr[:], AF.Exp, scale=-0.5),
                   reads=[rr], writes=[rr])
            p.emit("t", lambda e: e.matmul(pb[1][:, :C], self.ones_row_f[:], rr[:],
                                           start=True, stop=True),
                   reads=[self.ones_row_f, rr], writes=[pb[1]])
            p.emit("v", lambda e: e.tensor_copy(rbc[:], pb[1][:, :C]),
                   reads=[pb[1]], writes=[rbc])
            for i in range(16):
                xv, ob = xres[i % 2], otile[i % 2]
                p.dma("y", xv[:], xr_in[i * 128:(i + 1) * 128, c0:c0 + C])
                p.emit("v", lambda e, i=i, ob=ob: e.tensor_tensor(ob[:], ysb[i][:], rbc[:], ALU.mult),
                       reads=[ysb[i], rbc], writes=[ob])
                p.emit("v", lambda e, ob=ob, xv=xv: e.tensor_tensor(ob[:], ob[:], xv[:], ALU.add),
                       reads=[ob, xv], writes=[ob])
                x_out_writer(i, ch, ob)

    def phase_l1_attn(self, xr_in, x_out_writer):
        p, pb = self.p, self.pb
        NH, HD, DST, QL, NCK = 64, 64, 128, 256, 16
        DI = 4096
        conv_w = self.din("L1_attn_conv_w", [4352, 4])
        conv_b = self.din("L1_attn_conv_b", [4352])
        dt_bias = self.din("L1_attn_dt_bias", [64])
        A_log = self.din("L1_attn_A_log", [64])
        D_param = self.din("L1_attn_D_param", [64])
        norm_w = self.din("L1_attn_norm_w", [4096])
        self._dtbuf = self.sb("m_dt", [64, T], F32)
        win_s = self.get_ws("L1_attn_W_inT", [D, 8512], D, 8512, "win_s")
        wout_s = self.get_ws("L1_attn_W_outT", [DI, D], DI, D, "wout_s")
        zs = self.dscratch("zs", [DI, T], BF16)
        xbcp = self.dscratch("xbcp", [4352, T], F32)
        xbcc = self.dscratch("xbcc", [4352, T], BF16)
        acum_scr = self.dscratch("acum_scr", [64, T], F32)
        csscr = self.dscratch("csscr", [NCK, 64], F32)
        yns = self.dscratch("yns", [DI, T], BF16)
        identb = self.ident_bf()
        identf = self.sb("ident_f", [128, 128], F32)
        p.dma("y", identf[:], self.const("identf_c", np.eye(128, dtype=np.float32))[:])

        ln = [self.sb("m_ln%d" % i, [128, C], BF16) for i in range(NDT)]
        slab = [self.sb("m_wk%d" % i, [128, 1024], BF16) for i in range(4)]
        ztile = [self.sb("m_zt%d" % i, [128, C], BF16) for i in range(2)]
        xbt = [self.sb("m_xbt%d" % i, [128, C], F32) for i in range(2)]

        groups = [list(range(8 * g, 8 * g + 8)) for g in range(8)] + [[64, 65, 66]]
        for ch in range(NCH):
            c0 = ch * C
            self.ln_chunk(xr_in, ch, ln)
            for ots in groups:
                g0 = ots[0] * 128
                gw = (ots[-1] * 128 + (64 if ots[-1] == 66 else 128)) - g0
                for kt in range(16):
                    b = slab[kt % 4]
                    p.dma("y" if kt % 2 == 0 else "g", b[:, :gw],
                          win_s[kt, :, g0:g0 + gw], writes=[b], track_out=False,
                          deps=self.wready["win_s"])
                    for oi, ot in enumerate(ots):
                        mw = 64 if ot == 66 else 128
                        p.acc("t", lambda e, b=b, oi=oi, kt=kt, mw=mw:
                              e.matmul(pb[oi][:mw, :C], b[:, oi * 128:oi * 128 + mw],
                                       ln[kt][:], start=(kt == 0), stop=(kt == 15)),
                              reads=[b, ln[kt]], bank=pb[oi],
                              first=(kt == 0), last=(kt == 15))
                for oi, ot in enumerate(ots):
                    if ot < 32:
                        zt = ztile[oi % 2]
                        p.emit("v", lambda e, oi=oi, zt=zt: e.tensor_copy(zt[:], pb[oi][:, :C]),
                               reads=[pb[oi]], writes=[zt])
                        p.dma("y", zs[ot * 128:(ot + 1) * 128, c0:c0 + C], zt[:],
                              reads=[zt], writes=["zs:%d:%d" % (ot, ch)], track_out=False)
                    elif ot < 66:
                        xt = xbt[oi % 2]
                        p.emit("v", lambda e, oi=oi, xt=xt: e.tensor_copy(xt[:], pb[oi][:, :C]),
                               reads=[pb[oi]], writes=[xt])
                        r0 = (ot - 32) * 128
                        p.dma("y", xbcp[r0:r0 + 128, c0:c0 + C], xt[:],
                              reads=[xt], writes=["xbcp:%d:%d" % (ot, ch)], track_out=False)
                    else:
                        p.emit("v", lambda e, oi=oi, c0=c0: e.tensor_copy(
                            self._dtbuf[:, c0:c0 + C], pb[oi][:64, :C]),
                            reads=[pb[oi]], writes=[self._dtbuf])
        p.barrier()

        cw_sb = self.sb("m_cw", [128, 34 * 4], F32)
        cb_sb = self.sb("m_cb", [128, 34], F32)
        p.dma("y", cw_sb[:].rearrange("p (a k) -> p a k", k=4),
              conv_w.rearrange("(a p) k -> p a k", p=128), writes=[cw_sb], track_out=False)
        p.dma("y", cb_sb[:], conv_b.rearrange("(a p) -> p a", p=128),
              writes=[cb_sb], track_out=False, allow_slow_non_contiguous=True)
        SEG = 1024
        xin = [self.sb("m_xin%d" % i, [128, SEG + 3], F32) for i in range(2)]
        cacc = [self.sb("m_cacc%d" % i, [128, SEG], F32) for i in range(2)]
        cvout = [self.sb("m_cvout%d" % i, [128, SEG], BF16) for i in range(2)]
        for a in range(34):
            r0 = a * 128
            for sg in range(T // SEG):
                xi = xin[sg % 2]
                ca, cv = cacc[sg % 2], cvout[sg % 2]
                s0 = sg * SEG
                if sg == 0:
                    p.emit("g", lambda e, xi=xi: e.memset(xi[:, :3], 0.0), writes=[xi])
                    p.dma("y", xi[:, 3:], xbcp[r0:r0 + 128, 0:SEG], writes=[xi],
                          track_out=False)
                else:
                    p.dma("y", xi[:], xbcp[r0:r0 + 128, s0 - 3:s0 + SEG], writes=[xi],
                          track_out=False)
                p.emit("v", lambda e, xi=xi, a=a, ca=ca: e.tensor_scalar(
                    ca[:], xi[:, 0:SEG], cw_sb[:, 4 * a:4 * a + 1], cb_sb[:, a:a + 1],
                    ALU.mult, ALU.add), reads=[xi, cw_sb, cb_sb], writes=[ca])
                for k in range(1, 4):
                    p.emit("v", lambda e, xi=xi, a=a, k=k, ca=ca: e.scalar_tensor_tensor(
                        ca[:], xi[:, k:k + SEG], cw_sb[:, 4 * a + k:4 * a + k + 1],
                        ca[:], ALU.mult, ALU.add),
                        reads=[xi, cw_sb, ca], writes=[ca])
                p.emit("s", lambda e, ca=ca, cv=cv: e.activation(cv[:], ca[:], AF.Silu),
                       reads=[ca], writes=[cv])
                p.dma("y", xbcc[r0:r0 + 128, s0:s0 + SEG], cv[:],
                      reads=[cv], writes=["xbcc:%d:%d" % (a, sg)], track_out=False)
        dtb_col = self.sb("m_dtb", [64, 1], F32)
        acol = self.sb("m_acol", [64, 1], F32)
        dcol = self.sb("m_dcol", [64, 1], F32)
        p.dma("y", dtb_col[:], dt_bias.unsqueeze(1))
        p.dma("y", acol[:], A_log.unsqueeze(1))
        p.dma("y", dcol[:], D_param.unsqueeze(1))
        p.emit("s", lambda e: e.activation(acol[:], acol[:], AF.Exp),
               reads=[acol], writes=[acol])
        p.emit("v", lambda e: e.tensor_scalar(acol[:], acol[:], -1.0, None, ALU.mult),
               reads=[acol], writes=[acol])
        dts = self._dtbuf
        stmp = self.sb("m_stmp", [64, 2048], F32)
        one64 = self.fconst(1.0)
        for sg in range(2):
            s0 = sg * 2048
            sl = slice(s0, s0 + 2048)
            p.emit("v", lambda e, sl=sl: e.tensor_scalar(
                dts[:, sl], dts[:, sl], dtb_col[:], None, ALU.add),
                reads=[dts, dtb_col], writes=[dts])
            p.emit("s", lambda e, sl=sl: e.activation(stmp[:], dts[:, sl], AF.Abs),
                   reads=[dts], writes=[stmp])
            p.emit("s", lambda e: e.activation(stmp[:], stmp[:], AF.Exp, scale=-1.0),
                   reads=[stmp], writes=[stmp])
            p.emit("s", lambda e: e.activation(stmp[:], stmp[:], AF.Ln, bias=one64[:64, :]),
                   reads=[stmp, one64], writes=[stmp])
            p.emit("v", lambda e, sl=sl: e.tensor_scalar(
                dts[:, sl], dts[:, sl], 0.0, None, ALU.max),
                reads=[dts], writes=[dts])
            p.emit("v", lambda e, sl=sl: e.tensor_tensor(
                dts[:, sl], dts[:, sl], stmp[:], ALU.add),
                reads=[dts, stmp], writes=[dts])
        m1 = np.zeros((64, 128), np.float32)
        for k in range(64):
            m1[k, (k % 2) * 64:(k % 2) * 64 + 64] = 1.0
        pat = np.zeros((64, 32), np.float32)
        for k in range(64):
            pat[k, k // 2] = 1.0
        m1_sb = self.sb("m_m1", [64, 128], F32)
        pat_sb = self.sb("m_pat", [64, 32], F32)
        p.dma("y", m1_sb[:], self.const("m1_c", m1)[:])
        p.dma("y", pat_sb[:], self.const("pat_c", pat)[:])
        m2 = self.sb("m_m2", [64, 32], F32)
        p.emit("v", lambda e: e.tensor_tensor(
            m2[:], pat_sb[:], dcol[:].to_broadcast([64, 32]), ALU.mult),
            reads=[pat_sb, dcol], writes=[m2])
        p.emit("t", lambda e: e.matmul(pb[0][:, :32], m1_sb[:], m2[:], start=True, stop=True),
               reads=[m1_sb, m2], writes=[pb[0]])
        dbc = self.sb("m_dbc", [128, 32], F32)
        p.emit("v", lambda e: e.tensor_copy(dbc[:], pb[0][:, :32]),
               reads=[pb[0]], writes=[dbc])
        nw_sb = self.sb("m_nw", [128, 32], F32)
        p.dma("y", nw_sb[:], norm_w.rearrange("(a p) -> p a", p=128),
              writes=[nw_sb], track_out=False, allow_slow_non_contiguous=True)
        p.barrier()

        mask = np.zeros((128, 512), np.float32)
        for st in range(2):
            for pp in range(128):
                s_idx = st * 128 + pp
                mask[pp, st * 256 + s_idx:st * 256 + 256] = 1.0
        mask_sb = self.sb("m_mask", [128, 512], BF16)
        p.dma("y", mask_sb[:], self.const("mask_c", mask.astype(bfdt))[:])
        acum = self.sb("m_acum", [64, QL], F32)
        dAc = self.sb("m_dAc", [64, QL], F32)
        onesQ = self.sb("m_onesQ", [64, QL], F32)
        p.emit("g", lambda e: e.memset(onesQ[:], 1.0), writes=[onesQ])
        dec = self.sb("m_dec", [64, QL], F32)
        decf = self.sb("m_decf", [64, QL], F32)
        dtcb = self.sb("m_dtcb", [64, QL], BF16)
        acT = self.sb("m_acT", [128, 128], F32)
        acTn = self.sb("m_acTn", [128, 128], F32)
        dcT = self.sb("m_dcT", [128, 128], F32)
        dtT = self.sb("m_dtT", [128, 128], BF16)
        Bsb = self.sb("m_B", [128, QL], BF16)
        Csb = self.sb("m_C", [128, QL], BF16)
        BT = self.sb("m_BT", [128, 2 * 128], BF16)
        msc = self.sb("m_msc", [128, 2 * QL], BF16)
        Xb = self.sb("m_Xb", [128, 32 * QL], BF16)
        Xsb = [Xb[:, j * QL:(j + 1) * QL] for j in range(32)]
        XT = self.sb("m_XT", [128, 8192], BF16)
        yc = [self.sb("m_y%d" % j, [128, QL], BF16) for j in range(32)]
        P = self.sb("m_P", [128, 4096], F32)
        Pbf = self.sb("m_Pbf", [128, 4096], BF16)
        eA = [self.sb("m_eA%d" % i, [128, QL], BF16) for i in range(2)]
        Csc = [self.sb("m_Csc%d" % i, [128, QL], BF16) for i in range(2)]
        mtmp = [self.sb("m_mtmp%d" % i, [128, 2 * QL], F32) for i in range(2)]
        me = [self.sb("m_me%d" % i, [128, 2 * QL], BF16) for i in range(2)]
        Mt = [[self.sb("m_Mt%d_%d" % (i, st), [128, QL], BF16) for st in range(2)]
              for i in range(2)]
        sBT = [[self.sb("m_sBT%d_%d" % (i, st), [128, 128], BF16) for st in range(2)]
               for i in range(2)]
        arow = [self.sb("m_ar%d" % i, [1, QL], F32) for i in range(2)]
        csrow = self.sb("m_csr", [1, 64], F32)
        excsT = self.sb("m_excsT", [128, 64], F32)
        zb = Xb
        sqg = [self.sb("m_sqg%d" % i, [128, QL], BF16) for i in range(2)]
        gtmp = [self.sb("m_gt%d" % i, [128, QL], BF16) for i in range(2)]
        rr = self.sb("m_rr", [1, QL], F32)
        rbc = self.sb("m_rbc", [128, QL], F32)
        p.emit("g", lambda e: e.memset(P[:], 0.0), writes=[P])

        eps5 = self.fconst(1e-5)
        for c in range(NCK):
            c0 = c * QL
            p.emit("v", lambda e, c0=c0: e.tensor_scalar(
                dAc[:], dts[:, c0:c0 + QL], acol[:], None, ALU.mult),
                reads=[dts, acol], writes=[dAc])
            p.emit("v", lambda e: e.tensor_tensor_scan(
                acum[:], onesQ[:], dAc[:], 0.0, ALU.mult, ALU.add),
                reads=[onesQ, dAc], writes=[acum])
            p.dma("y", acum_scr[:, c0:c0 + QL], acum[:], reads=[acum],
                  writes=["acs:%d" % c], track_out=False)
            p.dma("y", csscr[c].unsqueeze(1), acum[:, QL - 1:QL], reads=[acum],
                  writes=["css:%d" % c], track_out=False)
            p.emit("v", lambda e: e.tensor_scalar(
                dec[:], acum[:], acum[:, QL - 1:QL], None, ALU.subtract),
                reads=[acum], writes=[dec])
            p.emit("s", lambda e: e.activation(decf[:], dec[:], AF.Exp, scale=-1.0),
                   reads=[dec], writes=[decf])
            p.emit("v", lambda e, c0=c0: e.tensor_copy(dtcb[:], dts[:, c0:c0 + QL]),
                   reads=[dts], writes=[dtcb])
            for sub in range(2):
                p.emit("t", lambda e, sub=sub: e.transpose(
                    pb[7][:, :64], acum[:, sub * 128:(sub + 1) * 128], identf[:64, :64]),
                    reads=[acum, identf], writes=[pb[7]])
                p.emit("v", lambda e, sub=sub: e.tensor_copy(
                    acT[:, sub * 64:(sub + 1) * 64], pb[7][:, :64]),
                    reads=[pb[7]], writes=[acT])
                p.emit("t", lambda e, sub=sub: e.transpose(
                    pb[6][:, :64], decf[:, sub * 128:(sub + 1) * 128],
                    identf[:64, :64]), reads=[decf, identf], writes=[pb[6]])
                p.emit("v", lambda e, sub=sub: e.tensor_copy(
                    dcT[:, sub * 64:(sub + 1) * 64], pb[6][:, :64]),
                    reads=[pb[6]], writes=[dcT])
                p.emit("t", lambda e, sub=sub: e.transpose(
                    pb[6][:, :].bitcast(BF16)[:, :64], dtcb[:, sub * 128:(sub + 1) * 128],
                    identb[:64, :64]), reads=[dtcb, identb], writes=[pb[6]])
                p.emit("v", lambda e, sub=sub: e.tensor_copy(
                    dtT[:, sub * 64:(sub + 1) * 64], pb[6][:, :].bitcast(BF16)[:, :64]),
                    reads=[pb[6]], writes=[dtT])
            p.emit("v", lambda e: e.tensor_scalar(acTn[:], acT[:], -1.0, None, ALU.mult),
                   reads=[acT], writes=[acTn])
            p.dma("y", Bsb[:], xbcc[4096:4224, c0:c0 + QL], writes=[Bsb], track_out=False)
            p.dma("y", Csb[:], xbcc[4224:4352, c0:c0 + QL], writes=[Csb], track_out=False)
            for st in range(2):
                p.emit("t", lambda e, st=st: e.transpose(
                    pb[5][:, :].bitcast(BF16)[:, :128], Bsb[:, st * 128:(st + 1) * 128],
                    identb[:]), reads=[Bsb, identb], writes=[pb[5]])
                p.emit("v", lambda e, st=st: e.tensor_copy(
                    BT[:, st * 128:(st + 1) * 128], pb[5][:, :].bitcast(BF16)[:, :128]),
                    reads=[pb[5]], writes=[BT])
                p.emit("t", lambda e, st=st: e.matmul(
                    pb[st][:, :QL], Bsb[:, st * 128:(st + 1) * 128], Csb[:],
                    start=True, stop=True), reads=[Bsb, Csb], writes=[pb[st]])
                p.emit("v", lambda e, st=st: e.tensor_tensor(
                    msc[:, st * QL:(st + 1) * QL], pb[st][:, :QL],
                    mask_sb[:, st * QL:(st + 1) * QL], ALU.mult),
                    reads=[pb[st], mask_sb], writes=[msc])
            for jb in range(4):
                p.dma("g", Xb[:, jb * 8 * QL:(jb + 1) * 8 * QL].rearrange(
                          "p (j t) -> p j t", t=QL),
                      xbcc.rearrange("(j p) t -> p j t", p=128)[:, jb * 8:(jb + 1) * 8,
                                                               c0:c0 + QL],
                      writes=["Xb"], track_out=False)
            for sub in range(2):
                p.dma("y", XT[:, sub * 4096:(sub + 1) * 4096],
                      xbcc[0:4096, c0 + sub * 128:c0 + (sub + 1) * 128],
                      writes=["XTs%d" % sub, "ynb%d" % (2 * sub), "ynb%d" % (2 * sub + 1)],
                      track_out=False, transpose=True)
            for sub in range(2):
                xv = XT[:, sub * 4096:(sub + 1) * 4096].rearrange("p (h i) -> p h i", i=64)
                dv = dtT[:, sub * 64:(sub + 1) * 64].unsqueeze(2).to_broadcast([128, 64, 64])
                p.emit("v", lambda e, xv=xv, dv=dv: e.tensor_tensor(xv, xv, dv, ALU.mult),
                       reads=[dtT], writes=["XTs%d" % sub])
            p.emit("v", lambda e: e.tensor_copy(Pbf[:], P[:]), reads=[P], writes=[Pbf])
            p.dma("g", csrow[:], csscr[c].unsqueeze(0), writes=[csrow],
                  track_out=False, reads=["css:%d" % c])
            p.emit("s", lambda e: e.activation(csrow[:], csrow[:], AF.Exp),
                   reads=[csrow], writes=[csrow])
            p.emit("t", lambda e: e.matmul(pb[2][:, :64], self.ones_row_f[:], csrow[:],
                                           start=True, stop=True),
                   reads=[self.ones_row_f, csrow], writes=[pb[2]])
            p.emit("v", lambda e: e.tensor_copy(excsT[:], pb[2][:, :64]),
                   reads=[pb[2]], writes=[excsT])
            pv = P[:].rearrange("p (h i) -> p h i", i=64)
            ev = excsT[:].unsqueeze(2).to_broadcast([128, 64, 64])
            p.emit("v", lambda e, pv=pv, ev=ev: e.tensor_tensor(pv, pv, ev, ALU.mult),
                   reads=[excsT, P], writes=[P])
            def prep(h):
                par = h % 2
                bA = pb[0 + par]
                ar = arow[par]
                eAp, Cscp, mtp, mep = eA[par], Csc[par], mtmp[par], me[par]
                Mtp, sBTp = Mt[par], sBT[par]
                p.dma("g", ar[:], acum_scr[h:h + 1, c0:c0 + QL], writes=[ar],
                      track_out=False, reads=["acs:%d" % c])
                p.emit("t", lambda e, ar=ar, bA=bA: e.matmul(
                    bA[:, :QL], self.ones_row_f[:], ar[:], start=True, stop=True),
                    reads=[self.ones_row_f, ar], writes=[bA])
                p.emit("s", lambda e, bA=bA, eAp=eAp: e.activation(eAp[:], bA[:, :QL], AF.Exp),
                       reads=[bA], writes=[eAp])
                p.emit("v", lambda e, eAp=eAp, Cscp=Cscp: e.tensor_tensor(
                    Cscp[:], Csb[:], eAp[:], ALU.mult),
                    reads=[Csb, eAp], writes=[Cscp])
                for st in range(2):
                    p.emit("v", lambda e, st=st, h=h, bA=bA, mtp=mtp: e.tensor_scalar(
                        mtp[:, st * QL:(st + 1) * QL], bA[:, :QL],
                        acTn[:, st * 64 + h:st * 64 + h + 1],
                        0.0, ALU.add, ALU.min),
                        reads=[bA, acTn], writes=[mtp])
                p.emit("s", lambda e, mtp=mtp, mep=mep: e.activation(mep[:], mtp[:], AF.Exp),
                       reads=[mtp], writes=[mep])
                for st in range(2):
                    p.emit("v", lambda e, st=st, mep=mep, Mtp=Mtp: e.tensor_tensor(
                        Mtp[st][:], mep[:, st * QL:(st + 1) * QL],
                        msc[:, st * QL:(st + 1) * QL], ALU.mult),
                        reads=[mep, msc], writes=[Mtp[st]])
                for st in range(2):
                    p.emit("v", lambda e, st=st, h=h, sBTp=sBTp: e.tensor_scalar(
                        sBTp[st][:], BT[:, st * 128:(st + 1) * 128],
                        dcT[:, st * 64 + h:st * 64 + h + 1], None, ALU.mult),
                        reads=[BT, dcT], writes=[sBTp[st]])

            def consume(h):
                par = h % 2
                bY, bS = pb[2 + par], pb[4 + par]
                Cscp, Mtp, sBTp = Csc[par], Mt[par], sBT[par]
                hb = (h % 2) * 64
                p.emit("t", lambda e, h=h, hb=hb, bY=bY, Cscp=Cscp: e.matmul(
                    bY[hb:hb + 64, :QL], Pbf[:, h * 64:(h + 1) * 64], Cscp[:],
                    start=True, stop=False), reads=[Pbf, Cscp], writes=[bY])
                for st in range(2):
                    p.acc("t", lambda e, st=st, h=h, hb=hb, bY=bY, Mtp=Mtp: e.matmul(
                        bY[hb:hb + 64, :QL], XT[:, st * 4096 + h * 64:st * 4096 + h * 64 + 64],
                        Mtp[st][:], start=False, stop=(st == 1)),
                        reads=["XTs%d" % st, Mtp[st]], bank=bY,
                        first=False, last=(st == 1))
                for st in range(2):
                    p.acc("t", lambda e, st=st, h=h, bS=bS, sBTp=sBTp: e.matmul(
                        bS[:, :64], sBTp[st][:], XT[:, st * 4096 + h * 64:st * 4096 + h * 64 + 64],
                        start=(st == 0), stop=(st == 1)),
                        reads=[sBTp[st], "XTs%d" % st], bank=bS,
                        first=(st == 0), last=(st == 1))
                xj, half = h // 2, (h % 2) * 64
                p.emit("v", lambda e, h=h, xj=xj, half=half, bY=bY: e.scalar_tensor_tensor(
                    yc[xj][half:half + 64, :], Xsb[xj][half:half + 64, :],
                    dbc[half:half + 64, xj:xj + 1], bY[half:half + 64, :QL],
                    ALU.mult, ALU.add),
                    reads=["Xb", dbc, bY], writes=["yck%d" % xj])
                p.emit("v", lambda e, h=h, bS=bS: e.tensor_tensor(
                    P[:, h * 64:(h + 1) * 64], P[:, h * 64:(h + 1) * 64],
                    bS[:, :64], ALU.add), reads=[P, bS], writes=[P])

            prep(0)
            for h in range(NH):
                if h + 1 < NH:
                    prep(h + 1)
                consume(h)
            for jb in range(4):
                p.dma("y", zb[:, jb * 8 * QL:(jb + 1) * 8 * QL].rearrange(
                          "p (j t) -> p j t", t=QL),
                      zs.rearrange("(j p) t -> p j t", p=128)[:, jb * 8:(jb + 1) * 8,
                                                             c0:c0 + QL],
                      writes=["Xb"], track_out=False)
            for j in range(32):
                gt = gtmp[j % 2]
                p.emit("s", lambda e, j=j, gt=gt: e.activation(
                    gt[:], zb[:, j * QL:(j + 1) * QL], AF.Silu),
                    reads=["Xb"], writes=[gt])
                p.emit("v", lambda e, j=j, gt=gt: e.tensor_tensor(
                    yc[j][:], yc[j][:], gt[:], ALU.mult),
                    reads=[gt], writes=["yck%d" % j])
                sq = sqg[j % 2]
                p.emit("s", lambda e, j=j, sq=sq: e.activation(sq[:], yc[j][:], AF.Square),
                       reads=["yck%d" % j], writes=[sq])
                p.acc("t", lambda e, sq=sq, j=j: e.matmul(
                    pb[7][:1, :QL], self.ones_col_bf[:], sq[:],
                    start=(j == 0), stop=(j == 31)),
                    reads=[self.ones_col_bf, sq], bank=pb[7],
                    first=(j == 0), last=(j == 31))
            p.emit("s", lambda e: e.activation(rr[:], pb[7][:1, :QL], AF.Ln,
                                               bias=eps5[:1, :], scale=1.0 / 4096.0),
                   reads=[pb[7], eps5], writes=[rr])
            p.emit("s", lambda e: e.activation(rr[:], rr[:], AF.Exp, scale=-0.5),
                   reads=[rr], writes=[rr])
            p.emit("t", lambda e: e.matmul(pb[7][:, :QL], self.ones_row_f[:], rr[:],
                                           start=True, stop=True),
                   reads=[self.ones_row_f, rr], writes=[pb[7]])
            p.emit("v", lambda e: e.tensor_copy(rbc[:], pb[7][:, :QL]),
                   reads=[pb[7]], writes=[rbc])
            for j in range(32):
                p.emit("v", lambda e, j=j: e.scalar_tensor_tensor(
                    yc[j][:], yc[j][:], nw_sb[:, j:j + 1], rbc[:], ALU.mult, ALU.mult),
                    reads=[nw_sb, rbc], writes=["yck%d" % j])
                p.dma("y", yns[j * 128:(j + 1) * 128, c0:c0 + QL], yc[j][:],
                      reads=["yck%d" % j], writes=["yns:%d:%d" % (j, c)],
                      track_out=False)
        p.barrier()

        ynb = [XT[:, j * QL:(j + 1) * QL] for j in range(32)]
        xres = xbt
        otile = [xin[0], xin[1]]
        eslab = [Xb[:, i * 2048:(i + 1) * 2048] for i in range(4)]
        for cc in range(T // QL):
            c0 = cc * QL
            for jb in range(4):
                p.dma("y", XT[:, jb * 8 * QL:(jb + 1) * 8 * QL].rearrange(
                          "p (j t) -> p j t", t=QL),
                      yns.rearrange("(j p) t -> p j t", p=128)[:, jb * 8:(jb + 1) * 8,
                                                               c0:c0 + QL],
                      writes=["ynb%d" % jb], track_out=False)
            for g in range(2):
                for kt2 in range(16):
                    b = eslab[kt2 % 4]
                    bk = "eslab%d" % (kt2 % 4)
                    p.dma("y" if kt2 % 2 == 0 else "g",
                          b.rearrange("p (k m) -> p k m", m=1024),
                          wout_s[2 * kt2:2 * kt2 + 2, :,
                                 g * 1024:(g + 1) * 1024].transpose([1, 0, 2]),
                          writes=[bk], track_out=False, deps=self.wready["wout_s"])
                    for kl in range(2):
                        kt = 2 * kt2 + kl
                        for oi in range(8):
                            p.acc("t", lambda e, b=b, oi=oi, kt=kt, kl=kl: e.matmul(
                                pb[oi][:, :QL],
                                b[:, kl * 1024 + oi * 128:kl * 1024 + (oi + 1) * 128],
                                ynb[kt], start=(kt == 0), stop=(kt == 31)),
                                reads=[bk, "ynb%d" % (kt // 8)], bank=pb[oi],
                                first=(kt == 0), last=(kt == 31))
                for oi in range(8):
                    i = g * 8 + oi
                    xv, ob = xres[oi % 2][:, :QL], otile[oi % 2][:, :QL]
                    xk, ok = xres[oi % 2], otile[oi % 2]
                    p.dma("y", xv, xr_in[i * 128:(i + 1) * 128, c0:c0 + QL],
                          writes=[xk], track_out=False)
                    p.emit("v", lambda e, oi=oi, xv=xv, ob=ob: e.tensor_tensor(
                        ob, pb[oi][:, :QL], xv, ALU.add),
                        reads=[pb[oi], xk], writes=[ok])
                    x_out_writer(i, cc, ob)

    def finish(self):
        self.p.barrier()
        self.p.build()



def _qkv_perm():
    i = np.arange(2048)
    permq = (i % 128) * 16 + i // 128
    j = np.arange(1024)
    permkv = (j % 128) * 8 + j // 128
    return permq, permkv


def prep_l0attn_inputs(W_Q, W_K, W_V, W_O):
    permq, permkv = _qkv_perm()
    W_Q, W_K = np.asarray(W_Q), np.asarray(W_K)
    W_V, W_O = np.asarray(W_V), np.asarray(W_O)
    wq = W_Q[permq].T
    wk = W_K[permkv].T
    wv = W_V[permkv].T
    return {
        "L0_attn_WqkvT": np.ascontiguousarray(np.concatenate([wq, wk, wv], axis=1)),
        "L0_attn_WOT": np.ascontiguousarray(W_O.T),
    }


def build_l0attn_program():
    kb = KB(blocks=(0,))
    xr_in = kb.din("x_in", [D, T])
    x_out = kb.dout("x_out", [D, T])

    def writer(i, ch, ob):
        key = getattr(kb, "_writer_key", None)
        kb.p.dma("y", x_out[i * 128:(i + 1) * 128, ch * C:(ch + 1) * C], ob[:],
                 reads=[ob if key is None else key],
                 writes=["x_out:%d:%d" % (i, ch)], track_out=False)
        kb._writer_key = None
    kb.prelude_shared()
    kb.phase_begin('a')
    kb.phase_l0_attn(xr_in, writer)
    kb.finish()
    return kb


def prep_l0ffn_inputs(gup, gus, dnp, dns):
    return {
        "L0_ffn_gate_up_packed": np.ascontiguousarray(np.asarray(gup)),
        "L0_ffn_gate_up_scalesT": np.ascontiguousarray(np.asarray(gus).T),
        "L0_ffn_down_packed": np.ascontiguousarray(np.asarray(dnp)),
        "L0_ffn_down_scalesT": np.ascontiguousarray(np.asarray(dns).T),
    }


def build_l0ffn_program():
    kb = KB(blocks=(1,))
    xr_in = kb.din("x_in", [D, T])
    x_out = kb.dout("x_out", [D, T])

    def writer(i, ch, ob):
        kb.p.dma("y", x_out[i * 128:(i + 1) * 128, ch * C:(ch + 1) * C], ob[:],
                 reads=[ob], writes=["x_out:%d:%d" % (i, ch)], track_out=False)
    kb.prelude_shared()
    kb.phase_begin('b')
    kb.phase_l0_ffn(xr_in, writer)
    kb.finish()
    return kb


def prep_l1attn_inputs(W_in, conv_w, conv_b, dt_bias, A_log, D_param, norm_w, W_out):
    return {
        "L1_attn_W_inT": np.ascontiguousarray(np.asarray(W_in).T),
        "L1_attn_W_outT": np.ascontiguousarray(np.asarray(W_out).T),
        "L1_attn_conv_w": np.ascontiguousarray(np.asarray(conv_w)),
        "L1_attn_conv_b": np.ascontiguousarray(np.asarray(conv_b)),
        "L1_attn_dt_bias": np.ascontiguousarray(np.asarray(dt_bias)),
        "L1_attn_A_log": np.ascontiguousarray(np.asarray(A_log)),
        "L1_attn_D_param": np.ascontiguousarray(np.asarray(D_param)),
        "L1_attn_norm_w": np.ascontiguousarray(np.asarray(norm_w)),
    }


def build_l1attn_program():
    kb = KB(blocks=(2,))
    xr_in = kb.din("x_in", [D, T])
    x_out = kb.dout("x_out", [D, T])

    def writer(i, cc, ob):
        kb.p.dma("y", x_out[i * 128:(i + 1) * 128, cc * 256:(cc + 1) * 256], ob[:],
                 reads=[ob], writes=["x_out:%d:%d" % (i, cc)], track_out=False)
    kb.prelude_shared()
    kb.phase_begin('c')
    kb.phase_l1_attn(xr_in, writer)
    kb.finish()
    return kb


def build_l1ffn_program():
    kb = KB(blocks=(3,))
    xr_in = kb.din("x_in", [D, T])
    x_out = kb.dout("x_out", [D, T])

    def writer(i, ch, ob):
        kb.p.dma("y", x_out[i * 128:(i + 1) * 128, ch * C:(ch + 1) * C], ob[:],
                 reads=[ob], writes=["x_out:%d:%d" % (i, ch)], track_out=False)
    kb.prelude_shared()
    kb.phase_begin('d')
    kb.phase_l1_ffn(xr_in, writer)
    kb.finish()
    return kb



def prep_l1ffn_inputs(L1_ffn_w_gate_up, L1_ffn_w_down):
    return {
        "L1_ffn_w_gate_upT": np.ascontiguousarray(np.asarray(L1_ffn_w_gate_up).T),
        "L1_ffn_w_downT": np.ascontiguousarray(np.asarray(L1_ffn_w_down).T),
    }


def build_fused_program():
    kb = KB()
    x_in = kb.din("x_in", [D, T])
    x_out = kb.dout("x_out", [D, T])
    xr1 = kb.dscratch("xr1", [D, T], F32)
    xr2 = kb.dscratch("xr2", [D, T], F32)
    xr3 = kb.dscratch("xr3", [D, T], F32)
    kb.prelude_shared()
    kb.get_ws("L0_attn_WqkvT", [D, 4096], D, 4096, "wqkv_s")
    kb.get_ws("L0_attn_WOT", [D, D], D, D, "wot_s")
    kb.get_ws("L1_attn_W_inT", [D, 8512], D, 8512, "win_s")
    kb.get_ws("L1_attn_W_outT", [4096, D], 4096, D, "wout_s")
    kb.get_ws("L1_ffn_w_gate_upT", [D, 16384], D, 16384, "wgu_s")
    kb.get_ws("L1_ffn_w_downT", [8192, D], 8192, D, "wdn_s")

    def mkw(dst, csz):
        def w(i, ch, ob):
            key = getattr(kb, "_writer_key", None)
            kb.p.dma("y", dst[i * 128:(i + 1) * 128, ch * csz:(ch + 1) * csz],
                     ob[:], reads=[ob if key is None else key],
                     writes=["%s:%d:%d" % (dst.tensor.name, i, ch)],
                     track_out=False)
            kb._writer_key = None
        return w
    kb.phase_begin("a")
    kb.phase_l0_attn(x_in, mkw(xr1, C))
    kb.phase_begin("b")
    kb.phase_l0_ffn(xr1, mkw(xr2, C))
    kb.phase_begin("c")
    kb.phase_l1_attn(xr2, mkw(xr3, 256))
    kb.phase_begin("d")
    kb.phase_l1_ffn(xr3, mkw(x_out, C))
    kb.finish()
    return kb


_PROGRAMS = {}


def _get_fused():
    if "f" not in _PROGRAMS:
        _PROGRAMS["f"] = build_fused_program()
    return _PROGRAMS["f"]


def kernel(**inputs):
    x = np.asarray(inputs["L0_attn_x"])
    ncores = x.shape[0]
    wm = {}
    wm.update(prep_l0attn_inputs(inputs["L0_attn_W_Q"], inputs["L0_attn_W_K"],
                                 inputs["L0_attn_W_V"], inputs["L0_attn_W_O"]))
    wm.update(prep_l0ffn_inputs(inputs["L0_ffn_gate_up_packed"],
                                inputs["L0_ffn_gate_up_scales"],
                                inputs["L0_ffn_down_packed"],
                                inputs["L0_ffn_down_scales"]))
    wm.update(prep_l1attn_inputs(inputs["L1_attn_W_in"], inputs["L1_attn_conv_w"],
                                 inputs["L1_attn_conv_b"], inputs["L1_attn_dt_bias"],
                                 inputs["L1_attn_A_log"], inputs["L1_attn_D_param"],
                                 inputs["L1_attn_norm_w"], inputs["L1_attn_W_out"]))
    wm.update(prep_l1ffn_inputs(inputs["L1_ffn_w_gate_up"], inputs["L1_ffn_w_down"]))
    kb = _get_fused()
    in_maps = [dict(wm, x_in=np.ascontiguousarray(x[c].T)) for c in range(ncores)]
    res = run_bass_kernel_spmd(kb.nc, in_maps, core_ids=list(range(ncores)))
    out = np.stack([np.asarray(res.results[c]["x_out"]).T for c in range(ncores)], axis=0)
    return np.ascontiguousarray(out.astype(np.float32))
